# version 09
# baseline (speedup 1.0000x reference)
import numpy as np
import concourse.bass as bass
import concourse.mybir as mybir
from concourse.bass_utils import run_bass_kernel_spmd

F32 = mybir.dt.float32
BF16 = mybir.dt.bfloat16
OP = mybir.AluOpType
AF = mybir.ActivationFunctionType
AX = mybir.AxisListType

D = 1024
NH0 = 16
HID = 4096
NH1 = 8
INTER = 4096

TILE = 512
NSUB = 4
KW = TILE + 12
XNW = TILE + 16
HW_ = TILE + 4
JW = 140



class Sem:
    def __init__(self, nc, name):
        self.h = nc.alloc_semaphore(name)
        self.n = 0


class B:
    ENGS = ("sync", "tensor", "vector", "scalar", "gpsimd")

    def __init__(self, nc):
        self.nc = nc
        self.q = {e: [] for e in self.ENGS}
        self.e = {e: Sem(nc, "e_" + e) for e in self.ENGS}
        self.dma = {}
        self._last_inc = {e: None for e in self.ENGS}
        self.spacer = None
        self._spc_rot = {e: 0 for e in self.ENGS}

    def dsem(self, name):
        if name not in self.dma:
            self.dma[name] = Sem(self.nc, "d_" + name)
        return self.dma[name]

    def op(self, eng, fn, inc=False, sem=None, inck=1):
        if inc or sem is not None:
            s = self.e[eng] if sem is None else sem
            if self._last_inc[eng] is s and self.spacer is not None:
                self.q[eng].append(self.spacer(eng))
            s.n += inck
            h, k = s.h, inck
            self.q[eng].append(lambda E: fn(E).then_inc(h, k))
            self._last_inc[eng] = s
            return s.n
        self.q[eng].append(lambda E: fn(E))
        self._last_inc[eng] = None
        return None

    def wait(self, eng, sem, val=None):
        v = sem.n if val is None else val
        if v <= 0:
            return
        h = sem.h
        self.q[eng].append(lambda E: E.wait_ge(h, v))

    def waitc(self, eng, *sems):
        for s in sems:
            self.wait(eng, s)

    def barrier(self):
        for e in self.ENGS:
            if self.spacer is not None and e in ("vector", "scalar", "gpsimd"):
                self.op(e, self.spacer(e), inc=True)
            else:
                self.op(e, lambda E: E.nop(nofuse=True), inc=True)
        for e in self.ENGS:
            for o in self.ENGS:
                if o != e:
                    self.wait(e, self.e[o])

    def finish(self, block):
        for e in self.ENGS:
            thunks = self.q[e]

            def body(E, thunks=thunks):
                for t in thunks:
                    t(E)
            getattr(block, e)(body)


def build(n_seq=2, seq_len=4096, taps=(), stop_after=None):
    NT = seq_len // TILE
    nc = bass.Bass("TRN2", target_bir_lowering=False, debug=False)

    x_in = nc.dram_tensor("x_in", [n_seq, seq_len, D], F32, kind="ExternalInput").ap()
    y_out = nc.dram_tensor("y_out", [n_seq, seq_len, D], F32, kind="ExternalOutput").ap()

    w_qkvT = nc.dram_tensor("w_qkvT", [D, 3 * D], F32, kind="ExternalInput").ap()
    w_projT = nc.dram_tensor("w_projT", [D, D], F32, kind="ExternalInput").ap()
    w_w1T = nc.dram_tensor("w_w1T", [D, HID], F32, kind="ExternalInput").ap()
    w_w2T = nc.dram_tensor("w_w2T", [HID, D], F32, kind="ExternalInput").ap()
    w_l1fT = nc.dram_tensor("w_l1fT", [D, D], F32, kind="ExternalInput").ap()
    w_q1 = nc.dram_tensor("w_q1", [D, D], F32, kind="ExternalInput").ap()
    w_k1 = nc.dram_tensor("w_k1", [D, D], F32, kind="ExternalInput").ap()
    w_v1 = nc.dram_tensor("w_v1", [D, D], F32, kind="ExternalInput").ap()
    w_o1 = nc.dram_tensor("w_o1", [D, D], F32, kind="ExternalInput").ap()
    w_g1 = nc.dram_tensor("w_g1", [D, INTER], F32, kind="ExternalInput").ap()
    w_u1 = nc.dram_tensor("w_u1", [D, INTER], F32, kind="ExternalInput").ap()
    w_d1 = nc.dram_tensor("w_d1", [INTER, D], F32, kind="ExternalInput").ap()

    qkv_b_qk = nc.dram_tensor("qkv_b_qk", [128, 16], F32, kind="ExternalInput").ap()
    qkv_b_vcol = nc.dram_tensor("qkv_b_vcol", [128, 8], F32, kind="ExternalInput").ap()
    proj_b_row = nc.dram_tensor("proj_b_row", [1, D], F32, kind="ExternalInput").ap()
    conv_w3 = nc.dram_tensor("conv_w3", [3, 128, 32], F32, kind="ExternalInput").ap()
    conv_b_c = nc.dram_tensor("conv_b_c", [128, 32], F32, kind="ExternalInput").ap()
    anorm_c = nc.dram_tensor("anorm_c", [128, 8], F32, kind="ExternalInput").ap()
    mnorm_c = nc.dram_tensor("mnorm_c", [128, 8], F32, kind="ExternalInput").ap()
    onorm_c = nc.dram_tensor("onorm_c", [128, 1], F32, kind="ExternalInput").ap()
    brpb = nc.dram_tensor("brpb", [NH0, 128, JW], F32, kind="ExternalInput").ap()
    bmask3 = nc.dram_tensor("bmask3", [3, 128, JW], F32, kind="ExternalInput").ap()
    tril_T = nc.dram_tensor("tril_T", [128, 128], F32, kind="ExternalInput").ap()
    ident_f = nc.dram_tensor("ident_f", [128, 128], F32, kind="ExternalInput").ap()
    ident_b = nc.dram_tensor("ident_b", [128, 128], BF16, kind="ExternalInput").ap()
    ones_row = nc.dram_tensor("ones_row", [1, 128], F32, kind="ExternalInput").ap()
    ones_col = nc.dram_tensor("ones_col", [128, 1], F32, kind="ExternalInput").ap()
    consts_in = nc.dram_tensor("consts_in", [128, 8], F32, kind="ExternalInput").ap()
    spc_in = nc.dram_tensor("spc_in", [128, 64], F32, kind="ExternalInput").ap()

    sc_qkvT = [nc.dram_tensor(f"s_qkvT{h}", [8, 128, 3 * D], BF16).ap() for h in range(2)]
    sc_projT = [nc.dram_tensor(f"s_projT{h}", [8, 128, D], BF16).ap() for h in range(2)]
    sc_w1q = nc.dram_tensor("s_w1q", [32, 128, D], BF16).ap()
    sc_w2q = nc.dram_tensor("s_w2q", [32, 128, D], BF16).ap()
    sc_l1f = nc.dram_tensor("s_l1f", [8, 128, D], BF16).ap()
    sc_wq = [nc.dram_tensor(f"s_wq{h}", [8, 128, D], BF16).ap() for h in range(2)]
    sc_wk = [nc.dram_tensor(f"s_wk{h}", [8, 128, D], BF16).ap() for h in range(2)]
    sc_wv = [nc.dram_tensor(f"s_wv{h}", [8, 128, D], BF16).ap() for h in range(2)]
    sc_wo = [nc.dram_tensor(f"s_wo{h}", [8, 128, D], BF16).ap() for h in range(2)]
    sc_wg = [nc.dram_tensor(f"s_wg{h}", [32, 128, D], BF16).ap() for h in range(2)]
    sc_wu = [nc.dram_tensor(f"s_wu{h}", [32, 128, D], BF16).ap() for h in range(2)]
    sc_wd = [nc.dram_tensor(f"s_wd{h}", [32, 128, D], BF16).ap() for h in range(2)]
    sc_B = nc.dram_tensor("s_B", [3, 128, NH0 * JW], F32).ap()
    sc_hact = nc.dram_tensor("s_hact", [32, 128, HW_], F32).ap()
    sc_qh = nc.dram_tensor("s_qh", [32, 128, TILE], BF16).ap()

    AA = {}

    def sb(name, shape, dt=F32):
        AA[name] = nc.alloc_sbuf_tensor(name, list(shape), dt).ap()
        return AA[name]

    C = sb("c_c", [128, 8])
    sb("c_convw", [128, 96])
    sb("c_convb", [128, 32])
    sb("c_qkvb", [128, 16])
    sb("c_anorm", [128, 8])
    sb("c_mnorm", [128, 8])
    sb("c_onorm", [128, 1])
    sb("c_pbrow", [128, D])
    sb("c_tril", [128, 128])
    sb("c_idf", [128, 128])
    sb("c_idb", [128, 128], BF16)
    sb("c_or", [1, 128])
    sb("c_oc", [128, 1])
    sb("c_wm", [128, 8])
    ST = sb("c_stat", [128, 64])
    sb("S_sb", [128, NH1 * 128])
    sb("S_h", [128, NH1 * 128], BF16)
    sb("S_l", [128, NH1 * 128], BF16)
    NRING = 6
    for i in range(NRING):
        sb(f"ring{i}", [128, 1024], BF16)
    sb("v1t_l", [128, 4 * D], BF16)
    sb("Bt", [128, NH0 * JW])

    sb("xg", [128, 4 * D])
    sb("xh", [128, D])
    sb("tmp", [128, D])
    sb("xnf_h", [128, 8 * XNW], BF16)
    sb("xnf_l", [128, 8 * XNW], BF16)
    sb("xnq", [128, 4 * D], BF16)
    sb("xnqh", [128, D], BF16)
    sb("xnq_fm", [128, 8 * XNW], BF16)
    sb("q_fm_h", [128, 8 * TILE], BF16)
    sb("q_fm_l", [128, 8 * TILE], BF16)
    sb("k_fm_h", [128, 8 * KW], BF16)
    sb("k_fm_l", [128, 8 * KW], BF16)
    sb("vt_h", [128, 1024], BF16)
    sb("vt_l", [128, 1024], BF16)
    sb("vtm_h", [128, 1024], BF16)
    sb("vtm_l", [128, 1024], BF16)
    sb("ssb", [128, 4 * JW])
    sb("et_h", [128, 4 * JW], BF16)
    sb("et_l", [128, 4 * JW], BF16)
    sb("pt_h", [128, 256], BF16)
    sb("pt_l", [128, 256], BF16)
    sb("ptm_h", [128, 256], BF16)
    sb("ptm_l", [128, 256], BF16)
    sb("of_h", [128, 8 * 128], BF16)
    sb("of_l", [128, 8 * 128], BF16)
    sb("x1", [128, 4 * D])
    sb("qw", [128, D])
    sb("qr", [128, D])
    sb("qg", [128, D])
    AA["sq"] = AA["qg"]
    sb("ha", [128, 2 * HW_])
    sb("qhb", [128, 2 * TILE], BF16)
    sb("dqrow", [128, HW_])
    AA["srow"] = AA["dqrow"][:, 0:TILE]
    AA["hacc"] = AA["qhb"].bitcast(F32)[:, 0:TILE]
    AA["hacc2"] = AA["of_h"].bitcast(F32)[:, 0:TILE]
    AA["tms_h"] = AA["qhb"][:, 0:D]
    AA["tms_l"] = AA["dqrow"].bitcast(BF16)[:, 0:D]
    sb("xn1", [128, 4 * D])
    AA["ob2"] = AA["tmp"][:, 0:512]
    AA["h1f_h"] = AA["xnf_h"][:, 0:8 * TILE]
    AA["h1f_l"] = AA["xnf_l"][:, 0:8 * TILE]
    AA["k1t_h"] = AA["xnq"][:, 0:4 * D]
    AA["k1t_l"] = AA["xnq_fm"][:, 0:4 * D]
    AA["v1t_h"] = AA["Bt"].bitcast(BF16)[:, 0:4 * D]

    AA["onT_h"] = AA["of_h"][:, 0:NH1 * 128]
    AA["onT_l"] = AA["of_l"][:, 0:NH1 * 128]
    AA["am_h"] = AA["pt_h"]
    AA["am_l"] = AA["pt_l"]
    AA["h2f_h"] = AA["xnf_h"][:, 0:8 * TILE]
    AA["h2f_l"] = AA["xnf_l"][:, 0:8 * TILE]
    AA["xn1q"] = AA["xnq"][:, 0:4 * D]
    AA["x1qf"] = AA["xnq_fm"][:, 0:8 * TILE]
    AA["sg"] = AA["vt_h"].bitcast(F32)[:, 0:256]
    AA["x2"] = AA["xg"][:, 0:4 * D]
    AA["on_h"] = AA["et_h"][:, 0:128]
    AA["on_l"] = AA["et_l"][:, 0:128]
    AA["gu_h"] = AA["vtm_h"][:, 0:256]
    AA["gu_l"] = AA["vtm_l"][:, 0:256]

    sb("c_spc", [128, 64])

    PS = [nc.alloc_psum_tensor(f"ps{i}", [128, 512], F32).ap() for i in range(8)]
    PSB = [p.bitcast(BF16) for p in PS]

    bb = B(nc)

    def _mk_spacer(eng):
        base = {"vector": 0, "scalar": 21, "gpsimd": 42, "tensor": 0, "sync": 0}[eng]
        if eng in ("tensor", "sync"):
            return lambda E: E.nop(nofuse=True)
        rot = bb._spc_rot[eng] % 20
        bb._spc_rot[eng] += 1
        dst = AA["c_spc"][:, base + rot:base + rot + 1]
        src = AA["c_spc"][:, base + 20:base + 21]
        if eng == "scalar":
            return lambda E: E.copy(dst, src)
        return lambda E: E.tensor_copy(dst, src)
    bb.spacer = _mk_spacer
    dW = bb.dsem("w")
    dX = bb.dsem("x")
    dO = bb.dsem("o")
    dOp = [bb.dsem("o0"), bb.dsem("o1")]
    dP = bb.dsem("p")
    dH = bb.dsem("h")
    dHa = [bb.dsem("ha0"), bb.dsem("ha1")]
    dHb = [bb.dsem("hb0"), bb.dsem("hb1")]
    dHl = bb.dsem("hl")
    dHp = [bb.dsem("hl0"), bb.dsem("hl1")]
    dBt = bb.dsem("b")
    eT, eV, eS, eG, eY = (bb.e[k] for k in ("tensor", "vector", "scalar", "gpsimd", "sync"))

    CM128 = C[:, 0:1]
    CMH = C[:, 1:2]
    CPH = C[:, 2:3]
    CEPS5 = C[:, 3:4]
    CEPS8 = C[:, 4:5]
    for val, ap in ((-128.0, CM128), (-0.5, CMH), (0.5, CPH), (1e-5, CEPS5),
                    (1e-8, CEPS8), (0.0, C[:, 5:6]), (1.0, C[:, 6:7])):
        nc.const_aps.register(F32, val, ap)

    dbg = {}

    def tap(name, src_ap, shape):
        if name not in taps:
            return
        d = nc.dram_tensor("dbg_" + name, list(shape), F32, kind="ExternalOutput").ap()
        dbg[name] = d
        for e in ("vector", "scalar", "tensor", "gpsimd"):
            bb.wait("sync", bb.e[e])
        sv = bb.op("sync", lambda E: E.dma_start(out=d[:], in_=src_ap),
                   sem=bb.dsem("dbg"), inck=16)
        for e in B.ENGS:
            bb.wait(e, bb.dsem("dbg"), sv)

    def round_from_w(Fc, dst_ap, qw_, qr_, qg_, stt="vector"):
        se = eV if stt == "vector" else eG
        bb.wait(stt, se)
        bb.wait("scalar", eS)
        bb.op(stt, lambda E: E.tensor_copy(qr_[:, 0:Fc], qw_[:, 0:Fc]), inc=True, sem=se)
        for Bv in (128.0, 64.0, 32.0, 16.0, 8.0, 4.0, 2.0, 1.0):
            bb.wait("scalar", se)
            bb.op("scalar", lambda E: E.activation(qg_[:, 0:Fc], qr_[:, 0:Fc], AF.Sign,
                                                   bias=CM128), inc=True)
            bb.wait(stt, eS)
            bb.op(stt, lambda E, c=-Bv / 2.0: E.scalar_tensor_tensor(
                qr_[:, 0:Fc], qg_[:, 0:Fc], c, qr_[:, 0:Fc], OP.mult, OP.add),
                inc=True, sem=se)
        bb.wait(stt, se)
        bb.op(stt, lambda E: E.scalar_tensor_tensor(
            qg_[:, 0:Fc], qr_[:, 0:Fc], -1.0, qw_[:, 0:Fc], OP.mult, OP.add),
            inc=True, sem=se)
        bb.wait("scalar", se)
        bb.op("scalar", lambda E: E.activation(dst_ap, qg_[:, 0:Fc], AF.Copy, bias=-0.5),
              inc=True)

    def round2(Fc, chains):
        rv = {}
        sv = {}
        for k, (Wb, Rb, Gb, dst, wv) in enumerate(chains):
            bb.wait("vector", eV, wv)
            bb.wait("vector", eS)
            rv[k] = bb.op("vector", lambda E, W=Wb, Rb=Rb, Fc=Fc: E.tensor_copy(
                Rb[:, 0:Fc], W[:, 0:Fc]), inc=True)
        for Bv in (128.0, 64.0, 32.0, 16.0, 8.0, 4.0, 2.0, 1.0):
            for k, (Wb, Rb, Gb, dst, wv) in enumerate(chains):
                bb.wait("scalar", eV, rv[k])
                sv[k] = bb.op("scalar", lambda E, Rb=Rb, Gb=Gb, Fc=Fc: E.activation(
                    Gb[:, 0:Fc], Rb[:, 0:Fc], AF.Sign, bias=CM128), inc=True)
            for k, (Wb, Rb, Gb, dst, wv) in enumerate(chains):
                bb.wait("vector", eS, sv[k])
                rv[k] = bb.op("vector", lambda E, Rb=Rb, Gb=Gb, Fc=Fc, c=-Bv / 2.0:
                              E.scalar_tensor_tensor(
                                  Rb[:, 0:Fc], Gb[:, 0:Fc], c, Rb[:, 0:Fc],
                                  OP.mult, OP.add), inc=True)
        for k, (Wb, Rb, Gb, dst, wv) in enumerate(chains):
            bb.wait("vector", eV, rv[k])
            rv[k] = bb.op("vector", lambda E, W=Wb, Rb=Rb, Gb=Gb, Fc=Fc:
                          E.scalar_tensor_tensor(
                              Gb[:, 0:Fc], Rb[:, 0:Fc], -1.0, W[:, 0:Fc],
                              OP.mult, OP.add), inc=True)
        out = []
        for k, (Wb, Rb, Gb, dst, wv) in enumerate(chains):
            bb.wait("scalar", eV, rv[k])
            out.append(bb.op("scalar", lambda E, Gb=Gb, dst=dst, Fc=Fc: E.activation(
                dst, Gb[:, 0:Fc], AF.Copy, bias=-0.5), inc=True))
        return out

    def prep():
        small = [
            (AA["c_spc"][:], spc_in[:]),
            (C[:], consts_in[:]),
            (AA["c_convw"][:, 0:32], conv_w3[0]),
            (AA["c_convw"][:, 32:64], conv_w3[1]),
            (AA["c_convw"][:, 64:96], conv_w3[2]),
            (AA["c_convb"][:], conv_b_c[:]),
            (AA["c_qkvb"][:], qkv_b_qk[:]),
            (AA["c_anorm"][:], anorm_c[:]),
            (AA["c_mnorm"][:], mnorm_c[:]),
            (AA["c_onorm"][:], onorm_c[:]),
            (AA["c_tril"][:], tril_T[:]),
            (AA["c_idf"][:], ident_f[:]),
            (AA["c_idb"][:], ident_b[:]),
            (AA["c_or"][:], ones_row[:]),
            (AA["c_oc"][:], ones_col[:]),
        ]
        for dst, src in small:
            bb.op("sync", lambda E, d=dst, s=src: E.dma_start(out=d, in_=s), sem=dP, inck=16)
        for e in ("vector", "scalar", "tensor", "gpsimd"):
            bb.wait(e, dP)
        bb.op("vector", lambda E: E.tensor_scalar_mul(
            AA["c_qkvb"][:, 0:8], AA["c_qkvb"][:, 0:8], 0.125), inc=True)

        for v in range(3):
            bb.waitc("sync", eV)
            bb.wait("sync", dP)
            bb.op("sync", lambda E, v=v: E.dma_start(out=AA["tmp"][:, 0:JW], in_=bmask3[v]),
                  sem=dP, inck=16)
            for h in range(NH0):
                bb.op("sync", lambda E, h=h: E.dma_start(
                    out=AA["Bt"][:, h * JW:(h + 1) * JW], in_=brpb[h]), sem=dP, inck=16)
            bb.wait("vector", dP)
            for h in range(NH0):
                bb.op("vector", lambda E, h=h: E.tensor_tensor(
                    AA["Bt"][:, h * JW:(h + 1) * JW], AA["Bt"][:, h * JW:(h + 1) * JW],
                    AA["tmp"][:, 0:JW], OP.add), inc=(h == NH0 - 1))
            bb.wait("sync", eV)
            bb.op("sync", lambda E, v=v: E.dma_start(out=sc_B[v], in_=AA["Bt"][:]),
                  sem=dP, inck=16)

        def mean_pass(src, nk, F, acc_col):
            for k in range(nk):
                buf = (AA["xg"] if k % 2 == 0 else AA["xn1"])[:, 0:F]
                bb.wait("sync", eV)
                bb.op("sync", lambda E, b=buf, s=src, k=k: E.dma_start(
                    out=b, in_=s[k * 128:(k + 1) * 128, :]), sem=dP, inck=16)
                bb.wait("vector", dP)
                col = acc_col if k == 0 else 62
                bb.op("vector", lambda E, b=buf, c=col: E.tensor_reduce(
                    ST[:, c:c + 1], b, AX.X, OP.add, apply_absolute_value=True), inc=True)
                if k > 0:
                    bb.wait("vector", eV)
                    bb.op("vector", lambda E, c=acc_col: E.tensor_tensor(
                        ST[:, c:c + 1], ST[:, c:c + 1], ST[:, 62:63], OP.add), inc=True)

        mean_pass(w_w1T, 8, HID, 56)
        mean_pass(w_w2T, 32, D, 57)
        mean_pass(w_l1fT, 8, D, 58)
        bb.wait("tensor", eV)
        for i, c in enumerate((56, 57, 58)):
            bb.op("tensor", lambda E, c=c, i=i: E.matmul(
                PS[6][0:1, 4 * i:4 * i + 1], AA["c_oc"][:, 0:1], ST[:, c:c + 1],
                start=True, stop=True), inc=(i == 2))
        bb.wait("vector", eT)
        bb.op("vector", lambda E: E.tensor_scalar_mul(PS[6][0:1, 16:17], PS[6][0:1, 0:1], 1.0 / (D * HID)))
        bb.op("vector", lambda E: E.tensor_scalar_mul(PS[6][0:1, 17:18], PS[6][0:1, 4:5], 1.0 / (HID * D)))
        bb.op("vector", lambda E: E.tensor_scalar_mul(PS[6][0:1, 18:19], PS[6][0:1, 8:9], 1.0 / (D * D)),
              inc=True)
        bb.wait("vector", eV)
        bb.op("vector", lambda E: E.tensor_scalar_max(PS[6][0:1, 16:17], PS[6][0:1, 16:17], 1e-5))
        bb.op("vector", lambda E: E.tensor_scalar_max(PS[6][0:1, 17:18], PS[6][0:1, 17:18], 1e-5))
        bb.op("vector", lambda E: E.tensor_scalar_add(PS[6][0:1, 18:19], PS[6][0:1, 18:19], 1e-8),
              inc=True)
        bb.wait("vector", eV)
        bb.op("vector", lambda E: E.tensor_copy(ST[0:1, 59:62], PS[6][0:1, 16:19]), inc=True)
        bb.wait("tensor", eV)
        bb.op("tensor", lambda E: E.matmul(PS[7][:, 0:3], AA["c_or"][0:1, :], ST[0:1, 59:62],
                                           start=True, stop=True), inc=True)
        bb.wait("vector", eT)
        bb.op("vector", lambda E: E.tensor_copy(AA["c_wm"][:, 0:3], PS[7][:, 0:3]), inc=True)
        bb.wait("vector", eV)
        bb.op("vector", lambda E: E.reciprocal(AA["c_wm"][:, 3:4], AA["c_wm"][:, 2:3]))
        bb.op("vector", lambda E: E.reciprocal(AA["c_wm"][:, 4:5], AA["c_wm"][:, 0:1]))
        bb.op("vector", lambda E: E.reciprocal(AA["c_wm"][:, 5:6], AA["c_wm"][:, 1:2]), inc=True)

        def hi_lo(src, F, nk, dh, dl, scale_col=None, col_per_k=False, qcols=None,
                  ccmajor=False):
            for k in range(nk):
                buf = (AA["xg"] if k % 2 == 0 else AA["xn1"])[:, 0:F]
                hib = AA["x1"].bitcast(BF16)[:, 0:F]
                lob = AA["Bt"].bitcast(BF16)[:, 0:F]
                bb.waitc("sync", eV, eS)
                bb.wait("sync", dP)
                bb.op("sync", lambda E, b=buf, s=src, k=k: E.dma_start(
                    out=b, in_=s[k * 128:(k + 1) * 128, :]), sem=dP, inck=16)
                bb.wait("scalar", dP)
                bb.wait("vector", dP)
                segs = [(0, F, None)] if qcols is None else [(0, qcols, 0.125), (qcols, F, None)]
                for a0, a1, imm in segs:
                    scap = ((scale_col[:, k:k + 1] if col_per_k else scale_col)
                            if scale_col is not None else (1.0 if imm is None else imm))
                    bb.op("vector", lambda E, b=buf, hb=hib, a0=a0, a1=a1, sc=scap:
                          E.tensor_scalar(hb[:, a0:a1], b[:, a0:a1], sc, None, OP.mult),
                          inc=True)
                    bb.wait("vector", eV)
                    bb.op("vector", lambda E, b=buf, hb=hib, lb=lob, a0=a0, a1=a1, sc=scap:
                          E.scalar_tensor_tensor(lb[:, a0:a1], b[:, a0:a1], sc,
                                                 hb[:, a0:a1], OP.mult, OP.subtract),
                          inc=True)
                bb.wait("sync", eV)
                if ccmajor:
                    for ci in range(F // 128):
                        bb.op("sync", lambda E, hb=hib, d=dh, k=k, ci=ci: E.dma_start(
                            out=d[ci][:, k * 128:(k + 1) * 128],
                            in_=hb[:, ci * 128:(ci + 1) * 128]), sem=dP, inck=16)
                        bb.op("sync", lambda E, lb=lob, d=dl, k=k, ci=ci: E.dma_start(
                            out=d[ci][:, k * 128:(k + 1) * 128],
                            in_=lb[:, ci * 128:(ci + 1) * 128]), sem=dP, inck=16)
                else:
                    bb.op("sync", lambda E, hb=hib, d=dh, k=k: E.dma_start(out=d[k], in_=hb),
                          sem=dP, inck=16)
                    bb.op("sync", lambda E, lb=lob, d=dl, k=k: E.dma_start(out=d[k], in_=lb),
                          sem=dP, inck=16)

        hi_lo(w_qkvT, 3 * D, 8, sc_qkvT[0], sc_qkvT[1], qcols=D)
        hi_lo(w_projT, D, 8, sc_projT[0], sc_projT[1])
        hi_lo(w_q1, D, 8, sc_wq[0], sc_wq[1], scale_col=AA["c_anorm"], col_per_k=True)
        hi_lo(w_k1, D, 8, sc_wk[0], sc_wk[1], scale_col=AA["c_anorm"], col_per_k=True)
        hi_lo(w_v1, D, 8, sc_wv[0], sc_wv[1], scale_col=AA["c_anorm"], col_per_k=True)
        hi_lo(w_o1, D, 8, sc_wo[0], sc_wo[1], scale_col=AA["c_onorm"])
        hi_lo(w_g1, INTER, 8, sc_wg[0], sc_wg[1], scale_col=AA["c_mnorm"],
              col_per_k=True, ccmajor=True)
        hi_lo(w_u1, INTER, 8, sc_wu[0], sc_wu[1], scale_col=AA["c_mnorm"],
              col_per_k=True, ccmajor=True)
        hi_lo(w_d1, D, 32, sc_wd[0], sc_wd[1])

        def ternary(src, F, nk, dst, recip_col, ccmajor=False):
            for k in range(nk):
                buf = (AA["xg"] if k % 2 == 0 else AA["xn1"])[:, 0:F]
                qb = AA["x1"].bitcast(BF16)[:, 0:F]
                bb.waitc("sync", eV, eS)
                bb.wait("sync", dP)
                bb.op("sync", lambda E, b=buf, s=src, k=k: E.dma_start(
                    out=b, in_=s[k * 128:(k + 1) * 128, :]), sem=dP, inck=16)
                bb.wait("vector", dP)
                for a0 in range(0, F, 2048):
                    a1 = min(a0 + 2048, F)
                    Fs = a1 - a0
                    bseg = buf[:, a0:a1]
                    qseg = qb[:, a0:a1]
                    s1 = AA["xnf_h"].bitcast(F32)[:, 0:Fs]
                    s2 = AA["xnf_l"].bitcast(F32)[:, 0:Fs]
                    bb.op("vector", lambda E, b=bseg, s1=s1, rc=recip_col: E.tensor_scalar(
                        s1, b, rc, 1.4990234375, OP.mult, OP.min), inc=True)
                    bb.wait("vector", eV)
                    bb.op("vector", lambda E, s1=s1: E.tensor_scalar_max(s1, s1, -1.4990234375),
                          inc=True)
                    bb.wait("scalar", eV)
                    bb.op("scalar", lambda E, s1=s1, s2=s2: E.activation(s2, s1, AF.Sign, bias=CPH),
                          inc=True)
                    bb.wait("scalar", eS)
                    bb.op("scalar", lambda E, s1=s1: E.activation(s1, s1, AF.Sign, bias=CMH),
                          inc=True)
                    bb.wait("vector", eS)
                    bb.op("vector", lambda E, s1=s1, s2=s2: E.tensor_tensor(s2, s2, s1, OP.add),
                          inc=True)
                    bb.wait("vector", eV)
                    bb.op("vector", lambda E, qs=qseg, s2=s2: E.tensor_scalar_mul(qs, s2, 0.5),
                          inc=True)
                bb.wait("sync", eV)
                if ccmajor:
                    for ci in range(F // 128):
                        bb.op("sync", lambda E, qb=qb, d=dst, k=k, ci=ci: E.dma_start(
                            out=d[ci][:, k * 128:(k + 1) * 128],
                            in_=qb[:, ci * 128:(ci + 1) * 128]), sem=dP, inck=16)
                else:
                    bb.op("sync", lambda E, qb=qb, d=dst, k=k: E.dma_start(out=d[k], in_=qb),
                          sem=dP, inck=16)

        ternary(w_w1T, HID, 8, sc_w1q, AA["c_wm"][:, 4:5], ccmajor=True)
        ternary(w_w2T, D, 32, sc_w2q, AA["c_wm"][:, 5:6])
        ternary(w_l1fT, D, 8, sc_l1f, AA["c_wm"][:, 3:4])

        bb.waitc("sync", eV, eS)
        bb.op("sync", lambda E: E.dma_start(out=ST[:, 48:56], in_=qkv_b_vcol[:]),
            sem=dP, inck=16)
        bb.op("sync", lambda E: E.dma_start(out=AA["tmp"][0:1, 0:D], in_=proj_b_row[:]),
              sem=dP, inck=16)
        for k in range(8):
            buf = AA["xg"][:, 0:D]
            if k > 0:
                bb.wait("sync", eT)
            bb.op("sync", lambda E, b=buf, k=k: E.dma_start(
                out=b, in_=w_projT[k * 128:(k + 1) * 128, :]), sem=dP, inck=16)
            bb.wait("tensor", dP)
            for half in range(2):
                bb.op("tensor", lambda E, b=buf, k=k, h=half: E.matmul(
                    PS[4 + h][0:1, 0:512], ST[:, 48 + k:49 + k],
                    b[:, h * 512:(h + 1) * 512], start=(k == 0), stop=(k == 7)),
                    inc=(half == 1))
        bb.wait("vector", eT)
        bb.wait("vector", dP)
        for half in range(2):
            bb.op("vector", lambda E, h=half: E.tensor_tensor(
                AA["tmp"][0:1, h * 512:(h + 1) * 512], AA["tmp"][0:1, h * 512:(h + 1) * 512],
                PS[4 + h][0:1, 0:512], OP.add), inc=(half == 1))
        for half in range(2):
            bb.wait("tensor", eV)
            bb.op("tensor", lambda E, h=half: E.matmul(
                PS[6][:, 0:512], AA["c_or"][0:1, :], AA["tmp"][0:1, h * 512:(h + 1) * 512],
                start=True, stop=True), inc=True)
            bb.wait("vector", eT)
            bb.op("vector", lambda E, h=half: E.tensor_copy(
                AA["c_pbrow"][:, h * 512:(h + 1) * 512], PS[6][:, 0:512]), inc=True)
            bb.wait("tensor", eV)

        bb.wait("sync", dP)
        bb.barrier()

    prep()
    tap("wm", AA["c_wm"][:, 0:8], (128, 8))

    def rg_dummy(b=7, col=480):
        bb.op("tensor", lambda E, b=b, col=col: E.matmul(
            PS[b][:, col:col + 1], AA["c_idb"], AA["c_idb"][:, 0:1],
            start=True, stop=True))

    rs = {"idx": 0, "free_at": [0] * NRING}
    dWs = [bb.dsem(f"w{i}") for i in range(NRING)]

    def ring_load(chunks):
        out = []
        for (src, ncol) in chunks:
            i = rs["idx"] % NRING
            rs["idx"] += 1
            if rs["free_at"][i] > 0:
                bb.wait("sync", eT, rs["free_at"][i])
            v = bb.op("sync", lambda E, i=i, s=src, n=ncol: E.dma_start(
                out=AA[f"ring{i}"][:, 0:n], in_=s), sem=dWs[i], inck=16)
            out.append((i, AA[f"ring{i}"], v, dWs[i]))
        return out

    def ring_consumed(slot):
        rs["free_at"][slot] = eT.n

    R = {}

    def rwait(eng, sem, *keys):
        for k in keys:
            v = R.get(k, 0)
            if v:
                bb.wait(eng, sem, v)

    def rbase(sem, keys):
        for k in keys:
            R[k] = sem.n

    def do_tile(seq, t):
        T0 = t * TILE
        first = (t == 0)
        last = (t == NT - 1)
        xrow = x_in[seq]

        bb.waitc("sync", eV, eS, eG, eT)
        for g in range(4):
            bb.op("sync", lambda E, g=g: E.dma_start(
                out=AA["xg"][:, g * D:(g + 1) * D],
                in_=xrow[T0 + g * 128: T0 + (g + 1) * 128, :]), sem=dX, inck=16)
        bb.wait("vector", eS)
        if first or last:
            bb.op("vector", lambda E: E.memset(AA["xh"][:], 0.0), inc=True)
            bb.wait("sync", eV)
        if not first:
            bb.op("sync", lambda E: E.dma_start(
                out=AA["xh"][120:128, :], in_=xrow[T0 - 8:T0, :]), sem=dX, inck=16)
        if not last:
            bb.op("sync", lambda E: E.dma_start(
                out=AA["xh"][0:8, :], in_=xrow[T0 + TILE:T0 + TILE + 8, :]), sem=dX, inck=16)

        bb.wait("vector", dX)
        bb.wait("scalar", dX)
        bb.wait("vector", eV)
        for g in range(4):
            bb.op("vector", lambda E, g=g: E.tensor_reduce(
                ST[:, g:g + 1], AA["xg"][:, g * D:(g + 1) * D], AX.X, OP.add), inc=True)
            bb.op("scalar", lambda E, g=g: E.activation(
                AA["sq"][:, 0:D], AA["xg"][:, g * D:(g + 1) * D], AF.Square,
                accum_out=ST[:, 6 + g:7 + g]), inc=True)
            bb.wait("scalar", eS)
        bb.op("vector", lambda E: E.tensor_reduce(ST[:, 5:6], AA["xh"][:], AX.X, OP.add),
              inc=True)
        bb.op("scalar", lambda E: E.activation(AA["sq"][:, 0:D], AA["xh"][:], AF.Square,
                                               accum_out=ST[:, 10:11]), inc=True)
        bb.wait("vector", eS)
        bb.wait("vector", eV)
        bb.op("vector", lambda E: E.tensor_scalar_mul(ST[:, 0:4], ST[:, 0:4], 1.0 / D))
        bb.op("vector", lambda E: E.tensor_scalar_mul(ST[:, 5:6], ST[:, 5:6], 1.0 / D))
        bb.op("vector", lambda E: E.tensor_scalar_mul(ST[:, 6:11], ST[:, 6:11], 1.0 / D),
              inc=True)
        bb.wait("vector", eV)
        bb.op("vector", lambda E: E.tensor_tensor(ST[:, 11:15], ST[:, 0:4], ST[:, 0:4], OP.mult))
        bb.op("vector", lambda E: E.tensor_tensor(ST[:, 15:16], ST[:, 5:6], ST[:, 5:6], OP.mult),
              inc=True)
        bb.wait("vector", eV)
        bb.op("vector", lambda E: E.tensor_tensor(ST[:, 11:15], ST[:, 6:10], ST[:, 11:15], OP.subtract))
        bb.op("vector", lambda E: E.tensor_tensor(ST[:, 15:16], ST[:, 10:11], ST[:, 15:16], OP.subtract),
              inc=True)
        bb.wait("scalar", eV)
        bb.op("scalar", lambda E: E.activation(ST[:, 16:21], ST[:, 11:16], AF.Ln, bias=CEPS5),
              inc=True)
        bb.wait("scalar", eS)
        bb.op("scalar", lambda E: E.activation(ST[:, 16:21], ST[:, 16:21], AF.Exp, scale=-0.5),
              inc=True)

        rwait("vector", dOp[0], ("obD", 0))
        rwait("vector", dOp[1], ("obD", 1))

        def _xn_stage(g, Wb):
            halo = (g == 4)
            xsrc = AA["xh"][:] if halo else AA["xg"][:, g * D:(g + 1) * D]
            mcol = ST[:, 5:6] if halo else ST[:, g:g + 1]
            rcol = ST[:, 20:21] if halo else ST[:, 16 + g:17 + g]
            acol = 25 if halo else 21 + g
            scol = 30 if halo else 26 + g
            dcol = 35 if halo else 31 + g
            tcol = 54 if g % 2 == 0 else 55
            bb.wait("vector", eS)
            bb.op("vector", lambda E, xs=xsrc, m=mcol, r=rcol: E.tensor_scalar(
                AA["tmp"][:], xs, m, r, OP.subtract, OP.mult), inc=True)
            bb.wait("scalar", eV)
            bb.op("scalar", lambda E: E.activation(AA["tms_h"][:], AA["tmp"][:], AF.Copy),
                  inc=True)
            bb.wait("vector", eS)
            bb.op("vector", lambda E: E.tensor_tensor(
                AA["tms_l"][:], AA["tmp"][:], AA["tms_h"][:], OP.subtract))
            bb.op("vector", lambda E, a=acol: E.tensor_reduce(
                ST[:, a:a + 1], AA["tmp"][:], AX.X, OP.max, apply_absolute_value=True),
                inc=True)
            bb.wait("vector", eV)
            bb.op("vector", lambda E, a=acol: E.tensor_scalar_max(
                ST[:, a:a + 1], ST[:, a:a + 1], 1e-5), inc=True)
            bb.wait("vector", eV)
            bb.op("vector", lambda E, a=acol, tc=tcol: E.reciprocal(
                ST[:, tc:tc + 1], ST[:, a:a + 1]))
            bb.op("vector", lambda E, a=acol, d=dcol: E.tensor_tensor(
                ST[:, d:d + 1], ST[:, a:a + 1], AA["c_wm"][:, 0:1], OP.mult), inc=True)
            bb.wait("vector", eV)
            bb.op("vector", lambda E, s=scol, tc=tcol: E.tensor_scalar_mul(
                ST[:, s:s + 1], ST[:, tc:tc + 1], 127.0))
            bb.op("vector", lambda E, d=dcol: E.tensor_scalar_mul(
                ST[:, d:d + 1], ST[:, d:d + 1], 1.0 / 127.0), inc=True)
            bb.wait("vector", eV)
            return bb.op("vector", lambda E, s=scol, W=Wb: E.tensor_scalar(
                W[:, 0:D], AA["tmp"][:], ST[:, s:s + 1], 128.5, OP.mult, OP.add),
                inc=True)

        def _trans(g, specs):
            halo = (g == 4)
            bb.waitc("tensor", eV, eS)
            for dc in range(8):
                pv = PSB[dc % 2]
                for src_b, which in specs:
                    dst_fm = {"h": AA["xnf_h"], "l": AA["xnf_l"], "q": AA["xnq_fm"]}[which]
                    bb.op("tensor", lambda E, sb_=src_b, dc=dc, pv=pv:
                          E.transpose(pv[:, 0:128], sb_[:, dc * 128:(dc + 1) * 128],
                                      AA["c_idb"]), inc=True)
                    bb.wait("scalar", eT)
                    if halo:
                        bb.op("scalar", lambda E, d=dst_fm, dc=dc, pv=pv: E.copy(
                            d[:, dc * XNW + 0: dc * XNW + 8], pv[:, 120:128]))
                        bb.op("scalar", lambda E, d=dst_fm, dc=dc, pv=pv: E.copy(
                            d[:, dc * XNW + 520: dc * XNW + 528], pv[:, 0:8]), inc=True)
                    else:
                        bb.op("scalar", lambda E, d=dst_fm, dc=dc, pv=pv, g=g: E.copy(
                            d[:, dc * XNW + 8 + g * 128: dc * XNW + 8 + (g + 1) * 128],
                            pv[:, 0:128]), inc=True)
                    bb.wait("tensor", eS)

        for ga, gb in ((0, 1), (2, 3)):
            qda = AA["xnq"][:, ga * D:(ga + 1) * D]
            qdb = AA["xnq"][:, gb * D:(gb + 1) * D]
            wva = _xn_stage(ga, AA["qw"])
            _trans(ga, ((AA["tms_h"], "h"), (AA["tms_l"], "l")))
            wvb = _xn_stage(gb, AA["Bt"])
            _trans(gb, ((AA["tms_h"], "h"), (AA["tms_l"], "l")))
            round2(D, [(AA["qw"], AA["qr"], AA["qg"], qda, wva),
                       (AA["Bt"], AA["Bt"][:, D:2 * D], AA["x1"], qdb, wvb)])
            _trans(ga, ((qda, "q"),))
            _trans(gb, ((qdb, "q"),))
        qdh = AA["xnqh"][:]
        wvh = _xn_stage(4, AA["qw"])
        _trans(4, ((AA["tms_h"], "h"), (AA["tms_l"], "l")))
        round2(D, [(AA["qw"], AA["qr"], AA["qg"], qdh, wvh)])
        _trans(4, ((qdh, "q"),))

        tap("xn_t", AA["tmp"][:], (128, D))
        tap("xnq", AA["xnq"].bitcast(F32)[:, 0:2 * D], (128, 2 * D))

        def _passthrough():
            bb.waitc("sync", eV, eS, eT, eG)
            for g in range(4):
                bb.op("sync", lambda E, g=g: E.dma_start(
                    out=y_out[seq][T0 + g * 128: T0 + (g + 1) * 128, :],
                    in_=AA["xg"][:, g * D:(g + 1) * D]), sem=dO, inck=16)
        if stop_after == "ln":
            _passthrough()
            return

        for fog in range(4):
            isq = fog < 2
            base = 8 if isq else 2
            width = TILE if isq else KW
            bb.waitc("tensor", eS, eV)
            for kc in range(8):
                sh_i, sh_ap, sh_v, sh_s = ring_load(
                    [(sc_qkvT[0][kc][:, fog * 512:(fog + 1) * 512], 512)])[0]
                sl_i, sl_ap, sl_v, sl_s = ring_load(
                    [(sc_qkvT[1][kc][:, fog * 512:(fog + 1) * 512], 512)])[0]
                bb.wait("tensor", sh_s, sh_v)
                bb.wait("tensor", sl_s, sl_v)
                ngi = 1 if isq else 2
                for fi in range(4):
                    for gi, (n0, n1) in enumerate(((0, 512),) if isq else ((0, 512), (512, KW))):
                        pb = fi if gi == 0 else 6
                        po = 0 if gi == 0 else fi * 128
                        for pi, (wap, xap) in enumerate((
                                (sh_ap, AA["xnf_h"]), (sh_ap, AA["xnf_l"]), (sl_ap, AA["xnf_h"]))):
                            if gi == 0:
                                st = (kc == 0 and pi == 0)
                                sp = (kc == 7 and pi == 2)
                            else:
                                st = (kc == 0 and fi == 0 and pi == 0)
                                sp = (kc == 7 and fi == 3 and pi == 2)
                            bb.op("tensor", lambda E, w=wap, x=xap, fi=fi, kc=kc, n0=n0, n1=n1,
                                  pb=pb, po=po, base=base, pi=pi, st=st, sp=sp:
                                  E.matmul(PS[pb][:, po:po + n1 - n0],
                                           w[:, fi * 128:(fi + 1) * 128],
                                           x[:, kc * XNW + base + n0: kc * XNW + base + n1],
                                           start=st, stop=sp),
                                  inc=(fi == 3 and gi == ngi - 1 and pi == 2))
                ring_consumed(sh_i)
                ring_consumed(sl_i)
            bb.wait("scalar", eT)
            bb.wait("vector", eT)
            dsth = AA["q_fm_h"] if isq else AA["k_fm_h"]
            dstl = AA["q_fm_l"] if isq else AA["k_fm_l"]
            for phase in range(2):
                for fi in range(4):
                    fo = fog * 4 + fi
                    foc = fo if isq else fo - 8
                    bcol = AA["c_qkvb"][:, fo:fo + 1]
                    for gi, (n0, n1) in enumerate(((0, 512),) if isq else ((0, 512), (512, KW))):
                        pb = fi if gi == 0 else 6
                        po = 0 if gi == 0 else fi * 128
                        if phase == 0:
                            bb.op("vector", lambda E, d=dsth, foc=foc, w=width, pb=pb, po=po,
                                  bc=bcol, n0=n0, n1=n1: E.tensor_scalar(
                                      d[:, foc * w + n0: foc * w + n1],
                                      PS[pb][:, po:po + n1 - n0], bc, None, OP.add),
                                  inc=(fi == 3 and gi == (0 if isq else 1)))
                        else:
                            bb.op("vector", lambda E, dh=dsth, dl=dstl, foc=foc, w=width, pb=pb,
                                  po=po, bc=bcol, n0=n0, n1=n1: E.scalar_tensor_tensor(
                                      dl[:, foc * w + n0: foc * w + n1],
                                      PS[pb][:, po:po + n1 - n0], bc,
                                      dh[:, foc * w + n0: foc * w + n1], OP.add, OP.subtract),
                                  inc=True)
                if phase == 0:
                    bb.wait("vector", eV)

        tap("k_fm_h", AA["k_fm_h"].bitcast(F32)[:, 0:4 * KW], (128, 4 * KW))
        if stop_after == "qkv":
            _passthrough()
            return

        rbase(eV, [("sc_bias", 0), ("sc_bias", 1), ("x1",), ("vV",), ("bias_last",)])
        rbase(eS, [("ssbS", 0), ("ssbS", 1), ("ptS", 0), ("ptS", 1), ("ofhS",),
                   ("vS",)])
        rbase(eT, [("etT", 0), ("etT", 1), ("avT", 0), ("avT", 1), ("av_last",)])
        for sub in range(NSUB):
            i0 = sub * 128
            var = 1 if (first and sub == 0) else (2 if (last and sub == NSUB - 1) else 0)
            rwait("sync", eV, ("bias_last",))
            bv = bb.op("sync", lambda E, v=var: E.dma_start(out=AA["Bt"][:], in_=sc_B[v]),
                       sem=dBt, inck=16)

            rwait("tensor", eV, ("x1",))
            rwait("tensor", eS, ("ofhS",))
            for kc in range(8):
                vh_i, vh_ap, vh_v, vh_s = ring_load([(sc_qkvT[0][kc][:, 2 * D:3 * D], D)])[0]
                vl_i, vl_ap, vl_v, vl_s = ring_load([(sc_qkvT[1][kc][:, 2 * D:3 * D], D)])[0]
                bb.wait("tensor", vh_s, vh_v)
                bb.wait("tensor", vl_s, vl_v)
                for half in range(2):
                    for pi, (wap, xap) in enumerate((
                            (vh_ap, AA["xnf_h"]), (vh_ap, AA["xnf_l"]), (vl_ap, AA["xnf_h"]))):
                        bb.op("tensor", lambda E, w=wap, x=xap, kc=kc, half=half, i0=i0, pi=pi:
                              E.matmul(PS[4 + half][:, 0:512],
                                       x[:, kc * XNW + i0 + 2: kc * XNW + i0 + 130],
                                       w[:, half * 512:(half + 1) * 512],
                                       start=(kc == 0 and pi == 0),
                                       stop=(kc == 7 and pi == 2)))
                        bb.op("tensor", lambda E, w=wap, x=xap, kc=kc, half=half, i0=i0, pi=pi:
                              E.matmul(PS[6 + half][0:12, 0:512],
                                       x[:, kc * XNW + i0 + 130: kc * XNW + i0 + 142],
                                       w[:, half * 512:(half + 1) * 512],
                                       start=(kc == 0 and pi == 0),
                                       stop=(kc == 7 and pi == 2)),
                              inc=(half == 1 and pi == 2))
                ring_consumed(vh_i)
                ring_consumed(vl_i)
            bb.wait("scalar", eT)
            for half in range(2):
                bb.op("scalar", lambda E, h=half: E.copy(
                    AA["vt_h"][:, h * 512:(h + 1) * 512], PS[4 + h][:, 0:512]))
                v_ = bb.op("scalar", lambda E, h=half: E.copy(
                    AA["vtm_h"][0:12, h * 512:(h + 1) * 512], PS[6 + h][0:12, 0:512]),
                    inc=(half == 1))
            R[("vS",)] = v_
            bb.wait("vector", eS)
            for half in range(2):
                bb.op("vector", lambda E, h=half: E.tensor_tensor(
                    AA["vt_l"][:, h * 512:(h + 1) * 512], PS[4 + h][:, 0:512],
                    AA["vt_h"][:, h * 512:(h + 1) * 512], OP.subtract))
                v_ = bb.op("vector", lambda E, h=half: E.tensor_tensor(
                    AA["vtm_l"][0:12, h * 512:(h + 1) * 512], PS[6 + h][0:12, 0:512],
                    AA["vtm_h"][0:12, h * 512:(h + 1) * 512], OP.subtract), inc=(half == 1))
            R[("vV",)] = v_

            if stop_after == "att1" and sub == 0:
                _passthrough()
                return
            bb.wait("vector", dBt, bv)
            at_ = {}

            def _ap(hp):
                par = hp % 2
                return (par, (2, 3) if par == 0 else (6, 7), 48 if par == 0 else 58,
                        AA["ssb"][:, par * 2 * JW:(par + 1) * 2 * JW],
                        AA["et_h"][:, par * 2 * JW:(par + 1) * 2 * JW],
                        AA["et_l"][:, par * 2 * JW:(par + 1) * 2 * JW],
                        AA["pt_h"][:, par * 128:(par + 1) * 128],
                        AA["pt_l"][:, par * 128:(par + 1) * 128],
                        AA["ptm_h"][:, par * 128:(par + 1) * 128],
                        AA["ptm_l"][:, par * 128:(par + 1) * 128])

            for it in range(9):
                if it < 8:
                    hp = it
                    par, tb, stb, ssb_, eth_, etl_, pth_, ptl_, ptmh_, ptml_ = _ap(hp)
                    sbank = par
                    rwait("tensor", eV, ("sc_bias", par))
                    if hp == 1:
                        rwait("tensor", eV, ("vV",))
                        rwait("tensor", eS, ("vS",))
                    scv = None
                    for hh in range(2):
                        h = 2 * hp + hh
                        fo = h // 2
                        pr = (h % 2) * 64
                        rg_dummy(sbank, 511)
                        for pi, (qa, ka) in enumerate((
                                (AA["q_fm_h"], AA["k_fm_h"]), (AA["q_fm_h"], AA["k_fm_l"]),
                                (AA["q_fm_l"], AA["k_fm_h"]))):
                            scv = bb.op("tensor", lambda E, qa=qa, ka=ka, fo=fo, pr=pr,
                                  i0=i0, sb_=sbank, hh=hh, pi=pi:
                                  E.matmul(PS[sb_][:, hh * JW:(hh + 1) * JW],
                                           qa[pr:pr + 64, fo * TILE + i0: fo * TILE + i0 + 128],
                                           ka[pr:pr + 64, fo * KW + i0: fo * KW + i0 + JW],
                                           start=(pi == 0), stop=(pi == 2)),
                                  inc=(hh == 1 and pi == 2))
                    rg_dummy(sbank, 511)
                    rwait("vector", eS, ("ssbS", par))
                    bb.wait("vector", eT, scv)
                    R[("sc_bias", par)] = bb.op(
                        "vector", lambda E, sb_=sbank, hp=hp, d=ssb_: E.tensor_tensor(
                            d[:], PS[sb_][:, 0:2 * JW],
                            AA["Bt"][:, hp * 2 * JW:(hp + 1) * 2 * JW], OP.add), inc=True)
                    if hp == 7:
                        R[("bias_last",)] = R[("sc_bias", par)]
                    bb.wait("vector", eV)
                    bb.op("vector", lambda E, d=ssb_, stb=stb: E.tensor_reduce(
                        ST[:, stb:stb + 2], d.rearrange("p (h j) -> p h j", j=JW), AX.X,
                        OP.max), inc=True)
                    bb.wait("vector", eV)
                    at_[hp] = {"neg": bb.op("vector", lambda E, stb=stb: E.tensor_scalar_mul(
                        ST[:, stb + 2:stb + 4], ST[:, stb:stb + 2], -1.0), inc=True)}
                if it >= 1:
                    hp = it - 1
                    par, tb, stb, ssb_, eth_, etl_, pth_, ptl_, ptmh_, ptml_ = _ap(hp)
                    bb.wait("vector", eS, at_[hp]["exp"])
                    bb.op("vector", lambda E, d=ssb_, stb=stb: E.tensor_reduce(
                        ST[:, stb:stb + 2], d.rearrange("p (h j) -> p h j", j=JW), AX.X,
                        OP.add), inc=True)
                    bb.wait("vector", eV)
                    bb.op("vector", lambda E, stb=stb: E.reciprocal(
                        ST[:, stb + 4:stb + 6], ST[:, stb:stb + 2]), inc=True)
                    bb.wait("vector", eV)
                    for hh in range(2):
                        v_ = bb.op("vector", lambda E, hh=hh, d=ssb_, stb=stb: E.tensor_scalar(
                            d[:, hh * JW:(hh + 1) * JW], d[:, hh * JW:(hh + 1) * JW],
                            ST[:, stb + 4 + hh:stb + 5 + hh], None, OP.mult), inc=(hh == 1))
                    at_[hp]["norm"] = v_
                    if stop_after == "att2" and sub == 0 and hp == 0:
                        _passthrough()
                        return
                if it < 8:
                    hp = it
                    par, tb, stb, ssb_, eth_, etl_, pth_, ptl_, ptmh_, ptml_ = _ap(hp)
                    bb.wait("scalar", eV, at_[hp]["neg"])
                    for hh in range(2):
                        v_ = bb.op("scalar", lambda E, hh=hh, d=ssb_, stb=stb: E.activation(
                            d[:, hh * JW:(hh + 1) * JW], d[:, hh * JW:(hh + 1) * JW],
                            AF.Exp, bias=ST[:, stb + 2 + hh:stb + 3 + hh]), inc=(hh == 1))
                    at_[hp]["exp"] = v_
                if it >= 1:
                    hp = it - 1
                    par, tb, stb, ssb_, eth_, etl_, pth_, ptl_, ptmh_, ptml_ = _ap(hp)
                    sbank = par
                    rwait("scalar", eT, ("etT", par))
                    bb.wait("scalar", eV, at_[hp]["norm"])
                    R[("ssbS", par)] = bb.op(
                        "scalar", lambda E, d=ssb_, e=eth_: E.activation(e[:], d[:], AF.Copy),
                        inc=True)
                    rwait("vector", eT, ("etT", par))
                    bb.wait("vector", eS, R[("ssbS", par)])
                    at_[hp]["etl"] = bb.op(
                        "vector", lambda E, d=ssb_, eh=eth_, el=etl_: E.tensor_tensor(
                            el[:], d[:], eh[:], OP.subtract), inc=True)
                    bb.wait("tensor", eV, at_[hp]["etl"])
                    rwait("tensor", eS, ("ptS", par))
                    for hh in range(2):
                        h = 2 * hp + hh
                        e_h = eth_[:, hh * JW:(hh + 1) * JW]
                        e_l = etl_[:, hh * JW:(hh + 1) * JW]
                        if hp == 0 and hh == 0:
                            rwait("tensor", eV, ("vV",))
                            rwait("tensor", eS, ("vS",))
                        bb.op("tensor", lambda E, e=e_h, tb=tb: E.transpose(
                            PSB[tb[0]][:, 0:128], e[:, 0:128], AA["c_idb"]))
                        bb.op("tensor", lambda E, e=e_l, tb=tb: E.transpose(
                            PSB[tb[1]][:, 0:128], e[:, 0:128], AA["c_idb"]))
                        bb.op("tensor", lambda E, e=e_h, tb=tb: E.transpose(
                            PSB[tb[0]][0:12, 128:256], e[:, 128:140], AA["c_idb"]))
                        v_ = bb.op("tensor", lambda E, e=e_l, tb=tb: E.transpose(
                            PSB[tb[1]][0:12, 128:256], e[:, 128:140], AA["c_idb"]), inc=True)
                        if hh == 1:
                            R[("etT", par)] = v_
                        bb.wait("scalar", eT)
                        bb.op("scalar", lambda E, tb=tb, d=pth_: E.copy(
                            d[:], PSB[tb[0]][:, 0:128]))
                        bb.op("scalar", lambda E, tb=tb, d=ptl_: E.copy(
                            d[:], PSB[tb[1]][:, 0:128]))
                        bb.op("scalar", lambda E, tb=tb, d=ptmh_: E.copy(
                            d[0:12, :], PSB[tb[0]][0:12, 128:256]))
                        v_ = bb.op("scalar", lambda E, tb=tb, d=ptml_: E.copy(
                            d[0:12, :], PSB[tb[1]][0:12, 128:256]), inc=True)
                        if hh == 1:
                            R[("ptS", par)] = v_
                        bb.wait("tensor", eS)
                        ob = 4 + (h // 8)
                        osl = (h % 8) // 2
                        opr = (h % 2) * 64
                        vc = h * 64
                        for pi, (pa, va) in enumerate((
                                (pth_, AA["vt_h"]), (pth_, AA["vt_l"]),
                                (ptl_, AA["vt_h"]))):
                            bb.op("tensor", lambda E, pa=pa, va=va, ob=ob, osl=osl, opr=opr,
                                  vc=vc, pi=pi:
                                  E.matmul(PS[ob][opr:opr + 64, osl * 128:(osl + 1) * 128],
                                           va[:, vc:vc + 64], pa[:],
                                           start=(pi == 0), stop=False))
                        rg_dummy(sbank, 511)
                        for pi, (ma, mv) in enumerate((
                                (ptmh_, AA["vtm_h"]), (ptmh_, AA["vtm_l"]),
                                (ptml_, AA["vtm_h"]))):
                            v_ = bb.op("tensor", lambda E, ma=ma, mv=mv, ob=ob, osl=osl,
                                  opr=opr, vc=vc, pi=pi:
                                  E.matmul(PS[ob][opr:opr + 64, osl * 128:(osl + 1) * 128],
                                           mv[0:12, vc:vc + 64], ma[0:12, :],
                                           start=False, stop=(pi == 2)),
                                  inc=(pi == 2))
                        if hh == 1:
                            R[("avT", par)] = v_
                            if hp == 7:
                                R[("av_last",)] = v_
                        rg_dummy(sbank, 511)

            if stop_after == "att3" and sub == 0:
                _passthrough()
                return
            bb.wait("scalar", eT)
            for hp in range(8):
                ob = 4 + hp // 4
                osl = hp % 4
                v_ = bb.op("scalar", lambda E, ob=ob, osl=osl, hp=hp: E.copy(
                    AA["of_h"][:, hp * 128:(hp + 1) * 128],
                    PS[ob][:, osl * 128:(osl + 1) * 128]), inc=(hp == 7))
            R[("ofhS",)] = v_
            bb.wait("vector", eS)
            for hp in range(8):
                ob = 4 + hp // 4
                osl = hp % 4
                bb.op("vector", lambda E, ob=ob, osl=osl, hp=hp: E.tensor_tensor(
                    AA["of_l"][:, hp * 128:(hp + 1) * 128],
                    PS[ob][:, osl * 128:(osl + 1) * 128],
                    AA["of_h"][:, hp * 128:(hp + 1) * 128], OP.subtract), inc=(hp == 7))

            bb.wait("tensor", eV)
            for kc in range(8):
                ph_i, ph_ap, ph_v, ph_s = ring_load([(sc_projT[0][kc], D)])[0]
                pl_i, pl_ap, pl_v, pl_s = ring_load([(sc_projT[1][kc], D)])[0]
                bb.wait("tensor", ph_s, ph_v)
                bb.wait("tensor", pl_s, pl_v)
                for half in range(2):
                    for pi, (oa, wa) in enumerate((
                            (AA["of_h"], ph_ap), (AA["of_h"], pl_ap), (AA["of_l"], ph_ap))):
                        bb.op("tensor", lambda E, oa=oa, wa=wa, kc=kc, half=half, pi=pi:
                              E.matmul(PS[6 + half][:, 0:512],
                                       oa[:, kc * 128:(kc + 1) * 128],
                                       wa[:, half * 512:(half + 1) * 512],
                                       start=(kc == 0 and pi == 0),
                                       stop=(kc == 7 and pi == 2)),
                              inc=(half == 1 and pi == 2))
                ring_consumed(ph_i)
                ring_consumed(pl_i)
            bb.wait("vector", eT)
            for half in range(2):
                bb.op("vector", lambda E, h=half, sub=sub: E.tensor_tensor(
                    AA["x1"][:, sub * D + h * 512: sub * D + (h + 1) * 512],
                    PS[6 + h][:, 0:512], AA["c_pbrow"][:, h * 512:(h + 1) * 512], OP.add),
                    inc=(half == 1))
            bb.wait("vector", eV)
            for half in range(2):
                v_ = bb.op("vector", lambda E, h=half, sub=sub: E.tensor_tensor(
                    AA["x1"][:, sub * D + h * 512: sub * D + (h + 1) * 512],
                    AA["x1"][:, sub * D + h * 512: sub * D + (h + 1) * 512],
                    AA["xg"][:, sub * D + h * 512: sub * D + (h + 1) * 512], OP.add),
                    inc=(half == 1))
            R[("x1",)] = v_

        tap("x1a", AA["x1"][:, 0:4 * D], (128, 4 * D))
        if stop_after == "attn":
            _passthrough()
            return

        bb.waitc("tensor", eV, eS)
        for gi in range(4):
            bb.op("tensor", lambda E, gi=gi: E.transpose(
                PS[4][0:1, gi * 128:(gi + 1) * 128], ST[:, 31 + gi:32 + gi], AA["c_idf"]))
        bb.op("tensor", lambda E: E.transpose(
            PS[6][0:1, 0:128], ST[:, 35:36], AA["c_idf"]), inc=True)
        bb.wait("vector", eT)
        bb.op("vector", lambda E: E.tensor_copy(AA["qw"][0:1, 0:512], PS[4][0:1, 0:512]))
        bb.op("vector", lambda E: E.tensor_copy(AA["qw"][0:1, 512:640], PS[6][0:1, 0:128]),
              inc=True)
        bb.wait("tensor", eV)
        for g in range(4):
            bb.op("tensor", lambda E, g=g: E.matmul(
                PS[5][:, g * 128:(g + 1) * 128], AA["c_or"][0:1, :],
                AA["qw"][0:1, g * 128:(g + 1) * 128], start=True, stop=True))
        bb.op("tensor", lambda E: E.matmul(PS[6][:, 0:2], AA["c_or"][0:1, :],
                                           AA["qw"][0:1, 638:640], start=True, stop=True))
        bb.op("tensor", lambda E: E.matmul(PS[6][:, 2:4], AA["c_or"][0:1, :],
                                           AA["qw"][0:1, 512:514], start=True, stop=True),
              inc=True)
        bb.wait("vector", eT)
        bb.op("vector", lambda E: E.tensor_copy(AA["dqrow"][:, 2:514], PS[5][:, 0:512]))
        bb.op("vector", lambda E: E.tensor_copy(AA["dqrow"][:, 0:2], PS[6][:, 0:2]))
        bb.op("vector", lambda E: E.tensor_copy(AA["dqrow"][:, 514:516], PS[6][:, 2:4]))
        bb.op("vector", lambda E: E.memset(AA["hacc"][:], -3.0e38))
        bb.op("vector", lambda E: E.memset(AA["hacc2"][:], 3.0e38), inc=True)
        bb.wait("tensor", eV)

        rbase(eV, [("faV", 0), ("faV", 1)])
        rbase(eS, [("faS", 0), ("faS", 1)])
        R[("faH", 0)] = dHa[0].n
        R[("faH", 1)] = dHa[1].n
        w1pre = ring_load([(sc_w1q[0], D)])[0]
        fa = {}
        for it in range(34):
            if it >= 2:
                cc = it - 2
                par = cc % 2
                hof = par * HW_
                bb.wait("vector", eS, fa[cc]["silu"])
                if cc >= 1:
                    bb.wait("vector", eV, fa[cc - 1]["folds"])
                bb.op("vector", lambda E, hof=hof: E.tensor_tensor(
                    AA["hacc"][:], AA["hacc"][:], AA["ha"][:, hof + 1:hof + 513], OP.max))
                fa[cc]["folds"] = bb.op("vector", lambda E, hof=hof: E.tensor_tensor(
                    AA["hacc2"][:], AA["hacc2"][:], AA["ha"][:, hof + 1:hof + 513],
                    OP.min), inc=True)
                bb.wait("sync", eS, fa[cc]["silu"])
                R[("faH", par)] = bb.op("sync", lambda E, cc=cc, hof=hof: E.dma_start(
                    out=sc_hact[cc][:, 0:514], in_=AA["ha"][:, hof:hof + 514]),
                    sem=dHa[par], inck=16)
            if 1 <= it <= 32:
                cc = it - 1
                par = cc % 2
                hof = par * HW_
                bufW = AA["qw"] if par == 0 else AA["tmp"]
                bufR = AA["qr"] if par == 0 else AA["qg"]
                w1c = AA["c_convw"][:, 32 + cc:33 + cc]
                w2c = AA["c_convw"][:, 64 + cc:65 + cc]
                bbc = AA["c_convb"][:, cc:cc + 1]
                bb.wait("vector", eS, fa[cc]["w0"])
                bb.op("vector", lambda E, hof=hof, w2=w2c, bw=bufW, br=bufR:
                      E.scalar_tensor_tensor(
                          br[:, 0:514], AA["ha"][:, hof + 2:hof + 516], w2, bw[:, 0:514],
                          OP.mult, OP.add), inc=True)
                bb.wait("vector", eV)
                fa[cc]["stt1"] = bb.op(
                    "vector", lambda E, hof=hof, w1=w1c, br=bufR: E.scalar_tensor_tensor(
                        br[:, 0:514], AA["ha"][:, hof + 1:hof + 515], w1, br[:, 0:514],
                        OP.mult, OP.add), inc=True)
                bb.wait("scalar", eV, fa[cc]["stt1"])
                fa[cc]["silu"] = bb.op(
                    "scalar", lambda E, hof=hof, b_=bbc, br=bufR: E.activation(
                        AA["ha"][:, hof:hof + 514], br[:, 0:514], AF.Silu, bias=b_),
                    inc=True)
                R[("faS", par)] = fa[cc]["silu"]
            if it < 32:
                cc = it
                par = cc % 2
                hof = par * HW_
                pb0 = 0 + 2 * par
                pb1 = 1 + 2 * par
                s_i, s_ap, s_v, s_s = w1pre
                if cc < 31:
                    w1pre = ring_load([(sc_w1q[cc + 1], D)])[0]
                rwait("tensor", eV, ("faV", par))
                bb.wait("tensor", s_s, s_v)
                for kc in range(8):
                    bb.op("tensor", lambda E, w=s_ap, kc=kc, cc=cc, pb=pb0:
                          E.matmul(PS[pb][:, 0:512], w[:, kc * 128:(kc + 1) * 128],
                                   AA["xnq_fm"][:, kc * XNW + 6: kc * XNW + 518],
                                   start=(kc == 0), stop=(kc == 7)))
                    bb.op("tensor", lambda E, w=s_ap, kc=kc, cc=cc, pb=pb1:
                          E.matmul(PS[pb][:, 0:4], w[:, kc * 128:(kc + 1) * 128],
                                   AA["xnq_fm"][:, kc * XNW + 518: kc * XNW + 522],
                                   start=(kc == 0), stop=(kc == 7)),
                          inc=(kc == 7))
                ring_consumed(s_i)
                bb.wait("vector", eT)
                rwait("vector", eS, ("faS", par))
                rwait("vector", dHa[par], ("faH", par))
                bb.op("vector", lambda E, hof=hof, pb=pb0: E.tensor_tensor(
                    AA["ha"][:, hof:hof + 512], PS[pb][:, 0:512], AA["dqrow"][:, 0:512],
                    OP.mult))
                vq = bb.op("vector", lambda E, hof=hof, pb=pb1: E.tensor_tensor(
                    AA["ha"][:, hof + 512:hof + 516], PS[pb][:, 0:4],
                    AA["dqrow"][:, 512:516], OP.mult), inc=True)
                R[("faV", par)] = vq
                fa[cc] = {"dq": vq}
                bb.wait("scalar", eV, vq)
                fa[cc]["w0"] = bb.op(
                    "scalar", lambda E, hof=hof, w0=AA["c_convw"][:, cc:cc + 1],
                    bw=(AA["qw"] if par == 0 else AA["tmp"]): E.activation(
                        bw[:, 0:514], AA["ha"][:, hof:hof + 514], AF.Copy, scale=w0),
                    inc=True)
        bb.waitc("tensor", eV, eS)
        for g in range(4):
            bb.op("tensor", lambda E, g=g: E.transpose(
                PS[4][:, g * 128:(g + 1) * 128], AA["hacc"][:, g * 128:(g + 1) * 128],
                AA["c_idf"]))
            bb.op("tensor", lambda E, g=g: E.transpose(
                PS[7][:, g * 128:(g + 1) * 128], AA["hacc2"][:, g * 128:(g + 1) * 128],
                AA["c_idf"]), inc=(g == 3))
        bb.wait("vector", eT)
        bb.op("vector", lambda E: E.tensor_reduce(
            ST[:, 36:40], PS[4].rearrange("p (g i) -> p g i", i=128), AX.X, OP.max), inc=True)
        bb.op("vector", lambda E: E.tensor_reduce(
            ST[:, 54:58], PS[7].rearrange("p (g i) -> p g i", i=128), AX.X, OP.min), inc=True)
        bb.wait("vector", eV)
        bb.op("vector", lambda E: E.tensor_scalar_mul(ST[:, 54:58], ST[:, 54:58], -1.0),
              inc=True)
        bb.wait("vector", eV)
        bb.op("vector", lambda E: E.tensor_tensor(ST[:, 36:40], ST[:, 36:40], ST[:, 54:58],
                                                  OP.max), inc=True)
        bb.wait("vector", eV)
        bb.op("vector", lambda E: E.tensor_scalar_max(ST[:, 36:40], ST[:, 36:40], 1e-5),
              inc=True)
        bb.wait("vector", eV)
        bb.op("vector", lambda E: E.reciprocal(ST[:, 40:44], ST[:, 36:40]))
        bb.op("vector", lambda E: E.tensor_tensor(
            ST[:, 44:48], ST[:, 36:40], AA["c_wm"][:, 1:2].broadcast_to((128, 4)), OP.mult),
            inc=True)
        bb.wait("vector", eV)
        bb.op("vector", lambda E: E.tensor_scalar_mul(ST[:, 40:44], ST[:, 40:44], 127.0))
        bb.op("vector", lambda E: E.tensor_scalar_mul(ST[:, 44:48], ST[:, 44:48], 1.0 / 127.0),
              inc=True)
        bb.wait("tensor", eV)
        for gi in range(4):
            bb.op("tensor", lambda E, gi=gi: E.transpose(
                PS[6][0:1, gi * 128:(gi + 1) * 128], ST[:, 40 + gi:41 + gi], AA["c_idf"]),
                inc=(gi == 3))
        bb.wait("vector", eT)
        bb.op("vector", lambda E: E.tensor_copy(AA["qw"][0:1, 0:512], PS[6][0:1, 0:512]),
              inc=True)
        bb.wait("tensor", eV)
        for g in range(4):
            bb.op("tensor", lambda E, g=g: E.matmul(
                PS[5][:, g * 128:(g + 1) * 128], AA["c_or"][0:1, :],
                AA["qw"][0:1, g * 128:(g + 1) * 128], start=True, stop=True), inc=(g == 3))
        bb.wait("vector", eT)
        bb.op("vector", lambda E: E.tensor_copy(AA["srow"][:, 0:512], PS[5][:, 0:512]),
              inc=True)

        bb.wait("sync", dHa[0])
        bb.wait("sync", dHa[1])
        rbase(eV, [("fbV", 0), ("fbV", 1)])
        rbase(eT, [("fbT", 0), ("fbT", 1)])
        R[("fbH", 0)] = dHb[0].n
        R[("fbH", 1)] = dHb[1].n
        for rnd in range(2):
            tss = (0, 1) if rnd == 0 else (2, 3)
            if rnd == 1:
                bb.wait("sync", dHb[0])
                bb.wait("sync", dHb[1])
                rbase(eT, [("fbT", 0), ("fbT", 1)])

            def _ldB(cc, rnd=rnd):
                par = cc % 2
                if rnd == 0:
                    rwait("sync", eV, ("fbV", par))
                    v = bb.op("sync", lambda E, cc=cc, hof=par * HW_: E.dma_start(
                        out=AA["ha"][:, hof:hof + 514], in_=sc_hact[cc][:, 0:514]),
                        sem=dHp[par], inck=16)
                else:
                    rwait("sync", eT, ("fbT", par))
                    v = bb.op("sync", lambda E, cc=cc, par=par: E.dma_start(
                        out=AA["qhb"][:, par * TILE:(par + 1) * TILE], in_=sc_qh[cc]),
                        sem=dHp[par], inck=16)
                return (v, ring_load([(sc_w2q[cc], D)])[0])

            def _w2mm(cc, qdst, w2s, par):
                s_i, s_ap, s_v, s_s = w2s
                bb.wait("tensor", s_s, s_v)
                for tsi, ts in enumerate(tss):
                    for half in range(2):
                        bank = 4 + 2 * (ts % 2) + half
                        v_ = bb.op("tensor", lambda E, w=s_ap, cc=cc, ts=ts, half=half,
                              bank=bank, q=qdst: E.matmul(
                                  PS[bank][:, 0:512], q[:, ts * 128:(ts + 1) * 128],
                                  w[:, half * 512:(half + 1) * 512],
                                  start=(cc == 0), stop=(cc == 31)),
                              inc=(tsi == 1 and half == 1))
                ring_consumed(s_i)
                R[("fbT", par)] = v_

            if rnd == 0:
                ofhF = AA["of_h"].bitcast(F32)
                oflF = AA["of_l"].bitcast(F32)
                qd0 = AA["qhb"][:, 0:TILE]
                qd1 = AA["qhb"][:, TILE:2 * TILE]
                pre = (_ldB(0), _ldB(1))
                for pc in range(16):
                    cc0, cc1 = 2 * pc, 2 * pc + 1
                    (ldv0, w2s0), (ldv1, w2s1) = pre
                    bb.wait("vector", dHp[0], ldv0)
                    rwait("scalar", eT, ("fbT", 0))
                    rwait("scalar", dHb[0], ("fbH", 0))
                    bb.wait("vector", eV)
                    R[("fbV", 0)] = bb.op("vector", lambda E: E.tensor_tensor(
                        AA["qw"][:, 0:512], AA["ha"][:, 1:513], AA["srow"][:, 0:512],
                        OP.mult), inc=True)
                    bb.wait("vector", eV)
                    wv0 = bb.op("vector", lambda E: E.tensor_scalar_add(
                        AA["qw"][:, 0:512], AA["qw"][:, 0:512], 128.5), inc=True)
                    bb.wait("vector", dHp[1], ldv1)
                    rwait("scalar", eT, ("fbT", 1))
                    rwait("scalar", dHb[1], ("fbH", 1))
                    R[("fbV", 1)] = bb.op("vector", lambda E: E.tensor_tensor(
                        AA["tmp"][:, 0:512], AA["ha"][:, HW_ + 1:HW_ + 513],
                        AA["srow"][:, 0:512], OP.mult), inc=True)
                    bb.wait("vector", eV)
                    wv1 = bb.op("vector", lambda E: E.tensor_scalar_add(
                        AA["tmp"][:, 0:512], AA["tmp"][:, 0:512], 128.5), inc=True)
                    if pc < 15:
                        pre = (_ldB(cc0 + 2), _ldB(cc1 + 2))
                    fin = round2(512, [
                        (AA["qw"], AA["qr"], AA["qg"], qd0, wv0),
                        (AA["tmp"], ofhF, oflF, qd1, wv1)])
                    for par, (cc, qdst, w2s, fv) in enumerate(
                            ((cc0, qd0, w2s0, fin[0]), (cc1, qd1, w2s1, fin[1]))):
                        bb.wait("sync", eS, fv)
                        R[("fbH", par)] = bb.op(
                            "sync", lambda E, cc=cc, q=qdst: E.dma_start(
                                out=sc_qh[cc], in_=q), sem=dHb[par], inck=16)
                        bb.wait("tensor", eS, fv)
                        _w2mm(cc, qdst, w2s, par)
            else:
                pre = _ldB(0)
                for cc in range(32):
                    par = cc % 2
                    qdst = AA["qhb"][:, par * TILE:(par + 1) * TILE]
                    ldv, w2s = pre
                    if cc < 31:
                        pre = _ldB(cc + 1)
                    bb.wait("tensor", dHp[par], ldv)
                    _w2mm(cc, qdst, w2s, par)
            bb.wait("vector", eT)
            for ts in tss:
                for half in range(2):
                    bank = 4 + 2 * (ts % 2) + half
                    bb.op("vector", lambda E, ts=ts, half=half, bank=bank:
                          E.scalar_tensor_tensor(
                              AA["x1"][:, ts * D + half * 512: ts * D + (half + 1) * 512],
                              PS[bank][:, 0:512], ST[:, 44 + ts:45 + ts],
                              AA["x1"][:, ts * D + half * 512: ts * D + (half + 1) * 512],
                              OP.mult, OP.add), inc=True)
            bb.wait("tensor", eV)

        tap("x1", AA["x1"][:, 0:4 * D], (128, 4 * D))

        if stop_after == "L0":
            bb.waitc("sync", eV, eT)
            for g in range(4):
                bb.op("sync", lambda E, g=g: E.dma_start(
                    out=y_out[seq][T0 + g * 128: T0 + (g + 1) * 128, :],
                    in_=AA["x1"][:, g * D:(g + 1) * D]), sem=dO, inck=16)
            return
        do_tile_l1(seq, t)

    def do_tile_l1(seq, t):
        T0 = t * TILE

        bb.wait("vector", eV)
        bb.wait("vector", eT)
        bb.wait("scalar", eS)
        bb.wait("scalar", eV)
        bb.wait("scalar", eT)
        if t == 0:
            bb.wait("vector", eT)
            bb.op("vector", lambda E: E.memset(AA["S_sb"][:], 0.0), inc=True)
        for g in range(4):
            bb.op("vector", lambda E, g=g: E.tensor_reduce(
                ST[:, g:g + 1], AA["x1"][:, g * D:(g + 1) * D], AX.X, OP.add), inc=True)
            bb.op("scalar", lambda E, g=g: E.activation(
                AA["sq"][:, 0:D], AA["x1"][:, g * D:(g + 1) * D], AF.Square,
                accum_out=ST[:, 6 + g:7 + g]), inc=True)
            bb.wait("scalar", eS)
        bb.wait("vector", eS)
        bb.wait("vector", eV)
        bb.op("vector", lambda E: E.tensor_scalar_mul(ST[:, 0:4], ST[:, 0:4], 1.0 / D))
        bb.op("vector", lambda E: E.tensor_scalar_mul(ST[:, 6:10], ST[:, 6:10], 1.0 / D),
              inc=True)
        bb.wait("vector", eV)
        bb.op("vector", lambda E: E.tensor_tensor(ST[:, 11:15], ST[:, 0:4], ST[:, 0:4], OP.mult),
              inc=True)
        bb.wait("vector", eV)
        bb.op("vector", lambda E: E.tensor_tensor(ST[:, 11:15], ST[:, 6:10], ST[:, 11:15],
                                                  OP.subtract), inc=True)
        bb.wait("scalar", eV)
        bb.op("scalar", lambda E: E.activation(ST[:, 16:20], ST[:, 11:15], AF.Ln, bias=CEPS5),
              inc=True)
        bb.wait("scalar", eS)
        bb.op("scalar", lambda E: E.activation(ST[:, 16:20], ST[:, 16:20], AF.Exp, scale=-0.5),
              inc=True)
        bb.wait("vector", eS)
        for g in range(4):
            bb.op("vector", lambda E, g=g: E.tensor_scalar(
                AA["xn1"][:, g * D:(g + 1) * D], AA["x1"][:, g * D:(g + 1) * D],
                ST[:, g:g + 1], ST[:, 16 + g:17 + g], OP.subtract, OP.mult),
                inc=(g == 3))

        bb.wait("scalar", eV)
        for g in range(4):
            bb.op("scalar", lambda E, g=g: E.activation(
                AA["sq"][:, 0:D], AA["xn1"][:, g * D:(g + 1) * D], AF.Square,
                accum_out=ST[:, 21 + g:22 + g]), inc=True)
            bb.wait("scalar", eS)
        bb.op("scalar", lambda E: E.activation(ST[:, 26:30], ST[:, 21:25], AF.Ln, bias=CEPS5,
                                               scale=1.0 / D), inc=True)
        bb.wait("scalar", eS)
        bb.op("scalar", lambda E: E.activation(ST[:, 26:30], ST[:, 26:30], AF.Exp, scale=-0.5),
              inc=True)
        for g in range(4):
            bb.wait("vector", eS)
            bb.wait("vector", eT)
            bb.op("vector", lambda E, g=g: E.tensor_scalar(
                AA["tmp"][:], AA["xn1"][:, g * D:(g + 1) * D], ST[:, 26 + g:27 + g], None,
                OP.mult), inc=True)
            bb.wait("scalar", eV)
            bb.op("scalar", lambda E: E.activation(AA["tms_h"][:], AA["tmp"][:], AF.Copy),
                  inc=True)
            bb.wait("vector", eS)
            bb.op("vector", lambda E: E.tensor_tensor(
                AA["tms_l"][:], AA["tmp"][:], AA["tms_h"][:], OP.subtract), inc=True)
            bb.waitc("tensor", eV, eS)
            for dc in range(8):
                pv = PSB[dc % 2]
                for which, src_b in (("h", AA["tms_h"]), ("l", AA["tms_l"])):
                    dst = AA["h1f_h"] if which == "h" else AA["h1f_l"]
                    bb.op("tensor", lambda E, sb_=src_b, dc=dc, pv=pv: E.transpose(
                        pv[:, 0:128], sb_[:, dc * 128:(dc + 1) * 128], AA["c_idb"]), inc=True)
                    bb.wait("scalar", eT)
                    bb.op("scalar", lambda E, d=dst, dc=dc, g=g, pv=pv: E.copy(
                        d[:, dc * TILE + g * 128: dc * TILE + (g + 1) * 128], pv[:, 0:128]),
                        inc=True)
                    bb.wait("tensor", eS)

        for wwhich, scw, dsth, dstl in (("q", sc_wq, AA["q_fm_h"], AA["q_fm_l"]),
                                        ("k", sc_wk, AA["k_fm_h"], AA["k_fm_l"])):
            for fog in range(2):
                bb.waitc("tensor", eS, eV)
                for kc in range(8):
                    sh_i, sh_ap, sh_v, sh_s = ring_load(
                        [(scw[0][kc][:, fog * 512:(fog + 1) * 512], 512)])[0]
                    sl_i, sl_ap, sl_v, sl_s = ring_load(
                        [(scw[1][kc][:, fog * 512:(fog + 1) * 512], 512)])[0]
                    bb.wait("tensor", sh_s, sh_v)
                    bb.wait("tensor", sl_s, sl_v)
                    for fi in range(4):
                        for pi, (wap, xap) in enumerate((
                                (sh_ap, AA["h1f_h"]), (sh_ap, AA["h1f_l"]),
                                (sl_ap, AA["h1f_h"]))):
                            bb.op("tensor", lambda E, w=wap, x=xap, fi=fi, kc=kc, pi=pi:
                                  E.matmul(PS[fi][:, 0:512],
                                           w[:, fi * 128:(fi + 1) * 128],
                                           x[:, kc * TILE: (kc + 1) * TILE],
                                           start=(kc == 0 and pi == 0),
                                           stop=(kc == 7 and pi == 2)),
                                  inc=(fi == 3 and pi == 2))
                    ring_consumed(sh_i)
                    ring_consumed(sl_i)
                bb.wait("scalar", eT)
                for fi in range(4):
                    fo = fog * 4 + fi
                    bb.op("scalar", lambda E, d=dsth, fo=fo, fi=fi: E.copy(
                        d.bitcast(F32)[:, 0:1] if False else d[:, fo * TILE:(fo + 1) * TILE],
                        PS[fi][:, 0:512]), inc=True)
                bb.wait("vector", eS)
                for fi in range(4):
                    fo = fog * 4 + fi
                    bb.op("vector", lambda E, dh=dsth, dl=dstl, fo=fo, fi=fi: E.tensor_tensor(
                        dl[:, fo * TILE:(fo + 1) * TILE], PS[fi][:, 0:512],
                        dh[:, fo * TILE:(fo + 1) * TILE], OP.subtract), inc=(fi == 3))

        for wwhich, scw, dsth, dstl in (("kt", sc_wk, AA["k1t_h"], AA["k1t_l"]),
                                        ("vt", sc_wv, AA["v1t_h"], AA["v1t_l"])):
            bb.waitc("tensor", eS, eV)
            for kc in range(8):
                sh_i, sh_ap, sh_v, sh_s = ring_load([(scw[0][kc], D)])[0]
                sl_i, sl_ap, sl_v, sl_s = ring_load([(scw[1][kc], D)])[0]
                bb.wait("tensor", sh_s, sh_v)
                bb.wait("tensor", sl_s, sl_v)
                for ts in range(4):
                    for half in range(2):
                        bank = 2 * ts + half
                        for pi, (wap, xap) in enumerate((
                                (sh_ap, AA["h1f_h"]), (sh_ap, AA["h1f_l"]),
                                (sl_ap, AA["h1f_h"]))):
                            bb.op("tensor", lambda E, w=wap, x=xap, kc=kc, ts=ts, half=half,
                                  bank=bank, pi=pi:
                                  E.matmul(PS[bank][:, 0:512],
                                           x[:, kc * TILE + ts * 128: kc * TILE + (ts + 1) * 128],
                                           w[:, half * 512:(half + 1) * 512],
                                           start=(kc == 0 and pi == 0),
                                           stop=(kc == 7 and pi == 2)),
                                  inc=(ts == 3 and half == 1 and pi == 2))
                ring_consumed(sh_i)
                ring_consumed(sl_i)
            bb.wait("scalar", eT)
            for ts in range(4):
                for half in range(2):
                    bank = 2 * ts + half
                    bb.op("scalar", lambda E, d=dsth, ts=ts, half=half, bank=bank: E.copy(
                        d[:, ts * D + half * 512: ts * D + (half + 1) * 512],
                        PS[bank][:, 0:512]), inc=(ts == 3 and half == 1))
            bb.wait("vector", eS)
            for ts in range(4):
                for half in range(2):
                    bank = 2 * ts + half
                    bb.op("vector", lambda E, dh=dsth, dl=dstl, ts=ts, half=half, bank=bank:
                          E.tensor_tensor(
                              dl[:, ts * D + half * 512: ts * D + (half + 1) * 512],
                              PS[bank][:, 0:512],
                              dh[:, ts * D + half * 512: ts * D + (half + 1) * 512],
                              OP.subtract), inc=(ts == 3 and half == 1))

        rbase(eV, [("cmV", 0), ("cmV", 1), ("conV", 0), ("conV", 1), ("cx2V",),
                   ("amV", 0), ("amV", 1)]
              + [("csV", h) for h in range(NH1)] + [("cSl", h) for h in range(NH1)])
        rbase(eS, [("cnS", 0), ("cnS", 1)])
        rbase(eT, [("coT", 0), ("coT", 1), ("ctT", 0), ("ctT", 1)])

        def _cp(h):
            pu = h % 2
            return (pu, 6 + pu, 4 + pu, 31 if pu == 0 else 33,
                    AA["ssb"][:, pu * 140:pu * 140 + 128],
                    AA["pt_h"][:, pu * 128:(pu + 1) * 128],
                    AA["pt_l"][:, pu * 128:(pu + 1) * 128],
                    AA["et_h"][:, pu * 140:pu * 140 + 128],
                    AA["et_l"][:, pu * 140:pu * 140 + 128],
                    AA["sq"][:, pu * 128:(pu + 1) * 128])

        for c in range(4):
            cu = {}
            for i in range(10):
                if i < 8:
                    h = i
                    pu, ob_, tbk, stb2, AMF, amh_, aml_, onh_, onl_, sq_ = _cp(h)
                    rwait("tensor", eV, ("cmV", pu))
                    for pi, (ka, qa) in enumerate((
                            (AA["k_fm_h"], AA["q_fm_h"]), (AA["k_fm_h"], AA["q_fm_l"]),
                            (AA["k_fm_l"], AA["q_fm_h"]))):
                        v_ = bb.op("tensor", lambda E, ka=ka, qa=qa, h=h, c=c, pi=pi, pu=pu:
                              E.matmul(PS[pu][:, 0:128],
                                       ka[:, h * TILE + c * 128: h * TILE + (c + 1) * 128],
                                       qa[:, h * TILE + c * 128: h * TILE + (c + 1) * 128],
                                       start=(pi == 0), stop=(pi == 2)), inc=(pi == 2))
                    cu[h] = {}
                    bb.wait("vector", eT, v_)
                    R[("cmV", pu)] = bb.op(
                        "vector", lambda E, d=AMF, pu=pu: E.tensor_tensor(
                            d[:], PS[pu][:, 0:128], AA["c_tril"], OP.mult), inc=True)
                    bb.wait("vector", eV)
                    rwait("vector", eT, ("coT", pu))
                    bb.op("vector", lambda E, s=AMF, d=amh_: E.tensor_copy(d[:], s[:]),
                          inc=True)
                    bb.wait("vector", eV)
                    R[("amV", pu)] = cu[h]["am"] = bb.op(
                        "vector", lambda E, s=AMF, dh=amh_, dl=aml_: E.tensor_tensor(
                            dl[:], s[:], dh[:], OP.subtract), inc=True)
                if 1 <= i <= 8:
                    h = i - 1
                    pu, ob_, tbk, stb2, AMF, amh_, aml_, onh_, onl_, sq_ = _cp(h)
                    bb.wait("tensor", eV, cu[h]["am"])
                    rwait("tensor", eV, ("conV", pu))
                    rwait("tensor", eV, ("cSl", h))
                    for pi, (aa, va) in enumerate((
                            (amh_, AA["v1t_h"]), (amh_, AA["v1t_l"]),
                            (aml_, AA["v1t_h"]))):
                        bb.op("tensor", lambda E, aa=aa, va=va, h=h, c=c, pi=pi, ob=ob_:
                              E.matmul(PS[ob][:, 0:128], aa[:],
                                       va[:, c * D + h * 128: c * D + (h + 1) * 128],
                                       start=(pi == 0), stop=False))
                    if not (t == 0 and c == 0):
                        for pi, (qa, Sa) in enumerate((
                                (AA["q_fm_h"], AA["S_h"]), (AA["q_fm_h"], AA["S_l"]),
                                (AA["q_fm_l"], AA["S_h"]))):
                            v_ = bb.op("tensor", lambda E, qa=qa, Sa=Sa, h=h, c=c, pi=pi,
                                  ob=ob_:
                                  E.matmul(PS[ob][:, 0:128],
                                           qa[:, h * TILE + c * 128: h * TILE + (c + 1) * 128],
                                           Sa[:, h * 128:(h + 1) * 128],
                                           start=False, stop=(pi == 2)), inc=(pi == 2))
                    else:
                        v_ = bb.op("tensor", lambda E, h=h, c=c, ob=ob_, aa=aml_:
                              E.matmul(PS[ob][:, 0:128], aa[:],
                                       AA["v1t_l"][:, c * D + h * 128: c * D + (h + 1) * 128],
                                       start=False, stop=True), inc=True)
                    R[("coT", pu)] = cu[h]["o"] = v_
                    bb.wait("scalar", eT, cu[h]["o"])
                    bb.op("scalar", lambda E, d=sq_, ob=ob_, stb2=stb2: E.activation(
                        d[:], PS[ob][:, 0:128], AF.Square, accum_out=ST[:, stb2:stb2 + 1]),
                        inc=True)
                    bb.wait("scalar", eS)
                    rwait("scalar", eV, ("conV", pu))
                    bb.op("scalar", lambda E, stb2=stb2: E.activation(
                        ST[:, stb2 + 1:stb2 + 2], ST[:, stb2:stb2 + 1], AF.Ln,
                        bias=CEPS5, scale=1.0 / 128.0), inc=True)
                    bb.wait("scalar", eS)
                    cu[h]["exp"] = bb.op("scalar", lambda E, stb2=stb2: E.activation(
                        ST[:, stb2 + 1:stb2 + 2], ST[:, stb2 + 1:stb2 + 2], AF.Exp,
                        scale=-0.5), inc=True)
                    bb.wait("vector", eS, cu[h]["exp"])
                    rwait("vector", eT, ("ctT", pu))
                    bb.op("vector", lambda E, d=onh_, ob=ob_, stb2=stb2: E.tensor_scalar(
                        d[:], PS[ob][:, 0:128], ST[:, stb2 + 1:stb2 + 2], None, OP.mult),
                        inc=True)
                    bb.wait("vector", eV)
                    R[("conV", pu)] = cu[h]["onl"] = bb.op(
                        "vector", lambda E, dh=onh_, dl=onl_, ob=ob_, stb2=stb2:
                        E.scalar_tensor_tensor(
                            dl[:], PS[ob][:, 0:128], ST[:, stb2 + 1:stb2 + 2], dh[:],
                            OP.mult, OP.subtract), inc=True)
                if 2 <= i <= 9:
                    h = i - 2
                    pu, ob_, tbk, stb2, AMF, amh_, aml_, onh_, onl_, sq_ = _cp(h)
                    bb.wait("tensor", eV, cu[h]["onl"])
                    rwait("tensor", eS, ("cnS", pu))
                    bb.op("tensor", lambda E, s=onh_, tbk=tbk: E.transpose(
                        PSB[tbk][:, 0:128], s[:], AA["c_idb"]))
                    R[("ctT", pu)] = bb.op("tensor", lambda E, s=onl_, tbk=tbk: E.transpose(
                        PSB[tbk][:, 128:256], s[:], AA["c_idb"]), inc=True)
                    bb.wait("scalar", eT, R[("ctT", pu)])
                    bb.op("scalar", lambda E, h=h, tbk=tbk: E.copy(
                        AA["onT_h"][:, h * 128:(h + 1) * 128], PSB[tbk][:, 0:128]))
                    R[("cnS", pu)] = bb.op("scalar", lambda E, h=h, tbk=tbk: E.copy(
                        AA["onT_l"][:, h * 128:(h + 1) * 128], PSB[tbk][:, 128:256]),
                        inc=True)
                    sb_ = 2 + h // 4
                    so = (h % 4) * 128
                    rwait("tensor", eV, ("csV", h))
                    for pi, (ka, va) in enumerate((
                            (AA["k1t_h"], AA["v1t_h"]), (AA["k1t_h"], AA["v1t_l"]),
                            (AA["k1t_l"], AA["v1t_h"]))):
                        v_ = bb.op("tensor", lambda E, ka=ka, va=va, h=h, c=c, pi=pi,
                              sb_=sb_, so=so:
                              E.matmul(PS[sb_][:, so:so + 128],
                                       ka[:, c * D + h * 128: c * D + (h + 1) * 128],
                                       va[:, c * D + h * 128: c * D + (h + 1) * 128],
                                       start=(pi == 0), stop=(pi == 2)),
                              inc=(pi == 2))
                    bb.wait("vector", eT, v_)
                    R[("csV", h)] = bb.op("vector", lambda E, sb_=sb_, so=so, h=h:
                        E.tensor_tensor(
                            AA["S_sb"][:, h * 128:(h + 1) * 128],
                            AA["S_sb"][:, h * 128:(h + 1) * 128],
                            PS[sb_][:, so:so + 128], OP.add), inc=True)
                    bb.wait("vector", eV)
                    bb.op("vector", lambda E, h=h: E.tensor_copy(
                        AA["S_h"][:, h * 128:(h + 1) * 128],
                        AA["S_sb"][:, h * 128:(h + 1) * 128]), inc=True)
                    bb.wait("vector", eV)
                    R[("cSl", h)] = bb.op("vector", lambda E, h=h: E.tensor_tensor(
                        AA["S_l"][:, h * 128:(h + 1) * 128],
                        AA["S_sb"][:, h * 128:(h + 1) * 128],
                        AA["S_h"][:, h * 128:(h + 1) * 128], OP.subtract), inc=True)

            rwait("tensor", eS, ("cnS", 0), ("cnS", 1))
            rwait("tensor", eV, ("cx2V",))
            for kc in range(8):
                wh_i, wh_ap, wh_v, wh_s = ring_load([(sc_wo[0][kc], D)])[0]
                wl_i, wl_ap, wl_v, wl_s = ring_load([(sc_wo[1][kc], D)])[0]
                bb.wait("tensor", wh_s, wh_v)
                bb.wait("tensor", wl_s, wl_v)
                for half in range(2):
                    for pi, (oa, wa) in enumerate((
                            (AA["onT_h"], wh_ap), (AA["onT_h"], wl_ap), (AA["onT_l"], wh_ap))):
                        bb.op("tensor", lambda E, oa=oa, wa=wa, kc=kc, half=half, pi=pi:
                              E.matmul(PS[4 + half][:, 0:512],
                                       oa[:, kc * 128:(kc + 1) * 128],
                                       wa[:, half * 512:(half + 1) * 512],
                                       start=(kc == 0 and pi == 0),
                                       stop=(kc == 7 and pi == 2)),
                              inc=(half == 1 and pi == 2))
                ring_consumed(wh_i)
                ring_consumed(wl_i)
            bb.wait("vector", eT)
            for half in range(2):
                v_ = bb.op("vector", lambda E, c=c, half=half: E.tensor_tensor(
                    AA["x2"][:, c * D + half * 512: c * D + (half + 1) * 512],
                    PS[4 + half][:, 0:512],
                    AA["xn1"][:, c * D + half * 512: c * D + (half + 1) * 512], OP.add),
                    inc=(half == 1))
            R[("cx2V",)] = v_

        bb.wait("scalar", eV)
        for g in range(4):
            bb.op("scalar", lambda E, g=g: E.activation(
                AA["sq"][:, 0:D], AA["x2"][:, g * D:(g + 1) * D], AF.Square,
                accum_out=ST[:, 21 + g:22 + g]), inc=True)
            bb.wait("scalar", eS)
        bb.op("scalar", lambda E: E.activation(ST[:, 26:30], ST[:, 21:25], AF.Ln, bias=CEPS5,
                                               scale=1.0 / D), inc=True)
        bb.wait("scalar", eS)
        bb.op("scalar", lambda E: E.activation(ST[:, 26:30], ST[:, 26:30], AF.Exp, scale=-0.5),
              inc=True)
        for g in range(4):
            bb.wait("vector", eS)
            bb.wait("vector", eT)
            bb.op("vector", lambda E, g=g: E.tensor_scalar(
                AA["tmp"][:], AA["x2"][:, g * D:(g + 1) * D], ST[:, 26 + g:27 + g], None,
                OP.mult), inc=True)
            bb.wait("scalar", eV)
            bb.op("scalar", lambda E: E.activation(AA["tms_h"][:], AA["tmp"][:], AF.Copy),
                  inc=True)
            bb.wait("vector", eS)
            bb.op("vector", lambda E: E.tensor_tensor(
                AA["tms_l"][:], AA["tmp"][:], AA["tms_h"][:], OP.subtract), inc=True)
            bb.waitc("tensor", eV, eS)
            for dc in range(8):
                pv = PSB[dc % 2]
                for which, src_b in (("h", AA["tms_h"]), ("l", AA["tms_l"])):
                    dst = AA["h2f_h"] if which == "h" else AA["h2f_l"]
                    bb.op("tensor", lambda E, sb_=src_b, dc=dc, pv=pv: E.transpose(
                        pv[:, 0:128], sb_[:, dc * 128:(dc + 1) * 128], AA["c_idb"]), inc=True)
                    bb.wait("scalar", eT)
                    bb.op("scalar", lambda E, d=dst, dc=dc, g=g, pv=pv: E.copy(
                        d[:, dc * TILE + g * 128: dc * TILE + (g + 1) * 128], pv[:, 0:128]),
                        inc=True)
                    bb.wait("tensor", eS)

        for gp in range(2):
            bb.waitc("vector", eS, eT)
            wvs = []
            for gi in range(2):
                g = 2 * gp + gi
                c0 = 36 if gi == 0 else 44
                Wb = AA["qw"] if gi == 0 else AA["tmp"]
                bb.op("vector", lambda E, g=g, c0=c0: E.tensor_reduce(
                    ST[:, c0:c0 + 1], AA["xn1"][:, g * D:(g + 1) * D], AX.X, OP.max,
                    apply_absolute_value=True), inc=True)
                bb.wait("vector", eV)
                bb.op("vector", lambda E, c0=c0: E.tensor_scalar_add(
                    ST[:, c0:c0 + 1], ST[:, c0:c0 + 1], 1e-8), inc=True)
                bb.wait("vector", eV)
                bb.op("vector", lambda E, c0=c0: E.reciprocal(
                    ST[:, c0 + 1:c0 + 2], ST[:, c0:c0 + 1]))
                bb.op("vector", lambda E, g=g, c0=c0: E.tensor_tensor(
                    ST[:, 40 + g:41 + g], ST[:, c0:c0 + 1], AA["c_wm"][:, 2:3], OP.mult),
                    inc=True)
                bb.wait("vector", eV)
                bb.op("vector", lambda E, c0=c0: E.tensor_scalar_mul(
                    ST[:, c0 + 2:c0 + 3], ST[:, c0 + 1:c0 + 2], 127.0))
                bb.op("vector", lambda E, g=g: E.tensor_scalar_mul(
                    ST[:, 40 + g:41 + g], ST[:, 40 + g:41 + g], 1.0 / 127.0), inc=True)
                bb.wait("vector", eV)
                wvs.append(bb.op("vector", lambda E, g=g, c0=c0, W=Wb: E.tensor_scalar(
                    W[:], AA["xn1"][:, g * D:(g + 1) * D], ST[:, c0 + 2:c0 + 3], 128.5,
                    OP.mult, OP.add), inc=True))
            round2(D, [
                (AA["qw"], AA["qr"], AA["qg"],
                 AA["xn1q"][:, (2 * gp) * D:(2 * gp + 1) * D], wvs[0]),
                (AA["tmp"], AA["Bt"][:, 0:D], AA["Bt"][:, D:2 * D],
                 AA["xn1q"][:, (2 * gp + 1) * D:(2 * gp + 2) * D], wvs[1])])
            bb.waitc("tensor", eV, eS)
            for gi in range(2):
                g = 2 * gp + gi
                for dc in range(8):
                    pv = PSB[dc % 2]
                    bb.op("tensor", lambda E, dc=dc, g=g, pv=pv: E.transpose(
                        pv[:, 0:128], AA["xn1q"][:, g * D + dc * 128: g * D + (dc + 1) * 128],
                        AA["c_idb"]), inc=True)
                    bb.wait("scalar", eT)
                    bb.op("scalar", lambda E, dc=dc, g=g, pv=pv: E.copy(
                        AA["x1qf"][:, dc * TILE + g * 128: dc * TILE + (g + 1) * 128],
                        pv[:, 0:128]), inc=True)
                    bb.wait("tensor", eS)

        GUF = AA["ssb"]
        R[("obD", 0)] = dOp[0].n
        R[("obD", 1)] = dOp[1].n
        nout = 0
        for half in range(2):
            rbase(eV, [("muV", 0), ("muV", 1), ("mgV",)])
            rbase(eS, [("muS", 0), ("muS", 1)])
            rbase(eT, [("mdT",)])
            bb.waitc("tensor", eV, eS)
            mu = {}
            for it2 in range(33):
                if it2 < 32:
                    ic = it2
                    gpar = ic % 2
                    gb0 = 2 * gpar
                    gb1 = 2 * gpar + 1
                    gh_i, gh_ap, gh_v, gh_s = ring_load([(sc_wg[0][ic], D)])[0]
                    gl_i, gl_ap, gl_v, gl_s = ring_load([(sc_wg[1][ic], D)])[0]
                    uh_i, uh_ap, uh_v, uh_s = ring_load([(sc_wu[0][ic], D)])[0]
                    ul_i, ul_ap, ul_v, ul_s = ring_load([(sc_wu[1][ic], D)])[0]
                    rwait("tensor", eV, ("muV", gpar))
                    rwait("tensor", eS, ("muS", gpar))
                    bb.wait("tensor", gh_s, gh_v)
                    bb.wait("tensor", gl_s, gl_v)
                    bb.wait("tensor", uh_s, uh_v)
                    bb.wait("tensor", ul_s, ul_v)
                    for bank, wh, wl in ((gb0, gh_ap, gl_ap), (gb1, uh_ap, ul_ap)):
                        for pi, (wap, xap) in enumerate((
                                (wh, AA["h2f_h"]), (wh, AA["h2f_l"]), (wl, AA["h2f_h"]))):
                            for kc in range(8):
                                v_ = bb.op("tensor", lambda E, w=wap, x=xap, kc=kc,
                                      bank=bank, pi=pi, half=half:
                                      E.matmul(PS[bank][:, 0:256],
                                               w[:, kc * 128:(kc + 1) * 128],
                                               x[:, kc * TILE + half * 256: kc * TILE + half * 256 + 256],
                                               start=(kc == 0 and pi == 0),
                                               stop=(kc == 7 and pi == 2)),
                                      inc=(bank == gb1 and kc == 7 and pi == 2))
                    for si in (gh_i, gl_i, uh_i, ul_i):
                        ring_consumed(si)
                    mu[ic] = {"mm": v_}
                if it2 >= 1:
                    ic = it2 - 1
                    gpar = ic % 2
                    gb0 = 2 * gpar
                    gb1 = 2 * gpar + 1
                    bb.wait("scalar", eT, mu[ic]["mm"])
                    rwait("scalar", eV, ("mgV",))
                    R[("muS", gpar)] = mu[ic]["silu"] = bb.op(
                        "scalar", lambda E, gb0=gb0: E.activation(
                            AA["sg"][:, 0:256], PS[gb0][:, 0:256], AF.Silu), inc=True)
                    bb.wait("vector", eS, mu[ic]["silu"])
                    rwait("vector", eV, ("mguV",))
                    R[("muV", gpar)] = R[("mgV",)] = bb.op(
                        "vector", lambda E, gb1=gb1: E.tensor_tensor(
                            GUF[:, 0:256], AA["sg"][:, 0:256], PS[gb1][:, 0:256], OP.mult),
                        inc=True)
                    bb.wait("vector", eV)
                    rwait("vector", eT, ("mdT",))
                    bb.op("vector", lambda E: E.tensor_copy(
                        AA["gu_h"][:, 0:256], GUF[:, 0:256]), inc=True)
                    bb.wait("vector", eV)
                    guv = bb.op("vector", lambda E: E.tensor_tensor(
                        AA["gu_l"][:, 0:256], GUF[:, 0:256], AA["gu_h"][:, 0:256],
                        OP.subtract), inc=True)
                    R[("mguV",)] = guv
                    dh_i, dh_ap, dh_v, dh_s = ring_load([(sc_wd[0][ic], D)])[0]
                    dl_i, dl_ap, dl_v, dl_s = ring_load([(sc_wd[1][ic], D)])[0]
                    bb.wait("tensor", dh_s, dh_v)
                    bb.wait("tensor", dl_s, dl_v)
                    if ic == 0:
                        bb.wait("tensor", eV)
                    bb.wait("tensor", eV, guv)
                    for ts2 in range(2):
                        for dhf in range(2):
                            bank = 4 + 2 * ts2 + dhf
                            for pi, (ga, wa) in enumerate((
                                    (AA["gu_h"], dh_ap), (AA["gu_h"], dl_ap),
                                    (AA["gu_l"], dh_ap))):
                                v_ = bb.op("tensor", lambda E, ga=ga, wa=wa, ic=ic, ts2=ts2,
                                      dhf=dhf, bank=bank, pi=pi:
                                      E.matmul(PS[bank][:, 0:512],
                                               ga[:, ts2 * 128:(ts2 + 1) * 128],
                                               wa[:, dhf * 512:(dhf + 1) * 512],
                                               start=(ic == 0 and pi == 0),
                                               stop=(ic == 31 and pi == 2)),
                                      inc=(ts2 == 1 and dhf == 1 and pi == 2))
                    R[("mdT",)] = v_
                    ring_consumed(dh_i)
                    ring_consumed(dl_i)
            for ts2 in range(2):
                ts = half * 2 + ts2
                bb.waitc("tensor", eV)
                for kc in range(8):
                    fh_i, fh_ap, fh_v, fh_s = ring_load([(sc_l1f[kc], D)])[0]
                    bb.wait("tensor", fh_s, fh_v)
                    for dhf in range(2):
                        bb.op("tensor", lambda E, w=fh_ap, kc=kc, ts=ts, dhf=dhf:
                              E.matmul(PS[2 + dhf][:, 0:512],
                                       AA["x1qf"][:, kc * TILE + ts * 128: kc * TILE + (ts + 1) * 128],
                                       w[:, dhf * 512:(dhf + 1) * 512],
                                       start=(kc == 0), stop=(kc == 7)),
                              inc=(dhf == 1))
                    ring_consumed(fh_i)
                bb.wait("vector", eT)
                for dhf in range(2):
                    bank_d = 4 + 2 * ts2 + dhf
                    opar = nout % 2
                    nout += 1
                    ob2_ = AA["tmp"][:, opar * 512:(opar + 1) * 512]
                    rwait("vector", dOp[opar], ("obD", opar))
                    bb.op("vector", lambda E, ts=ts, dhf=dhf, d=ob2_: E.scalar_tensor_tensor(
                        d[:], PS[2 + dhf][:, 0:512], ST[:, 40 + ts:41 + ts],
                        AA["x1"][:, ts * D + dhf * 512: ts * D + (dhf + 1) * 512],
                        OP.mult, OP.add), inc=True)
                    bb.wait("vector", eV)
                    bb.op("vector", lambda E, ts=ts, dhf=dhf, d=ob2_: E.tensor_tensor(
                        d[:], d[:],
                        AA["x2"][:, ts * D + dhf * 512: ts * D + (dhf + 1) * 512], OP.add),
                        inc=True)
                    bb.wait("vector", eV)
                    bb.op("vector", lambda E, bd=bank_d, d=ob2_: E.tensor_tensor(
                        d[:], d[:], PS[bd][:, 0:512], OP.add), inc=True)
                    bb.wait("sync", eV)
                    R[("obD", opar)] = bb.op(
                        "sync", lambda E, ts=ts, dhf=dhf, T0=T0, d=ob2_: E.dma_start(
                            out=y_out[seq][T0 + ts * 128: T0 + (ts + 1) * 128,
                                           dhf * 512:(dhf + 1) * 512],
                            in_=d[:]), sem=dOp[opar], inck=16)

    if stop_after == "prep":
        for seq in range(n_seq):
            for t in range(NT):
                bb.waitc("sync", eV, eS, eG, eT)
                for g in range(4):
                    bb.op("sync", lambda E, g=g, seq=seq, t=t: E.dma_start(
                        out=y_out[seq][t * TILE + g * 128: t * TILE + (g + 1) * 128, :],
                        in_=x_in[seq][t * TILE + g * 128: t * TILE + (g + 1) * 128, :]),
                        sem=dO, inck=16)
    else:
        for seq in range(n_seq):
            for t in range(NT):
                do_tile(seq, t)

    bb.wait("sync", dO)
    bb.wait("sync", dOp[0])
    bb.wait("sync", dOp[1])
    bb.op("sync", lambda E: E.nop(nofuse=True), inc=True)

    with nc.Block() as block:
        bb.finish(block)
    return nc, dbg



def host_inputs(inp):
    f32 = np.float32
    out = {}
    out["w_qkvT"] = np.ascontiguousarray(inp["L0_attn_qkv_w"].T)
    out["w_projT"] = np.ascontiguousarray(inp["L0_attn_proj_w"].T)
    out["w_w1T"] = np.ascontiguousarray(inp["L0_ffn_w1"].T)
    out["w_w2T"] = np.ascontiguousarray(inp["L0_ffn_w2"].T)
    out["w_l1fT"] = np.ascontiguousarray(inp["L1_ffn_weight"].T)
    out["w_q1"] = np.ascontiguousarray(inp["L1_attn_Wq"])
    out["w_k1"] = np.ascontiguousarray(inp["L1_attn_Wk"])
    out["w_v1"] = np.ascontiguousarray(inp["L1_attn_Wv"])
    out["w_o1"] = np.ascontiguousarray(inp["L1_attn_Wo"])
    out["w_g1"] = np.ascontiguousarray(inp["L1_attn_Wgate"])
    out["w_u1"] = np.ascontiguousarray(inp["L1_attn_Wup"])
    out["w_d1"] = np.ascontiguousarray(inp["L1_attn_Wdown"])
    out["qkv_b_qk"] = np.ascontiguousarray(inp["L0_attn_qkv_b"][:2 * D].reshape(16, 128).T)
    out["qkv_b_vcol"] = np.ascontiguousarray(inp["L0_attn_qkv_b"][2 * D:].reshape(8, 128).T)
    out["proj_b_row"] = np.ascontiguousarray(inp["L0_attn_proj_b"].reshape(1, D))
    out["conv_w3"] = np.ascontiguousarray(
        inp["L0_ffn_conv_w"].reshape(32, 128, 3).transpose(2, 1, 0))
    out["conv_b_c"] = np.ascontiguousarray(inp["L0_ffn_conv_b"].reshape(32, 128).T)
    out["anorm_c"] = np.ascontiguousarray(inp["L1_attn_attn_norm_w"].reshape(8, 128).T)
    out["mnorm_c"] = np.ascontiguousarray(inp["L1_attn_mlp_norm_w"].reshape(8, 128).T)
    out["onorm_c"] = np.ascontiguousarray(inp["L1_attn_attn_out_norm_w"].reshape(128, 1))

    rpb = inp["L0_attn_rpb"]
    ii = np.arange(128)[:, None]
    jl = np.arange(JW)[None, :]
    out["brpb"] = np.ascontiguousarray(rpb[:, np.clip(6 + jl - ii, 0, 24)].astype(f32))
    r = jl - 6 - ii
    m_int = np.where((r >= -6) & (r <= 6), 0.0, -1e30).astype(f32)
    kloc = jl - 6
    ni_f = np.maximum(ii - 6, 0)
    m_first = np.where((kloc >= ni_f) & (kloc <= ni_f + 12), 0.0, -1e30).astype(f32)
    ni_l = np.minimum(ii - 6, 128 - 13)
    m_last = np.where((kloc >= ni_l) & (kloc <= ni_l + 12), 0.0, -1e30).astype(f32)
    out["bmask3"] = np.ascontiguousarray(np.stack([m_int, m_first, m_last]))
    jj = np.arange(128)
    out["tril_T"] = np.ascontiguousarray((jj[:, None] <= jj[None, :]).astype(f32))
    out["ident_f"] = np.eye(128, dtype=f32)
    import ml_dtypes
    out["ident_b"] = np.eye(128, dtype=ml_dtypes.bfloat16)
    out["ones_row"] = np.ones((1, 128), f32)
    out["ones_col"] = np.ones((128, 1), f32)
    cst = np.zeros((128, 8), f32)
    cst[:, 0] = -128.0
    cst[:, 1] = -0.5
    cst[:, 2] = 0.5
    cst[:, 3] = 1e-5
    cst[:, 4] = 1e-8
    cst[:, 6] = 1.0
    cst[:, 7] = 0.5
    out["consts_in"] = cst
    out["spc_in"] = np.zeros((128, 64), f32)
    return out


def kernel(**inputs):
    n_cores = 8
    Bf, Tf = inputs["L0_attn_x"].shape[0], inputs["L0_attn_x"].shape[1]
    per = Bf // n_cores
    nc, _ = build(n_seq=per, seq_len=Tf)
    common = host_inputs(inputs)
    x = np.ascontiguousarray(inputs["L0_attn_x"])
    in_maps = []
    for c in range(n_cores):
        m = dict(common)
        m["x_in"] = np.ascontiguousarray(x[c * per:(c + 1) * per])
        in_maps.append(m)
    res = run_bass_kernel_spmd(nc, in_maps, core_ids=list(range(n_cores)))
    return np.concatenate([r["y_out"] for r in res.results], axis=0)


# version 10
# speedup vs baseline: 1.0103x; 1.0103x over previous
import numpy as np
import concourse.bass as bass
import concourse.mybir as mybir
from concourse.bass_utils import run_bass_kernel_spmd

F32 = mybir.dt.float32
BF16 = mybir.dt.bfloat16
OP = mybir.AluOpType
AF = mybir.ActivationFunctionType
AX = mybir.AxisListType

D = 1024
NH0 = 16
HID = 4096
NH1 = 8
INTER = 4096

TILE = 512
NSUB = 4
KW = TILE + 12
XNW = TILE + 16
HW_ = TILE + 4
JW = 140



class Sem:
    def __init__(self, nc, name):
        self.h = nc.alloc_semaphore(name)
        self.n = 0


class B:
    ENGS = ("sync", "tensor", "vector", "scalar", "gpsimd")

    def __init__(self, nc):
        self.nc = nc
        self.q = {e: [] for e in self.ENGS}
        self.e = {e: Sem(nc, "e_" + e) for e in self.ENGS}
        self.dma = {}
        self._last_inc = {e: None for e in self.ENGS}
        self.spacer = None
        self._spc_rot = {e: 0 for e in self.ENGS}

    def dsem(self, name):
        if name not in self.dma:
            self.dma[name] = Sem(self.nc, "d_" + name)
        return self.dma[name]

    def op(self, eng, fn, inc=False, sem=None, inck=1):
        if inc or sem is not None:
            s = self.e[eng] if sem is None else sem
            if self._last_inc[eng] is s and self.spacer is not None:
                self.q[eng].append(self.spacer(eng))
            s.n += inck
            h, k = s.h, inck
            self.q[eng].append(lambda E: fn(E).then_inc(h, k))
            self._last_inc[eng] = s
            return s.n
        self.q[eng].append(lambda E: fn(E))
        self._last_inc[eng] = None
        return None

    def wait(self, eng, sem, val=None):
        v = sem.n if val is None else val
        if v <= 0:
            return
        h = sem.h
        self.q[eng].append(lambda E: E.wait_ge(h, v))

    def waitc(self, eng, *sems):
        for s in sems:
            self.wait(eng, s)

    def barrier(self):
        for e in self.ENGS:
            if self.spacer is not None and e in ("vector", "scalar", "gpsimd"):
                self.op(e, self.spacer(e), inc=True)
            else:
                self.op(e, lambda E: E.nop(nofuse=True), inc=True)
        for e in self.ENGS:
            for o in self.ENGS:
                if o != e:
                    self.wait(e, self.e[o])

    def finish(self, block):
        for e in self.ENGS:
            thunks = self.q[e]

            def body(E, thunks=thunks):
                for t in thunks:
                    t(E)
            getattr(block, e)(body)


def build(n_seq=2, seq_len=4096, taps=(), stop_after=None):
    NT = seq_len // TILE
    nc = bass.Bass("TRN2", target_bir_lowering=False, debug=False)

    x_in = nc.dram_tensor("x_in", [n_seq, seq_len, D], F32, kind="ExternalInput").ap()
    y_out = nc.dram_tensor("y_out", [n_seq, seq_len, D], F32, kind="ExternalOutput").ap()

    w_qkvT = nc.dram_tensor("w_qkvT", [D, 3 * D], F32, kind="ExternalInput").ap()
    w_projT = nc.dram_tensor("w_projT", [D, D], F32, kind="ExternalInput").ap()
    w_w1T = nc.dram_tensor("w_w1T", [D, HID], F32, kind="ExternalInput").ap()
    w_w2T = nc.dram_tensor("w_w2T", [HID, D], F32, kind="ExternalInput").ap()
    w_l1fT = nc.dram_tensor("w_l1fT", [D, D], F32, kind="ExternalInput").ap()
    w_q1 = nc.dram_tensor("w_q1", [D, D], F32, kind="ExternalInput").ap()
    w_k1 = nc.dram_tensor("w_k1", [D, D], F32, kind="ExternalInput").ap()
    w_v1 = nc.dram_tensor("w_v1", [D, D], F32, kind="ExternalInput").ap()
    w_o1 = nc.dram_tensor("w_o1", [D, D], F32, kind="ExternalInput").ap()
    w_g1 = nc.dram_tensor("w_g1", [D, INTER], F32, kind="ExternalInput").ap()
    w_u1 = nc.dram_tensor("w_u1", [D, INTER], F32, kind="ExternalInput").ap()
    w_d1 = nc.dram_tensor("w_d1", [INTER, D], F32, kind="ExternalInput").ap()

    qkv_b_qk = nc.dram_tensor("qkv_b_qk", [128, 16], F32, kind="ExternalInput").ap()
    qkv_b_vcol = nc.dram_tensor("qkv_b_vcol", [128, 8], F32, kind="ExternalInput").ap()
    proj_b_row = nc.dram_tensor("proj_b_row", [1, D], F32, kind="ExternalInput").ap()
    conv_w3 = nc.dram_tensor("conv_w3", [3, 128, 32], F32, kind="ExternalInput").ap()
    conv_b_c = nc.dram_tensor("conv_b_c", [128, 32], F32, kind="ExternalInput").ap()
    anorm_c = nc.dram_tensor("anorm_c", [128, 8], F32, kind="ExternalInput").ap()
    mnorm_c = nc.dram_tensor("mnorm_c", [128, 8], F32, kind="ExternalInput").ap()
    onorm_c = nc.dram_tensor("onorm_c", [128, 1], F32, kind="ExternalInput").ap()
    brpb = nc.dram_tensor("brpb", [NH0, 128, JW], F32, kind="ExternalInput").ap()
    bmask3 = nc.dram_tensor("bmask3", [3, 128, JW], F32, kind="ExternalInput").ap()
    tril_T = nc.dram_tensor("tril_T", [128, 128], F32, kind="ExternalInput").ap()
    ident_f = nc.dram_tensor("ident_f", [128, 128], F32, kind="ExternalInput").ap()
    ident_b = nc.dram_tensor("ident_b", [128, 128], BF16, kind="ExternalInput").ap()
    ones_row = nc.dram_tensor("ones_row", [1, 128], F32, kind="ExternalInput").ap()
    ones_col = nc.dram_tensor("ones_col", [128, 1], F32, kind="ExternalInput").ap()
    consts_in = nc.dram_tensor("consts_in", [128, 8], F32, kind="ExternalInput").ap()
    spc_in = nc.dram_tensor("spc_in", [128, 64], F32, kind="ExternalInput").ap()

    sc_qkvT = [nc.dram_tensor(f"s_qkvT{h}", [8, 128, 3 * D], BF16).ap() for h in range(2)]
    sc_projT = [nc.dram_tensor(f"s_projT{h}", [8, 128, D], BF16).ap() for h in range(2)]
    sc_w1q = nc.dram_tensor("s_w1q", [32, 128, D], BF16).ap()
    sc_w2q = nc.dram_tensor("s_w2q", [32, 128, D], BF16).ap()
    sc_l1f = nc.dram_tensor("s_l1f", [8, 128, D], BF16).ap()
    sc_wq = [nc.dram_tensor(f"s_wq{h}", [8, 128, D], BF16).ap() for h in range(2)]
    sc_wk = [nc.dram_tensor(f"s_wk{h}", [8, 128, D], BF16).ap() for h in range(2)]
    sc_wv = [nc.dram_tensor(f"s_wv{h}", [8, 128, D], BF16).ap() for h in range(2)]
    sc_wo = [nc.dram_tensor(f"s_wo{h}", [8, 128, D], BF16).ap() for h in range(2)]
    sc_wg = [nc.dram_tensor(f"s_wg{h}", [32, 128, D], BF16).ap() for h in range(2)]
    sc_wu = [nc.dram_tensor(f"s_wu{h}", [32, 128, D], BF16).ap() for h in range(2)]
    sc_wd = [nc.dram_tensor(f"s_wd{h}", [32, 128, D], BF16).ap() for h in range(2)]
    sc_B = nc.dram_tensor("s_B", [3, 128, NH0 * JW], F32).ap()
    sc_hact = nc.dram_tensor("s_hact", [32, 128, HW_], F32).ap()
    sc_qh = nc.dram_tensor("s_qh", [32, 128, TILE], BF16).ap()

    AA = {}

    def sb(name, shape, dt=F32):
        AA[name] = nc.alloc_sbuf_tensor(name, list(shape), dt).ap()
        return AA[name]

    C = sb("c_c", [128, 8])
    sb("c_convw", [128, 96])
    sb("c_convb", [128, 32])
    sb("c_qkvb", [128, 16])
    sb("c_anorm", [128, 8])
    sb("c_mnorm", [128, 8])
    sb("c_onorm", [128, 1])
    sb("c_pbrow", [128, D])
    sb("c_tril", [128, 128])
    sb("c_idf", [128, 128])
    sb("c_idb", [128, 128], BF16)
    sb("c_or", [1, 128])
    sb("c_oc", [128, 1])
    sb("c_wm", [128, 8])
    ST = sb("c_stat", [128, 64])
    sb("S_sb", [128, NH1 * 128])
    sb("S_h", [128, NH1 * 128], BF16)
    sb("S_l", [128, NH1 * 128], BF16)
    NRING = 6
    for i in range(NRING):
        sb(f"ring{i}", [128, 1024], BF16)
    sb("v1t_l", [128, 4 * D], BF16)
    sb("Bt", [128, NH0 * JW])

    sb("xg", [128, 4 * D])
    sb("xh", [128, D])
    sb("tmp", [128, D])
    sb("xnf_h", [128, 8 * XNW], BF16)
    sb("xnf_l", [128, 8 * XNW], BF16)
    sb("xnq", [128, 4 * D], BF16)
    sb("xnqh", [128, D], BF16)
    sb("xnq_fm", [128, 8 * XNW], BF16)
    sb("q_fm_h", [128, 8 * TILE], BF16)
    sb("q_fm_l", [128, 8 * TILE], BF16)
    sb("k_fm_h", [128, 8 * KW], BF16)
    sb("k_fm_l", [128, 8 * KW], BF16)
    sb("vt_h", [128, 1024], BF16)
    sb("vt_l", [128, 1024], BF16)
    sb("vtm_h", [128, 1024], BF16)
    sb("vtm_l", [128, 1024], BF16)
    sb("ssb", [128, 4 * JW])
    sb("et_h", [128, 4 * JW], BF16)
    sb("et_l", [128, 4 * JW], BF16)
    sb("pt_h", [128, 256], BF16)
    sb("pt_l", [128, 256], BF16)
    sb("ptm_h", [128, 256], BF16)
    sb("ptm_l", [128, 256], BF16)
    sb("of_h", [128, 8 * 128], BF16)
    sb("of_l", [128, 8 * 128], BF16)
    sb("x1", [128, 4 * D])
    sb("qw", [128, D])
    sb("qr", [128, D])
    sb("qg", [128, D])
    AA["sq"] = AA["qg"]
    sb("ha", [128, 2 * HW_])
    sb("qhb", [128, 2 * TILE], BF16)
    sb("dqrow", [128, HW_])
    AA["srow"] = AA["dqrow"][:, 0:TILE]
    AA["hacc"] = AA["qhb"].bitcast(F32)[:, 0:TILE]
    AA["hacc2"] = AA["of_h"].bitcast(F32)[:, 0:TILE]
    AA["tms_h"] = AA["qhb"][:, 0:D]
    AA["tms_l"] = AA["dqrow"].bitcast(BF16)[:, 0:D]
    sb("xn1", [128, 4 * D])
    AA["ob2"] = AA["tmp"][:, 0:512]
    AA["h1f_h"] = AA["xnf_h"][:, 0:8 * TILE]
    AA["h1f_l"] = AA["xnf_l"][:, 0:8 * TILE]
    AA["k1t_h"] = AA["xnq"][:, 0:4 * D]
    AA["k1t_l"] = AA["xnq_fm"][:, 0:4 * D]
    AA["v1t_h"] = AA["Bt"].bitcast(BF16)[:, 0:4 * D]

    AA["onT_h"] = AA["of_h"][:, 0:NH1 * 128]
    AA["onT_l"] = AA["of_l"][:, 0:NH1 * 128]
    AA["am_h"] = AA["pt_h"]
    AA["am_l"] = AA["pt_l"]
    AA["h2f_h"] = AA["xnf_h"][:, 0:8 * TILE]
    AA["h2f_l"] = AA["xnf_l"][:, 0:8 * TILE]
    AA["xn1q"] = AA["xnq"][:, 0:4 * D]
    AA["x1qf"] = AA["xnq_fm"][:, 0:8 * TILE]
    AA["sg"] = AA["vt_h"].bitcast(F32)[:, 0:256]
    AA["x2"] = AA["xg"][:, 0:4 * D]
    AA["on_h"] = AA["et_h"][:, 0:128]
    AA["on_l"] = AA["et_l"][:, 0:128]
    AA["gu_h"] = AA["vtm_h"][:, 0:256]
    AA["gu_l"] = AA["vtm_l"][:, 0:256]

    sb("c_spc", [128, 64])

    PS = [nc.alloc_psum_tensor(f"ps{i}", [128, 512], F32).ap() for i in range(8)]
    PSB = [p.bitcast(BF16) for p in PS]

    bb = B(nc)

    def _mk_spacer(eng):
        base = {"vector": 0, "scalar": 21, "gpsimd": 42, "tensor": 0, "sync": 0}[eng]
        if eng in ("tensor", "sync"):
            return lambda E: E.nop(nofuse=True)
        rot = bb._spc_rot[eng] % 20
        bb._spc_rot[eng] += 1
        dst = AA["c_spc"][:, base + rot:base + rot + 1]
        src = AA["c_spc"][:, base + 20:base + 21]
        if eng == "scalar":
            return lambda E: E.copy(dst, src)
        return lambda E: E.tensor_copy(dst, src)
    bb.spacer = _mk_spacer
    dW = bb.dsem("w")
    dX = bb.dsem("x")
    dO = bb.dsem("o")
    dOp = [bb.dsem("o0"), bb.dsem("o1")]
    dP = bb.dsem("p")
    dH = bb.dsem("h")
    dHa = [bb.dsem("ha0"), bb.dsem("ha1")]
    dHb = [bb.dsem("hb0"), bb.dsem("hb1")]
    dHl = bb.dsem("hl")
    dHp = [bb.dsem("hl0"), bb.dsem("hl1")]
    dBt = bb.dsem("b")
    eT, eV, eS, eG, eY = (bb.e[k] for k in ("tensor", "vector", "scalar", "gpsimd", "sync"))

    CM128 = C[:, 0:1]
    CMH = C[:, 1:2]
    CPH = C[:, 2:3]
    CEPS5 = C[:, 3:4]
    CEPS8 = C[:, 4:5]
    for val, ap in ((-128.0, CM128), (-0.5, CMH), (0.5, CPH), (1e-5, CEPS5),
                    (1e-8, CEPS8), (0.0, C[:, 5:6]), (1.0, C[:, 6:7])):
        nc.const_aps.register(F32, val, ap)

    dbg = {}

    def tap(name, src_ap, shape):
        if name not in taps:
            return
        d = nc.dram_tensor("dbg_" + name, list(shape), F32, kind="ExternalOutput").ap()
        dbg[name] = d
        for e in ("vector", "scalar", "tensor", "gpsimd"):
            bb.wait("sync", bb.e[e])
        sv = bb.op("sync", lambda E: E.dma_start(out=d[:], in_=src_ap),
                   sem=bb.dsem("dbg"), inck=16)
        for e in B.ENGS:
            bb.wait(e, bb.dsem("dbg"), sv)

    def round_from_w(Fc, dst_ap, qw_, qr_, qg_, stt="vector"):
        se = eV if stt == "vector" else eG
        bb.wait(stt, se)
        bb.wait("scalar", eS)
        bb.op(stt, lambda E: E.tensor_copy(qr_[:, 0:Fc], qw_[:, 0:Fc]), inc=True, sem=se)
        for Bv in (128.0, 64.0, 32.0, 16.0, 8.0, 4.0, 2.0, 1.0):
            bb.wait("scalar", se)
            bb.op("scalar", lambda E: E.activation(qg_[:, 0:Fc], qr_[:, 0:Fc], AF.Sign,
                                                   bias=CM128), inc=True)
            bb.wait(stt, eS)
            bb.op(stt, lambda E, c=-Bv / 2.0: E.scalar_tensor_tensor(
                qr_[:, 0:Fc], qg_[:, 0:Fc], c, qr_[:, 0:Fc], OP.mult, OP.add),
                inc=True, sem=se)
        bb.wait(stt, se)
        bb.op(stt, lambda E: E.scalar_tensor_tensor(
            qg_[:, 0:Fc], qr_[:, 0:Fc], -1.0, qw_[:, 0:Fc], OP.mult, OP.add),
            inc=True, sem=se)
        bb.wait("scalar", se)
        bb.op("scalar", lambda E: E.activation(dst_ap, qg_[:, 0:Fc], AF.Copy, bias=-0.5),
              inc=True)

    def round2(Fc, chains):
        rv = {}
        sv = {}
        for k, (Wb, Rb, Gb, dst, wv) in enumerate(chains):
            bb.wait("vector", eV, wv)
            bb.wait("vector", eS)
            rv[k] = bb.op("vector", lambda E, W=Wb, Rb=Rb, Fc=Fc: E.tensor_copy(
                Rb[:, 0:Fc], W[:, 0:Fc]), inc=True)
        for Bv in (128.0, 64.0, 32.0, 16.0, 8.0, 4.0, 2.0, 1.0):
            for k, (Wb, Rb, Gb, dst, wv) in enumerate(chains):
                bb.wait("scalar", eV, rv[k])
                sv[k] = bb.op("scalar", lambda E, Rb=Rb, Gb=Gb, Fc=Fc: E.activation(
                    Gb[:, 0:Fc], Rb[:, 0:Fc], AF.Sign, bias=CM128), inc=True)
            for k, (Wb, Rb, Gb, dst, wv) in enumerate(chains):
                bb.wait("vector", eS, sv[k])
                rv[k] = bb.op("vector", lambda E, Rb=Rb, Gb=Gb, Fc=Fc, c=-Bv / 2.0:
                              E.scalar_tensor_tensor(
                                  Rb[:, 0:Fc], Gb[:, 0:Fc], c, Rb[:, 0:Fc],
                                  OP.mult, OP.add), inc=True)
        for k, (Wb, Rb, Gb, dst, wv) in enumerate(chains):
            bb.wait("vector", eV, rv[k])
            rv[k] = bb.op("vector", lambda E, W=Wb, Rb=Rb, Gb=Gb, Fc=Fc:
                          E.scalar_tensor_tensor(
                              Gb[:, 0:Fc], Rb[:, 0:Fc], -1.0, W[:, 0:Fc],
                              OP.mult, OP.add), inc=True)
        out = []
        for k, (Wb, Rb, Gb, dst, wv) in enumerate(chains):
            bb.wait("scalar", eV, rv[k])
            out.append(bb.op("scalar", lambda E, Gb=Gb, dst=dst, Fc=Fc: E.activation(
                dst, Gb[:, 0:Fc], AF.Copy, bias=-0.5), inc=True))
        return out

    def prep():
        small = [
            (AA["c_spc"][:], spc_in[:]),
            (C[:], consts_in[:]),
            (AA["c_convw"][:, 0:32], conv_w3[0]),
            (AA["c_convw"][:, 32:64], conv_w3[1]),
            (AA["c_convw"][:, 64:96], conv_w3[2]),
            (AA["c_convb"][:], conv_b_c[:]),
            (AA["c_qkvb"][:], qkv_b_qk[:]),
            (AA["c_anorm"][:], anorm_c[:]),
            (AA["c_mnorm"][:], mnorm_c[:]),
            (AA["c_onorm"][:], onorm_c[:]),
            (AA["c_tril"][:], tril_T[:]),
            (AA["c_idf"][:], ident_f[:]),
            (AA["c_idb"][:], ident_b[:]),
            (AA["c_or"][:], ones_row[:]),
            (AA["c_oc"][:], ones_col[:]),
        ]
        for dst, src in small:
            bb.op("sync", lambda E, d=dst, s=src: E.dma_start(out=d, in_=s), sem=dP, inck=16)
        for e in ("vector", "scalar", "tensor", "gpsimd"):
            bb.wait(e, dP)
        bb.op("vector", lambda E: E.tensor_scalar_mul(
            AA["c_qkvb"][:, 0:8], AA["c_qkvb"][:, 0:8], 0.125), inc=True)

        for v in range(3):
            bb.waitc("sync", eV)
            bb.wait("sync", dP)
            bb.op("sync", lambda E, v=v: E.dma_start(out=AA["tmp"][:, 0:JW], in_=bmask3[v]),
                  sem=dP, inck=16)
            for h in range(NH0):
                bb.op("sync", lambda E, h=h: E.dma_start(
                    out=AA["Bt"][:, h * JW:(h + 1) * JW], in_=brpb[h]), sem=dP, inck=16)
            bb.wait("vector", dP)
            for h in range(NH0):
                bb.op("vector", lambda E, h=h: E.tensor_tensor(
                    AA["Bt"][:, h * JW:(h + 1) * JW], AA["Bt"][:, h * JW:(h + 1) * JW],
                    AA["tmp"][:, 0:JW], OP.add), inc=(h == NH0 - 1))
            bb.wait("sync", eV)
            bb.op("sync", lambda E, v=v: E.dma_start(out=sc_B[v], in_=AA["Bt"][:]),
                  sem=dP, inck=16)

        def mean_pass(src, nk, F, acc_col):
            for k in range(nk):
                buf = (AA["xg"] if k % 2 == 0 else AA["xn1"])[:, 0:F]
                bb.wait("sync", eV)
                bb.op("sync", lambda E, b=buf, s=src, k=k: E.dma_start(
                    out=b, in_=s[k * 128:(k + 1) * 128, :]), sem=dP, inck=16)
                bb.wait("vector", dP)
                col = acc_col if k == 0 else 62
                bb.op("vector", lambda E, b=buf, c=col: E.tensor_reduce(
                    ST[:, c:c + 1], b, AX.X, OP.add, apply_absolute_value=True), inc=True)
                if k > 0:
                    bb.wait("vector", eV)
                    bb.op("vector", lambda E, c=acc_col: E.tensor_tensor(
                        ST[:, c:c + 1], ST[:, c:c + 1], ST[:, 62:63], OP.add), inc=True)

        mean_pass(w_w1T, 8, HID, 56)
        mean_pass(w_w2T, 32, D, 57)
        mean_pass(w_l1fT, 8, D, 58)
        bb.wait("tensor", eV)
        for i, c in enumerate((56, 57, 58)):
            bb.op("tensor", lambda E, c=c, i=i: E.matmul(
                PS[6][0:1, 4 * i:4 * i + 1], AA["c_oc"][:, 0:1], ST[:, c:c + 1],
                start=True, stop=True), inc=(i == 2))
        bb.wait("vector", eT)
        bb.op("vector", lambda E: E.tensor_scalar_mul(PS[6][0:1, 16:17], PS[6][0:1, 0:1], 1.0 / (D * HID)))
        bb.op("vector", lambda E: E.tensor_scalar_mul(PS[6][0:1, 17:18], PS[6][0:1, 4:5], 1.0 / (HID * D)))
        bb.op("vector", lambda E: E.tensor_scalar_mul(PS[6][0:1, 18:19], PS[6][0:1, 8:9], 1.0 / (D * D)),
              inc=True)
        bb.wait("vector", eV)
        bb.op("vector", lambda E: E.tensor_scalar_max(PS[6][0:1, 16:17], PS[6][0:1, 16:17], 1e-5))
        bb.op("vector", lambda E: E.tensor_scalar_max(PS[6][0:1, 17:18], PS[6][0:1, 17:18], 1e-5))
        bb.op("vector", lambda E: E.tensor_scalar_add(PS[6][0:1, 18:19], PS[6][0:1, 18:19], 1e-8),
              inc=True)
        bb.wait("vector", eV)
        bb.op("vector", lambda E: E.tensor_copy(ST[0:1, 59:62], PS[6][0:1, 16:19]), inc=True)
        bb.wait("tensor", eV)
        bb.op("tensor", lambda E: E.matmul(PS[7][:, 0:3], AA["c_or"][0:1, :], ST[0:1, 59:62],
                                           start=True, stop=True), inc=True)
        bb.wait("vector", eT)
        bb.op("vector", lambda E: E.tensor_copy(AA["c_wm"][:, 0:3], PS[7][:, 0:3]), inc=True)
        bb.wait("vector", eV)
        bb.op("vector", lambda E: E.reciprocal(AA["c_wm"][:, 3:4], AA["c_wm"][:, 2:3]))
        bb.op("vector", lambda E: E.reciprocal(AA["c_wm"][:, 4:5], AA["c_wm"][:, 0:1]))
        bb.op("vector", lambda E: E.reciprocal(AA["c_wm"][:, 5:6], AA["c_wm"][:, 1:2]), inc=True)

        def hi_lo(src, F, nk, dh, dl, scale_col=None, col_per_k=False, qcols=None,
                  ccmajor=False):
            for k in range(nk):
                buf = (AA["xg"] if k % 2 == 0 else AA["xn1"])[:, 0:F]
                hib = AA["x1"].bitcast(BF16)[:, 0:F]
                lob = AA["Bt"].bitcast(BF16)[:, 0:F]
                bb.waitc("sync", eV, eS)
                bb.wait("sync", dP)
                bb.op("sync", lambda E, b=buf, s=src, k=k: E.dma_start(
                    out=b, in_=s[k * 128:(k + 1) * 128, :]), sem=dP, inck=16)
                bb.wait("scalar", dP)
                bb.wait("vector", dP)
                segs = [(0, F, None)] if qcols is None else [(0, qcols, 0.125), (qcols, F, None)]
                for a0, a1, imm in segs:
                    scap = ((scale_col[:, k:k + 1] if col_per_k else scale_col)
                            if scale_col is not None else (1.0 if imm is None else imm))
                    bb.op("vector", lambda E, b=buf, hb=hib, a0=a0, a1=a1, sc=scap:
                          E.tensor_scalar(hb[:, a0:a1], b[:, a0:a1], sc, None, OP.mult),
                          inc=True)
                    bb.wait("vector", eV)
                    bb.op("vector", lambda E, b=buf, hb=hib, lb=lob, a0=a0, a1=a1, sc=scap:
                          E.scalar_tensor_tensor(lb[:, a0:a1], b[:, a0:a1], sc,
                                                 hb[:, a0:a1], OP.mult, OP.subtract),
                          inc=True)
                bb.wait("sync", eV)
                if ccmajor:
                    for ci in range(F // 128):
                        bb.op("sync", lambda E, hb=hib, d=dh, k=k, ci=ci: E.dma_start(
                            out=d[ci][:, k * 128:(k + 1) * 128],
                            in_=hb[:, ci * 128:(ci + 1) * 128]), sem=dP, inck=16)
                        bb.op("sync", lambda E, lb=lob, d=dl, k=k, ci=ci: E.dma_start(
                            out=d[ci][:, k * 128:(k + 1) * 128],
                            in_=lb[:, ci * 128:(ci + 1) * 128]), sem=dP, inck=16)
                else:
                    bb.op("sync", lambda E, hb=hib, d=dh, k=k: E.dma_start(out=d[k], in_=hb),
                          sem=dP, inck=16)
                    bb.op("sync", lambda E, lb=lob, d=dl, k=k: E.dma_start(out=d[k], in_=lb),
                          sem=dP, inck=16)

        hi_lo(w_qkvT, 3 * D, 8, sc_qkvT[0], sc_qkvT[1], qcols=D)
        hi_lo(w_projT, D, 8, sc_projT[0], sc_projT[1])
        hi_lo(w_q1, D, 8, sc_wq[0], sc_wq[1], scale_col=AA["c_anorm"], col_per_k=True)
        hi_lo(w_k1, D, 8, sc_wk[0], sc_wk[1], scale_col=AA["c_anorm"], col_per_k=True)
        hi_lo(w_v1, D, 8, sc_wv[0], sc_wv[1], scale_col=AA["c_anorm"], col_per_k=True)
        hi_lo(w_o1, D, 8, sc_wo[0], sc_wo[1], scale_col=AA["c_onorm"])
        hi_lo(w_g1, INTER, 8, sc_wg[0], sc_wg[1], scale_col=AA["c_mnorm"],
              col_per_k=True, ccmajor=True)
        hi_lo(w_u1, INTER, 8, sc_wu[0], sc_wu[1], scale_col=AA["c_mnorm"],
              col_per_k=True, ccmajor=True)
        hi_lo(w_d1, D, 32, sc_wd[0], sc_wd[1])

        def ternary(src, F, nk, dst, recip_col, ccmajor=False):
            for k in range(nk):
                buf = (AA["xg"] if k % 2 == 0 else AA["xn1"])[:, 0:F]
                qb = AA["x1"].bitcast(BF16)[:, 0:F]
                bb.waitc("sync", eV, eS)
                bb.wait("sync", dP)
                bb.op("sync", lambda E, b=buf, s=src, k=k: E.dma_start(
                    out=b, in_=s[k * 128:(k + 1) * 128, :]), sem=dP, inck=16)
                bb.wait("vector", dP)
                for a0 in range(0, F, 2048):
                    a1 = min(a0 + 2048, F)
                    Fs = a1 - a0
                    bseg = buf[:, a0:a1]
                    qseg = qb[:, a0:a1]
                    s1 = AA["xnf_h"].bitcast(F32)[:, 0:Fs]
                    s2 = AA["xnf_l"].bitcast(F32)[:, 0:Fs]
                    bb.op("vector", lambda E, b=bseg, s1=s1, rc=recip_col: E.tensor_scalar(
                        s1, b, rc, 1.4990234375, OP.mult, OP.min), inc=True)
                    bb.wait("vector", eV)
                    bb.op("vector", lambda E, s1=s1: E.tensor_scalar_max(s1, s1, -1.4990234375),
                          inc=True)
                    bb.wait("scalar", eV)
                    bb.op("scalar", lambda E, s1=s1, s2=s2: E.activation(s2, s1, AF.Sign, bias=CPH),
                          inc=True)
                    bb.wait("scalar", eS)
                    bb.op("scalar", lambda E, s1=s1: E.activation(s1, s1, AF.Sign, bias=CMH),
                          inc=True)
                    bb.wait("vector", eS)
                    bb.op("vector", lambda E, s1=s1, s2=s2: E.tensor_tensor(s2, s2, s1, OP.add),
                          inc=True)
                    bb.wait("vector", eV)
                    bb.op("vector", lambda E, qs=qseg, s2=s2: E.tensor_scalar_mul(qs, s2, 0.5),
                          inc=True)
                bb.wait("sync", eV)
                if ccmajor:
                    for ci in range(F // 128):
                        bb.op("sync", lambda E, qb=qb, d=dst, k=k, ci=ci: E.dma_start(
                            out=d[ci][:, k * 128:(k + 1) * 128],
                            in_=qb[:, ci * 128:(ci + 1) * 128]), sem=dP, inck=16)
                else:
                    bb.op("sync", lambda E, qb=qb, d=dst, k=k: E.dma_start(out=d[k], in_=qb),
                          sem=dP, inck=16)

        ternary(w_w1T, HID, 8, sc_w1q, AA["c_wm"][:, 4:5], ccmajor=True)
        ternary(w_w2T, D, 32, sc_w2q, AA["c_wm"][:, 5:6])
        ternary(w_l1fT, D, 8, sc_l1f, AA["c_wm"][:, 3:4])

        bb.waitc("sync", eV, eS)
        bb.op("sync", lambda E: E.dma_start(out=ST[:, 48:56], in_=qkv_b_vcol[:]),
            sem=dP, inck=16)
        bb.op("sync", lambda E: E.dma_start(out=AA["tmp"][0:1, 0:D], in_=proj_b_row[:]),
              sem=dP, inck=16)
        for k in range(8):
            buf = AA["xg"][:, 0:D]
            if k > 0:
                bb.wait("sync", eT)
            bb.op("sync", lambda E, b=buf, k=k: E.dma_start(
                out=b, in_=w_projT[k * 128:(k + 1) * 128, :]), sem=dP, inck=16)
            bb.wait("tensor", dP)
            for half in range(2):
                bb.op("tensor", lambda E, b=buf, k=k, h=half: E.matmul(
                    PS[4 + h][0:1, 0:512], ST[:, 48 + k:49 + k],
                    b[:, h * 512:(h + 1) * 512], start=(k == 0), stop=(k == 7)),
                    inc=(half == 1))
        bb.wait("vector", eT)
        bb.wait("vector", dP)
        for half in range(2):
            bb.op("vector", lambda E, h=half: E.tensor_tensor(
                AA["tmp"][0:1, h * 512:(h + 1) * 512], AA["tmp"][0:1, h * 512:(h + 1) * 512],
                PS[4 + h][0:1, 0:512], OP.add), inc=(half == 1))
        for half in range(2):
            bb.wait("tensor", eV)
            bb.op("tensor", lambda E, h=half: E.matmul(
                PS[6][:, 0:512], AA["c_or"][0:1, :], AA["tmp"][0:1, h * 512:(h + 1) * 512],
                start=True, stop=True), inc=True)
            bb.wait("vector", eT)
            bb.op("vector", lambda E, h=half: E.tensor_copy(
                AA["c_pbrow"][:, h * 512:(h + 1) * 512], PS[6][:, 0:512]), inc=True)
            bb.wait("tensor", eV)

        bb.wait("sync", dP)
        bb.barrier()

    prep()
    tap("wm", AA["c_wm"][:, 0:8], (128, 8))

    def rg_dummy(b=7, col=480):
        bb.op("tensor", lambda E, b=b, col=col: E.matmul(
            PS[b][:, col:col + 1], AA["c_idb"], AA["c_idb"][:, 0:1],
            start=True, stop=True))

    rs = {"idx": 0, "free_at": [0] * NRING}
    dWs = [bb.dsem(f"w{i}") for i in range(NRING)]

    def ring_load(chunks):
        out = []
        for (src, ncol) in chunks:
            i = rs["idx"] % NRING
            rs["idx"] += 1
            if rs["free_at"][i] > 0:
                bb.wait("sync", eT, rs["free_at"][i])
            v = bb.op("sync", lambda E, i=i, s=src, n=ncol: E.dma_start(
                out=AA[f"ring{i}"][:, 0:n], in_=s), sem=dWs[i], inck=16)
            out.append((i, AA[f"ring{i}"], v, dWs[i]))
        return out

    def ring_consumed(slot):
        rs["free_at"][slot] = eT.n

    R = {}

    def rwait(eng, sem, *keys):
        for k in keys:
            v = R.get(k, 0)
            if v:
                bb.wait(eng, sem, v)

    def rbase(sem, keys):
        for k in keys:
            R[k] = sem.n

    def do_tile(seq, t):
        T0 = t * TILE
        first = (t == 0)
        last = (t == NT - 1)
        xrow = x_in[seq]

        bb.waitc("sync", eV, eS, eG, eT)
        for g in range(4):
            bb.op("sync", lambda E, g=g: E.dma_start(
                out=AA["xg"][:, g * D:(g + 1) * D],
                in_=xrow[T0 + g * 128: T0 + (g + 1) * 128, :]), sem=dX, inck=16)
        bb.wait("vector", eS)
        if first or last:
            bb.op("vector", lambda E: E.memset(AA["xh"][:], 0.0), inc=True)
            bb.wait("sync", eV)
        if not first:
            bb.op("sync", lambda E: E.dma_start(
                out=AA["xh"][120:128, :], in_=xrow[T0 - 8:T0, :]), sem=dX, inck=16)
        if not last:
            bb.op("sync", lambda E: E.dma_start(
                out=AA["xh"][0:8, :], in_=xrow[T0 + TILE:T0 + TILE + 8, :]), sem=dX, inck=16)

        bb.wait("vector", dX)
        bb.wait("scalar", dX)
        bb.wait("vector", eV)
        for g in range(4):
            bb.op("vector", lambda E, g=g: E.tensor_reduce(
                ST[:, g:g + 1], AA["xg"][:, g * D:(g + 1) * D], AX.X, OP.add), inc=True)
            bb.op("scalar", lambda E, g=g: E.activation(
                AA["sq"][:, 0:D], AA["xg"][:, g * D:(g + 1) * D], AF.Square,
                accum_out=ST[:, 6 + g:7 + g]), inc=True)
            bb.wait("scalar", eS)
        bb.op("vector", lambda E: E.tensor_reduce(ST[:, 5:6], AA["xh"][:], AX.X, OP.add),
              inc=True)
        bb.op("scalar", lambda E: E.activation(AA["sq"][:, 0:D], AA["xh"][:], AF.Square,
                                               accum_out=ST[:, 10:11]), inc=True)
        bb.wait("vector", eS)
        bb.wait("vector", eV)
        bb.op("vector", lambda E: E.tensor_scalar_mul(ST[:, 0:4], ST[:, 0:4], 1.0 / D))
        bb.op("vector", lambda E: E.tensor_scalar_mul(ST[:, 5:6], ST[:, 5:6], 1.0 / D))
        bb.op("vector", lambda E: E.tensor_scalar_mul(ST[:, 6:11], ST[:, 6:11], 1.0 / D),
              inc=True)
        bb.wait("vector", eV)
        bb.op("vector", lambda E: E.tensor_tensor(ST[:, 11:15], ST[:, 0:4], ST[:, 0:4], OP.mult))
        bb.op("vector", lambda E: E.tensor_tensor(ST[:, 15:16], ST[:, 5:6], ST[:, 5:6], OP.mult),
              inc=True)
        bb.wait("vector", eV)
        bb.op("vector", lambda E: E.tensor_tensor(ST[:, 11:15], ST[:, 6:10], ST[:, 11:15], OP.subtract))
        bb.op("vector", lambda E: E.tensor_tensor(ST[:, 15:16], ST[:, 10:11], ST[:, 15:16], OP.subtract),
              inc=True)
        bb.wait("scalar", eV)
        bb.op("scalar", lambda E: E.activation(ST[:, 16:21], ST[:, 11:16], AF.Ln, bias=CEPS5),
              inc=True)
        bb.wait("scalar", eS)
        bb.op("scalar", lambda E: E.activation(ST[:, 16:21], ST[:, 16:21], AF.Exp, scale=-0.5),
              inc=True)

        rwait("vector", dOp[0], ("obD", 0))
        rwait("vector", dOp[1], ("obD", 1))

        def _xn_stage(g, Wb):
            halo = (g == 4)
            xsrc = AA["xh"][:] if halo else AA["xg"][:, g * D:(g + 1) * D]
            mcol = ST[:, 5:6] if halo else ST[:, g:g + 1]
            rcol = ST[:, 20:21] if halo else ST[:, 16 + g:17 + g]
            acol = 25 if halo else 21 + g
            scol = 30 if halo else 26 + g
            dcol = 35 if halo else 31 + g
            tcol = 54 if g % 2 == 0 else 55
            bb.wait("vector", eS)
            bb.op("vector", lambda E, xs=xsrc, m=mcol, r=rcol: E.tensor_scalar(
                AA["tmp"][:], xs, m, r, OP.subtract, OP.mult), inc=True)
            bb.wait("scalar", eV)
            bb.op("scalar", lambda E: E.activation(AA["tms_h"][:], AA["tmp"][:], AF.Copy),
                  inc=True)
            bb.wait("vector", eS)
            bb.op("vector", lambda E: E.tensor_tensor(
                AA["tms_l"][:], AA["tmp"][:], AA["tms_h"][:], OP.subtract))
            bb.op("vector", lambda E, a=acol: E.tensor_reduce(
                ST[:, a:a + 1], AA["tmp"][:], AX.X, OP.max, apply_absolute_value=True),
                inc=True)
            bb.wait("vector", eV)
            bb.op("vector", lambda E, a=acol: E.tensor_scalar_max(
                ST[:, a:a + 1], ST[:, a:a + 1], 1e-5), inc=True)
            bb.wait("vector", eV)
            bb.op("vector", lambda E, a=acol, tc=tcol: E.reciprocal(
                ST[:, tc:tc + 1], ST[:, a:a + 1]))
            bb.op("vector", lambda E, a=acol, d=dcol: E.tensor_tensor(
                ST[:, d:d + 1], ST[:, a:a + 1], AA["c_wm"][:, 0:1], OP.mult), inc=True)
            bb.wait("vector", eV)
            bb.op("vector", lambda E, s=scol, tc=tcol: E.tensor_scalar_mul(
                ST[:, s:s + 1], ST[:, tc:tc + 1], 127.0))
            bb.op("vector", lambda E, d=dcol: E.tensor_scalar_mul(
                ST[:, d:d + 1], ST[:, d:d + 1], 1.0 / 127.0), inc=True)
            bb.wait("vector", eV)
            return bb.op("vector", lambda E, s=scol, W=Wb: E.tensor_scalar(
                W[:, 0:D], AA["tmp"][:], ST[:, s:s + 1], 128.5, OP.mult, OP.add),
                inc=True)

        def _trans(g, specs):
            halo = (g == 4)
            bb.waitc("tensor", eV, eS)
            for dc in range(8):
                pv = PSB[dc % 2]
                for src_b, which in specs:
                    dst_fm = {"h": AA["xnf_h"], "l": AA["xnf_l"], "q": AA["xnq_fm"]}[which]
                    bb.op("tensor", lambda E, sb_=src_b, dc=dc, pv=pv:
                          E.transpose(pv[:, 0:128], sb_[:, dc * 128:(dc + 1) * 128],
                                      AA["c_idb"]), inc=True)
                    bb.wait("scalar", eT)
                    if halo:
                        bb.op("scalar", lambda E, d=dst_fm, dc=dc, pv=pv: E.copy(
                            d[:, dc * XNW + 0: dc * XNW + 8], pv[:, 120:128]))
                        bb.op("scalar", lambda E, d=dst_fm, dc=dc, pv=pv: E.copy(
                            d[:, dc * XNW + 520: dc * XNW + 528], pv[:, 0:8]), inc=True)
                    else:
                        bb.op("scalar", lambda E, d=dst_fm, dc=dc, pv=pv, g=g: E.copy(
                            d[:, dc * XNW + 8 + g * 128: dc * XNW + 8 + (g + 1) * 128],
                            pv[:, 0:128]), inc=True)
                    bb.wait("tensor", eS)

        for ga, gb in ((0, 1), (2, 3)):
            qda = AA["xnq"][:, ga * D:(ga + 1) * D]
            qdb = AA["xnq"][:, gb * D:(gb + 1) * D]
            wva = _xn_stage(ga, AA["qw"])
            _trans(ga, ((AA["tms_h"], "h"), (AA["tms_l"], "l")))
            wvb = _xn_stage(gb, AA["Bt"])
            _trans(gb, ((AA["tms_h"], "h"), (AA["tms_l"], "l")))
            round2(D, [(AA["qw"], AA["qr"], AA["qg"], qda, wva),
                       (AA["Bt"], AA["Bt"][:, D:2 * D], AA["x1"], qdb, wvb)])
            _trans(ga, ((qda, "q"),))
            _trans(gb, ((qdb, "q"),))
        qdh = AA["xnqh"][:]
        wvh = _xn_stage(4, AA["qw"])
        _trans(4, ((AA["tms_h"], "h"), (AA["tms_l"], "l")))
        round2(D, [(AA["qw"], AA["qr"], AA["qg"], qdh, wvh)])
        _trans(4, ((qdh, "q"),))

        tap("xn_t", AA["tmp"][:], (128, D))
        tap("xnq", AA["xnq"].bitcast(F32)[:, 0:2 * D], (128, 2 * D))

        def _passthrough():
            bb.waitc("sync", eV, eS, eT, eG)
            for g in range(4):
                bb.op("sync", lambda E, g=g: E.dma_start(
                    out=y_out[seq][T0 + g * 128: T0 + (g + 1) * 128, :],
                    in_=AA["xg"][:, g * D:(g + 1) * D]), sem=dO, inck=16)
        if stop_after == "ln":
            _passthrough()
            return

        rbase(eV, [("qkev", 0), ("qkev", 1), ("qkev_first",), ("qkev_b0",),
                   ("qkev_mini",)])
        for fog in range(4):
            isq = fog < 2
            base = 8 if isq else 2
            width = TILE if isq else KW
            b0 = 0 if fog % 2 == 0 else 4
            mb = 4 if fog == 2 else 0
            if fog == 0:
                bb.waitc("tensor", eS)
            rwait("tensor", eV, ("qkev", fog % 2))
            if fog == 2:
                rwait("tensor", eV, ("qkev_first",))
            if fog == 3:
                rwait("tensor", eV, ("qkev_mini",))
                rwait("tensor", eV, ("qkev_b0",))
            mmv = None
            for kc in range(8):
                sh_i, sh_ap, sh_v, sh_s = ring_load(
                    [(sc_qkvT[0][kc][:, fog * 512:(fog + 1) * 512], 512)])[0]
                sl_i, sl_ap, sl_v, sl_s = ring_load(
                    [(sc_qkvT[1][kc][:, fog * 512:(fog + 1) * 512], 512)])[0]
                bb.wait("tensor", sh_s, sh_v)
                bb.wait("tensor", sl_s, sl_v)
                ngi = 1 if isq else 2
                for fi in range(4):
                    for gi, (n0, n1) in enumerate(((0, 512),) if isq else ((0, 512), (512, KW))):
                        pb = b0 + fi if gi == 0 else mb
                        po = 0 if gi == 0 else fi * 128
                        for pi, (wap, xap) in enumerate((
                                (sh_ap, AA["xnf_h"]), (sh_ap, AA["xnf_l"]), (sl_ap, AA["xnf_h"]))):
                            if gi == 0:
                                st = (kc == 0 and pi == 0)
                                sp = (kc == 7 and pi == 2)
                            else:
                                st = (kc == 0 and fi == 0 and pi == 0)
                                sp = (kc == 7 and fi == 3 and pi == 2)
                            mmv = bb.op("tensor", lambda E, w=wap, x=xap, fi=fi, kc=kc,
                                  n0=n0, n1=n1, pb=pb, po=po, base=base, pi=pi, st=st, sp=sp:
                                  E.matmul(PS[pb][:, po:po + n1 - n0],
                                           w[:, fi * 128:(fi + 1) * 128],
                                           x[:, kc * XNW + base + n0: kc * XNW + base + n1],
                                           start=st, stop=sp),
                                  inc=(fi == 3 and gi == ngi - 1 and pi == 2))
                ring_consumed(sh_i)
                ring_consumed(sl_i)
            bb.wait("vector", eT, mmv)
            dsth = AA["q_fm_h"] if isq else AA["k_fm_h"]
            dstl = AA["q_fm_l"] if isq else AA["k_fm_l"]
            chunks = [(b0 + fi, 0, fi, 0, 512) for fi in range(4)]
            if not isq:
                chunks = ([chunks[0]]
                          + [(mb, fi * 128, fi, 512, KW) for fi in range(4)]
                          + chunks[1:])
            for ci, (pb, po, fi, n0, n1) in enumerate(chunks):
                fo = fog * 4 + fi
                foc = fo if isq else fo - 8
                bcol = AA["c_qkvb"][:, fo:fo + 1]
                bb.op("vector", lambda E, d=dsth, foc=foc, w=width, pb=pb, po=po,
                      bc=bcol, n0=n0, n1=n1: E.tensor_scalar(
                          d[:, foc * w + n0: foc * w + n1],
                          PS[pb][:, po:po + n1 - n0], bc, None, OP.add), inc=True)
                bb.wait("vector", eV)
                v_ = bb.op("vector", lambda E, dh=dsth, dl=dstl, foc=foc, w=width, pb=pb,
                      po=po, bc=bcol, n0=n0, n1=n1: E.scalar_tensor_tensor(
                          dl[:, foc * w + n0: foc * w + n1],
                          PS[pb][:, po:po + n1 - n0], bc,
                          dh[:, foc * w + n0: foc * w + n1], OP.add, OP.subtract),
                      inc=True)
                if fog == 1 and ci == 0:
                    R[("qkev_first",)] = v_
                if fog == 2 and ci == 0:
                    R[("qkev_b0",)] = v_
                if fog == 2 and ci == 4:
                    R[("qkev_mini",)] = v_
            R[("qkev", fog % 2)] = v_

        tap("k_fm_h", AA["k_fm_h"].bitcast(F32)[:, 0:4 * KW], (128, 4 * KW))
        if stop_after == "qkv":
            _passthrough()
            return

        rbase(eV, [("sc_bias", 0), ("sc_bias", 1), ("x1",), ("vV",), ("bias_last",)])
        rbase(eS, [("ssbS", 0), ("ssbS", 1), ("ptS", 0), ("ptS", 1), ("ofhS",),
                   ("vS",)])
        rbase(eT, [("etT", 0), ("etT", 1), ("avT", 0), ("avT", 1), ("av_last",)])
        for sub in range(NSUB):
            i0 = sub * 128
            var = 1 if (first and sub == 0) else (2 if (last and sub == NSUB - 1) else 0)
            rwait("sync", eV, ("bias_last",))
            bv = bb.op("sync", lambda E, v=var: E.dma_start(out=AA["Bt"][:], in_=sc_B[v]),
                       sem=dBt, inck=16)

            rwait("tensor", eV, ("x1",))
            rwait("tensor", eS, ("ofhS",))
            if sub == 0:
                rwait("tensor", eV, ("qkev", 1))
            for kc in range(8):
                vh_i, vh_ap, vh_v, vh_s = ring_load([(sc_qkvT[0][kc][:, 2 * D:3 * D], D)])[0]
                vl_i, vl_ap, vl_v, vl_s = ring_load([(sc_qkvT[1][kc][:, 2 * D:3 * D], D)])[0]
                bb.wait("tensor", vh_s, vh_v)
                bb.wait("tensor", vl_s, vl_v)
                for half in range(2):
                    for pi, (wap, xap) in enumerate((
                            (vh_ap, AA["xnf_h"]), (vh_ap, AA["xnf_l"]), (vl_ap, AA["xnf_h"]))):
                        bb.op("tensor", lambda E, w=wap, x=xap, kc=kc, half=half, i0=i0, pi=pi:
                              E.matmul(PS[4 + half][:, 0:512],
                                       x[:, kc * XNW + i0 + 2: kc * XNW + i0 + 130],
                                       w[:, half * 512:(half + 1) * 512],
                                       start=(kc == 0 and pi == 0),
                                       stop=(kc == 7 and pi == 2)))
                        bb.op("tensor", lambda E, w=wap, x=xap, kc=kc, half=half, i0=i0, pi=pi:
                              E.matmul(PS[6 + half][0:12, 0:512],
                                       x[:, kc * XNW + i0 + 130: kc * XNW + i0 + 142],
                                       w[:, half * 512:(half + 1) * 512],
                                       start=(kc == 0 and pi == 0),
                                       stop=(kc == 7 and pi == 2)),
                              inc=(half == 1 and pi == 2))
                ring_consumed(vh_i)
                ring_consumed(vl_i)
            bb.wait("scalar", eT)
            for half in range(2):
                bb.op("scalar", lambda E, h=half: E.copy(
                    AA["vt_h"][:, h * 512:(h + 1) * 512], PS[4 + h][:, 0:512]))
                v_ = bb.op("scalar", lambda E, h=half: E.copy(
                    AA["vtm_h"][0:12, h * 512:(h + 1) * 512], PS[6 + h][0:12, 0:512]),
                    inc=(half == 1))
            R[("vS",)] = v_
            bb.wait("vector", eS)
            for half in range(2):
                bb.op("vector", lambda E, h=half: E.tensor_tensor(
                    AA["vt_l"][:, h * 512:(h + 1) * 512], PS[4 + h][:, 0:512],
                    AA["vt_h"][:, h * 512:(h + 1) * 512], OP.subtract))
                v_ = bb.op("vector", lambda E, h=half: E.tensor_tensor(
                    AA["vtm_l"][0:12, h * 512:(h + 1) * 512], PS[6 + h][0:12, 0:512],
                    AA["vtm_h"][0:12, h * 512:(h + 1) * 512], OP.subtract), inc=(half == 1))
            R[("vV",)] = v_

            if stop_after == "att1" and sub == 0:
                _passthrough()
                return
            bb.wait("vector", dBt, bv)
            at_ = {}

            def _ap(hp):
                par = hp % 2
                return (par, (2, 3) if par == 0 else (6, 7), 48 if par == 0 else 58,
                        AA["ssb"][:, par * 2 * JW:(par + 1) * 2 * JW],
                        AA["et_h"][:, par * 2 * JW:(par + 1) * 2 * JW],
                        AA["et_l"][:, par * 2 * JW:(par + 1) * 2 * JW],
                        AA["pt_h"][:, par * 128:(par + 1) * 128],
                        AA["pt_l"][:, par * 128:(par + 1) * 128],
                        AA["ptm_h"][:, par * 128:(par + 1) * 128],
                        AA["ptm_l"][:, par * 128:(par + 1) * 128])

            for it in range(9):
                if it < 8:
                    hp = it
                    par, tb, stb, ssb_, eth_, etl_, pth_, ptl_, ptmh_, ptml_ = _ap(hp)
                    sbank = par
                    rwait("tensor", eV, ("sc_bias", par))
                    if hp == 1:
                        rwait("tensor", eV, ("vV",))
                        rwait("tensor", eS, ("vS",))
                    scv = None
                    for hh in range(2):
                        h = 2 * hp + hh
                        fo = h // 2
                        pr = (h % 2) * 64
                        rg_dummy(sbank, 511)
                        for pi, (qa, ka) in enumerate((
                                (AA["q_fm_h"], AA["k_fm_h"]), (AA["q_fm_h"], AA["k_fm_l"]),
                                (AA["q_fm_l"], AA["k_fm_h"]))):
                            scv = bb.op("tensor", lambda E, qa=qa, ka=ka, fo=fo, pr=pr,
                                  i0=i0, sb_=sbank, hh=hh, pi=pi:
                                  E.matmul(PS[sb_][:, hh * JW:(hh + 1) * JW],
                                           qa[pr:pr + 64, fo * TILE + i0: fo * TILE + i0 + 128],
                                           ka[pr:pr + 64, fo * KW + i0: fo * KW + i0 + JW],
                                           start=(pi == 0), stop=(pi == 2)),
                                  inc=(hh == 1 and pi == 2))
                    rg_dummy(sbank, 511)
                    rwait("vector", eS, ("ssbS", par))
                    bb.wait("vector", eT, scv)
                    R[("sc_bias", par)] = bb.op(
                        "vector", lambda E, sb_=sbank, hp=hp, d=ssb_: E.tensor_tensor(
                            d[:], PS[sb_][:, 0:2 * JW],
                            AA["Bt"][:, hp * 2 * JW:(hp + 1) * 2 * JW], OP.add), inc=True)
                    if hp == 7:
                        R[("bias_last",)] = R[("sc_bias", par)]
                    bb.wait("vector", eV)
                    bb.op("vector", lambda E, d=ssb_, stb=stb: E.tensor_reduce(
                        ST[:, stb:stb + 2], d.rearrange("p (h j) -> p h j", j=JW), AX.X,
                        OP.max), inc=True)
                    bb.wait("vector", eV)
                    at_[hp] = {"neg": bb.op("vector", lambda E, stb=stb: E.tensor_scalar_mul(
                        ST[:, stb + 2:stb + 4], ST[:, stb:stb + 2], -1.0), inc=True)}
                if it >= 1:
                    hp = it - 1
                    par, tb, stb, ssb_, eth_, etl_, pth_, ptl_, ptmh_, ptml_ = _ap(hp)
                    bb.wait("vector", eS, at_[hp]["exp"])
                    bb.op("vector", lambda E, d=ssb_, stb=stb: E.tensor_reduce(
                        ST[:, stb:stb + 2], d.rearrange("p (h j) -> p h j", j=JW), AX.X,
                        OP.add), inc=True)
                    bb.wait("vector", eV)
                    bb.op("vector", lambda E, stb=stb: E.reciprocal(
                        ST[:, stb + 4:stb + 6], ST[:, stb:stb + 2]), inc=True)
                    bb.wait("vector", eV)
                    for hh in range(2):
                        v_ = bb.op("vector", lambda E, hh=hh, d=ssb_, stb=stb: E.tensor_scalar(
                            d[:, hh * JW:(hh + 1) * JW], d[:, hh * JW:(hh + 1) * JW],
                            ST[:, stb + 4 + hh:stb + 5 + hh], None, OP.mult), inc=(hh == 1))
                    at_[hp]["norm"] = v_
                    if stop_after == "att2" and sub == 0 and hp == 0:
                        _passthrough()
                        return
                if it < 8:
                    hp = it
                    par, tb, stb, ssb_, eth_, etl_, pth_, ptl_, ptmh_, ptml_ = _ap(hp)
                    bb.wait("scalar", eV, at_[hp]["neg"])
                    for hh in range(2):
                        v_ = bb.op("scalar", lambda E, hh=hh, d=ssb_, stb=stb: E.activation(
                            d[:, hh * JW:(hh + 1) * JW], d[:, hh * JW:(hh + 1) * JW],
                            AF.Exp, bias=ST[:, stb + 2 + hh:stb + 3 + hh]), inc=(hh == 1))
                    at_[hp]["exp"] = v_
                if it >= 1:
                    hp = it - 1
                    par, tb, stb, ssb_, eth_, etl_, pth_, ptl_, ptmh_, ptml_ = _ap(hp)
                    sbank = par
                    rwait("scalar", eT, ("etT", par))
                    bb.wait("scalar", eV, at_[hp]["norm"])
                    R[("ssbS", par)] = bb.op(
                        "scalar", lambda E, d=ssb_, e=eth_: E.activation(e[:], d[:], AF.Copy),
                        inc=True)
                    rwait("vector", eT, ("etT", par))
                    bb.wait("vector", eS, R[("ssbS", par)])
                    at_[hp]["etl"] = bb.op(
                        "vector", lambda E, d=ssb_, eh=eth_, el=etl_: E.tensor_tensor(
                            el[:], d[:], eh[:], OP.subtract), inc=True)
                    bb.wait("tensor", eV, at_[hp]["etl"])
                    rwait("tensor", eS, ("ptS", par))
                    for hh in range(2):
                        h = 2 * hp + hh
                        e_h = eth_[:, hh * JW:(hh + 1) * JW]
                        e_l = etl_[:, hh * JW:(hh + 1) * JW]
                        if hp == 0 and hh == 0:
                            rwait("tensor", eV, ("vV",))
                            rwait("tensor", eS, ("vS",))
                        bb.op("tensor", lambda E, e=e_h, tb=tb: E.transpose(
                            PSB[tb[0]][:, 0:128], e[:, 0:128], AA["c_idb"]))
                        bb.op("tensor", lambda E, e=e_l, tb=tb: E.transpose(
                            PSB[tb[1]][:, 0:128], e[:, 0:128], AA["c_idb"]))
                        bb.op("tensor", lambda E, e=e_h, tb=tb: E.transpose(
                            PSB[tb[0]][0:12, 128:256], e[:, 128:140], AA["c_idb"]))
                        v_ = bb.op("tensor", lambda E, e=e_l, tb=tb: E.transpose(
                            PSB[tb[1]][0:12, 128:256], e[:, 128:140], AA["c_idb"]), inc=True)
                        if hh == 1:
                            R[("etT", par)] = v_
                        bb.wait("scalar", eT)
                        bb.op("scalar", lambda E, tb=tb, d=pth_: E.copy(
                            d[:], PSB[tb[0]][:, 0:128]))
                        bb.op("scalar", lambda E, tb=tb, d=ptl_: E.copy(
                            d[:], PSB[tb[1]][:, 0:128]))
                        bb.op("scalar", lambda E, tb=tb, d=ptmh_: E.copy(
                            d[0:12, :], PSB[tb[0]][0:12, 128:256]))
                        v_ = bb.op("scalar", lambda E, tb=tb, d=ptml_: E.copy(
                            d[0:12, :], PSB[tb[1]][0:12, 128:256]), inc=True)
                        if hh == 1:
                            R[("ptS", par)] = v_
                        bb.wait("tensor", eS)
                        ob = 4 + (h // 8)
                        osl = (h % 8) // 2
                        opr = (h % 2) * 64
                        vc = h * 64
                        for pi, (pa, va) in enumerate((
                                (pth_, AA["vt_h"]), (pth_, AA["vt_l"]),
                                (ptl_, AA["vt_h"]))):
                            bb.op("tensor", lambda E, pa=pa, va=va, ob=ob, osl=osl, opr=opr,
                                  vc=vc, pi=pi:
                                  E.matmul(PS[ob][opr:opr + 64, osl * 128:(osl + 1) * 128],
                                           va[:, vc:vc + 64], pa[:],
                                           start=(pi == 0), stop=False))
                        rg_dummy(sbank, 511)
                        for pi, (ma, mv) in enumerate((
                                (ptmh_, AA["vtm_h"]), (ptmh_, AA["vtm_l"]),
                                (ptml_, AA["vtm_h"]))):
                            v_ = bb.op("tensor", lambda E, ma=ma, mv=mv, ob=ob, osl=osl,
                                  opr=opr, vc=vc, pi=pi:
                                  E.matmul(PS[ob][opr:opr + 64, osl * 128:(osl + 1) * 128],
                                           mv[0:12, vc:vc + 64], ma[0:12, :],
                                           start=False, stop=(pi == 2)),
                                  inc=(pi == 2))
                        if hh == 1:
                            R[("avT", par)] = v_
                            if hp == 7:
                                R[("av_last",)] = v_
                        rg_dummy(sbank, 511)

            if stop_after == "att3" and sub == 0:
                _passthrough()
                return
            bb.wait("scalar", eT)
            for hp in range(8):
                ob = 4 + hp // 4
                osl = hp % 4
                v_ = bb.op("scalar", lambda E, ob=ob, osl=osl, hp=hp: E.copy(
                    AA["of_h"][:, hp * 128:(hp + 1) * 128],
                    PS[ob][:, osl * 128:(osl + 1) * 128]), inc=(hp == 7))
            R[("ofhS",)] = v_
            bb.wait("vector", eS)
            for hp in range(8):
                ob = 4 + hp // 4
                osl = hp % 4
                bb.op("vector", lambda E, ob=ob, osl=osl, hp=hp: E.tensor_tensor(
                    AA["of_l"][:, hp * 128:(hp + 1) * 128],
                    PS[ob][:, osl * 128:(osl + 1) * 128],
                    AA["of_h"][:, hp * 128:(hp + 1) * 128], OP.subtract), inc=(hp == 7))

            bb.wait("tensor", eV)
            for kc in range(8):
                ph_i, ph_ap, ph_v, ph_s = ring_load([(sc_projT[0][kc], D)])[0]
                pl_i, pl_ap, pl_v, pl_s = ring_load([(sc_projT[1][kc], D)])[0]
                bb.wait("tensor", ph_s, ph_v)
                bb.wait("tensor", pl_s, pl_v)
                for half in range(2):
                    for pi, (oa, wa) in enumerate((
                            (AA["of_h"], ph_ap), (AA["of_h"], pl_ap), (AA["of_l"], ph_ap))):
                        bb.op("tensor", lambda E, oa=oa, wa=wa, kc=kc, half=half, pi=pi:
                              E.matmul(PS[6 + half][:, 0:512],
                                       oa[:, kc * 128:(kc + 1) * 128],
                                       wa[:, half * 512:(half + 1) * 512],
                                       start=(kc == 0 and pi == 0),
                                       stop=(kc == 7 and pi == 2)),
                              inc=(half == 1 and pi == 2))
                ring_consumed(ph_i)
                ring_consumed(pl_i)
            bb.wait("vector", eT)
            for half in range(2):
                bb.op("vector", lambda E, h=half, sub=sub: E.tensor_tensor(
                    AA["x1"][:, sub * D + h * 512: sub * D + (h + 1) * 512],
                    PS[6 + h][:, 0:512], AA["c_pbrow"][:, h * 512:(h + 1) * 512], OP.add),
                    inc=(half == 1))
            bb.wait("vector", eV)
            for half in range(2):
                v_ = bb.op("vector", lambda E, h=half, sub=sub: E.tensor_tensor(
                    AA["x1"][:, sub * D + h * 512: sub * D + (h + 1) * 512],
                    AA["x1"][:, sub * D + h * 512: sub * D + (h + 1) * 512],
                    AA["xg"][:, sub * D + h * 512: sub * D + (h + 1) * 512], OP.add),
                    inc=(half == 1))
            R[("x1",)] = v_

        tap("x1a", AA["x1"][:, 0:4 * D], (128, 4 * D))
        if stop_after == "attn":
            _passthrough()
            return

        bb.waitc("tensor", eV, eS)
        for gi in range(4):
            bb.op("tensor", lambda E, gi=gi: E.transpose(
                PS[4][0:1, gi * 128:(gi + 1) * 128], ST[:, 31 + gi:32 + gi], AA["c_idf"]))
        bb.op("tensor", lambda E: E.transpose(
            PS[6][0:1, 0:128], ST[:, 35:36], AA["c_idf"]), inc=True)
        bb.wait("vector", eT)
        bb.op("vector", lambda E: E.tensor_copy(AA["qw"][0:1, 0:512], PS[4][0:1, 0:512]))
        bb.op("vector", lambda E: E.tensor_copy(AA["qw"][0:1, 512:640], PS[6][0:1, 0:128]),
              inc=True)
        bb.wait("tensor", eV)
        for g in range(4):
            bb.op("tensor", lambda E, g=g: E.matmul(
                PS[5][:, g * 128:(g + 1) * 128], AA["c_or"][0:1, :],
                AA["qw"][0:1, g * 128:(g + 1) * 128], start=True, stop=True))
        bb.op("tensor", lambda E: E.matmul(PS[6][:, 0:2], AA["c_or"][0:1, :],
                                           AA["qw"][0:1, 638:640], start=True, stop=True))
        bb.op("tensor", lambda E: E.matmul(PS[6][:, 2:4], AA["c_or"][0:1, :],
                                           AA["qw"][0:1, 512:514], start=True, stop=True),
              inc=True)
        bb.wait("vector", eT)
        bb.op("vector", lambda E: E.tensor_copy(AA["dqrow"][:, 2:514], PS[5][:, 0:512]))
        bb.op("vector", lambda E: E.tensor_copy(AA["dqrow"][:, 0:2], PS[6][:, 0:2]))
        bb.op("vector", lambda E: E.tensor_copy(AA["dqrow"][:, 514:516], PS[6][:, 2:4]))
        bb.op("vector", lambda E: E.memset(AA["hacc"][:], -3.0e38))
        bb.op("vector", lambda E: E.memset(AA["hacc2"][:], 3.0e38), inc=True)
        bb.wait("tensor", eV)

        rbase(eV, [("faV", 0), ("faV", 1)])
        rbase(eS, [("faS", 0), ("faS", 1)])
        R[("faH", 0)] = dHa[0].n
        R[("faH", 1)] = dHa[1].n
        w1pre = ring_load([(sc_w1q[0], D)])[0]
        fa = {}
        for it in range(34):
            if it >= 2:
                cc = it - 2
                par = cc % 2
                hof = par * HW_
                bb.wait("vector", eS, fa[cc]["silu"])
                if cc >= 1:
                    bb.wait("vector", eV, fa[cc - 1]["folds"])
                bb.op("vector", lambda E, hof=hof: E.tensor_tensor(
                    AA["hacc"][:], AA["hacc"][:], AA["ha"][:, hof + 1:hof + 513], OP.max))
                fa[cc]["folds"] = bb.op("vector", lambda E, hof=hof: E.tensor_tensor(
                    AA["hacc2"][:], AA["hacc2"][:], AA["ha"][:, hof + 1:hof + 513],
                    OP.min), inc=True)
                bb.wait("sync", eS, fa[cc]["silu"])
                R[("faH", par)] = bb.op("sync", lambda E, cc=cc, hof=hof: E.dma_start(
                    out=sc_hact[cc][:, 0:514], in_=AA["ha"][:, hof:hof + 514]),
                    sem=dHa[par], inck=16)
            if 1 <= it <= 32:
                cc = it - 1
                par = cc % 2
                hof = par * HW_
                bufW = AA["qw"] if par == 0 else AA["tmp"]
                bufR = AA["qr"] if par == 0 else AA["qg"]
                w1c = AA["c_convw"][:, 32 + cc:33 + cc]
                w2c = AA["c_convw"][:, 64 + cc:65 + cc]
                bbc = AA["c_convb"][:, cc:cc + 1]
                bb.wait("vector", eS, fa[cc]["w0"])
                bb.op("vector", lambda E, hof=hof, w2=w2c, bw=bufW, br=bufR:
                      E.scalar_tensor_tensor(
                          br[:, 0:514], AA["ha"][:, hof + 2:hof + 516], w2, bw[:, 0:514],
                          OP.mult, OP.add), inc=True)
                bb.wait("vector", eV)
                fa[cc]["stt1"] = bb.op(
                    "vector", lambda E, hof=hof, w1=w1c, br=bufR: E.scalar_tensor_tensor(
                        br[:, 0:514], AA["ha"][:, hof + 1:hof + 515], w1, br[:, 0:514],
                        OP.mult, OP.add), inc=True)
                bb.wait("scalar", eV, fa[cc]["stt1"])
                fa[cc]["silu"] = bb.op(
                    "scalar", lambda E, hof=hof, b_=bbc, br=bufR: E.activation(
                        AA["ha"][:, hof:hof + 514], br[:, 0:514], AF.Silu, bias=b_),
                    inc=True)
                R[("faS", par)] = fa[cc]["silu"]
            if it < 32:
                cc = it
                par = cc % 2
                hof = par * HW_
                pb0 = 0 + 2 * par
                pb1 = 1 + 2 * par
                s_i, s_ap, s_v, s_s = w1pre
                if cc < 31:
                    w1pre = ring_load([(sc_w1q[cc + 1], D)])[0]
                rwait("tensor", eV, ("faV", par))
                bb.wait("tensor", s_s, s_v)
                for kc in range(8):
                    bb.op("tensor", lambda E, w=s_ap, kc=kc, cc=cc, pb=pb0:
                          E.matmul(PS[pb][:, 0:512], w[:, kc * 128:(kc + 1) * 128],
                                   AA["xnq_fm"][:, kc * XNW + 6: kc * XNW + 518],
                                   start=(kc == 0), stop=(kc == 7)))
                    bb.op("tensor", lambda E, w=s_ap, kc=kc, cc=cc, pb=pb1:
                          E.matmul(PS[pb][:, 0:4], w[:, kc * 128:(kc + 1) * 128],
                                   AA["xnq_fm"][:, kc * XNW + 518: kc * XNW + 522],
                                   start=(kc == 0), stop=(kc == 7)),
                          inc=(kc == 7))
                ring_consumed(s_i)
                bb.wait("vector", eT)
                rwait("vector", eS, ("faS", par))
                rwait("vector", dHa[par], ("faH", par))
                bb.op("vector", lambda E, hof=hof, pb=pb0: E.tensor_tensor(
                    AA["ha"][:, hof:hof + 512], PS[pb][:, 0:512], AA["dqrow"][:, 0:512],
                    OP.mult))
                vq = bb.op("vector", lambda E, hof=hof, pb=pb1: E.tensor_tensor(
                    AA["ha"][:, hof + 512:hof + 516], PS[pb][:, 0:4],
                    AA["dqrow"][:, 512:516], OP.mult), inc=True)
                R[("faV", par)] = vq
                fa[cc] = {"dq": vq}
                bb.wait("scalar", eV, vq)
                fa[cc]["w0"] = bb.op(
                    "scalar", lambda E, hof=hof, w0=AA["c_convw"][:, cc:cc + 1],
                    bw=(AA["qw"] if par == 0 else AA["tmp"]): E.activation(
                        bw[:, 0:514], AA["ha"][:, hof:hof + 514], AF.Copy, scale=w0),
                    inc=True)
        bb.waitc("tensor", eV, eS)
        for g in range(4):
            bb.op("tensor", lambda E, g=g: E.transpose(
                PS[4][:, g * 128:(g + 1) * 128], AA["hacc"][:, g * 128:(g + 1) * 128],
                AA["c_idf"]))
            bb.op("tensor", lambda E, g=g: E.transpose(
                PS[7][:, g * 128:(g + 1) * 128], AA["hacc2"][:, g * 128:(g + 1) * 128],
                AA["c_idf"]), inc=(g == 3))
        bb.wait("vector", eT)
        bb.op("vector", lambda E: E.tensor_reduce(
            ST[:, 36:40], PS[4].rearrange("p (g i) -> p g i", i=128), AX.X, OP.max), inc=True)
        bb.op("vector", lambda E: E.tensor_reduce(
            ST[:, 54:58], PS[7].rearrange("p (g i) -> p g i", i=128), AX.X, OP.min), inc=True)
        bb.wait("vector", eV)
        bb.op("vector", lambda E: E.tensor_scalar_mul(ST[:, 54:58], ST[:, 54:58], -1.0),
              inc=True)
        bb.wait("vector", eV)
        bb.op("vector", lambda E: E.tensor_tensor(ST[:, 36:40], ST[:, 36:40], ST[:, 54:58],
                                                  OP.max), inc=True)
        bb.wait("vector", eV)
        bb.op("vector", lambda E: E.tensor_scalar_max(ST[:, 36:40], ST[:, 36:40], 1e-5),
              inc=True)
        bb.wait("vector", eV)
        bb.op("vector", lambda E: E.reciprocal(ST[:, 40:44], ST[:, 36:40]))
        bb.op("vector", lambda E: E.tensor_tensor(
            ST[:, 44:48], ST[:, 36:40], AA["c_wm"][:, 1:2].broadcast_to((128, 4)), OP.mult),
            inc=True)
        bb.wait("vector", eV)
        bb.op("vector", lambda E: E.tensor_scalar_mul(ST[:, 40:44], ST[:, 40:44], 127.0))
        bb.op("vector", lambda E: E.tensor_scalar_mul(ST[:, 44:48], ST[:, 44:48], 1.0 / 127.0),
              inc=True)
        bb.wait("tensor", eV)
        for gi in range(4):
            bb.op("tensor", lambda E, gi=gi: E.transpose(
                PS[6][0:1, gi * 128:(gi + 1) * 128], ST[:, 40 + gi:41 + gi], AA["c_idf"]),
                inc=(gi == 3))
        bb.wait("vector", eT)
        bb.op("vector", lambda E: E.tensor_copy(AA["qw"][0:1, 0:512], PS[6][0:1, 0:512]),
              inc=True)
        bb.wait("tensor", eV)
        for g in range(4):
            bb.op("tensor", lambda E, g=g: E.matmul(
                PS[5][:, g * 128:(g + 1) * 128], AA["c_or"][0:1, :],
                AA["qw"][0:1, g * 128:(g + 1) * 128], start=True, stop=True), inc=(g == 3))
        bb.wait("vector", eT)
        bb.op("vector", lambda E: E.tensor_copy(AA["srow"][:, 0:512], PS[5][:, 0:512]),
              inc=True)

        bb.wait("sync", dHa[0])
        bb.wait("sync", dHa[1])
        rbase(eV, [("fbV", 0), ("fbV", 1)])
        rbase(eT, [("fbT", 0), ("fbT", 1)])
        R[("fbH", 0)] = dHb[0].n
        R[("fbH", 1)] = dHb[1].n
        for rnd in range(2):
            tss = (0, 1) if rnd == 0 else (2, 3)
            if rnd == 1:
                bb.wait("sync", dHb[0])
                bb.wait("sync", dHb[1])
                rbase(eT, [("fbT", 0), ("fbT", 1)])

            def _ldB(cc, rnd=rnd):
                par = cc % 2
                if rnd == 0:
                    rwait("sync", eV, ("fbV", par))
                    v = bb.op("sync", lambda E, cc=cc, hof=par * HW_: E.dma_start(
                        out=AA["ha"][:, hof:hof + 514], in_=sc_hact[cc][:, 0:514]),
                        sem=dHp[par], inck=16)
                else:
                    rwait("sync", eT, ("fbT", par))
                    v = bb.op("sync", lambda E, cc=cc, par=par: E.dma_start(
                        out=AA["qhb"][:, par * TILE:(par + 1) * TILE], in_=sc_qh[cc]),
                        sem=dHp[par], inck=16)
                return (v, ring_load([(sc_w2q[cc], D)])[0])

            def _w2mm(cc, qdst, w2s, par):
                s_i, s_ap, s_v, s_s = w2s
                bb.wait("tensor", s_s, s_v)
                for tsi, ts in enumerate(tss):
                    for half in range(2):
                        bank = 4 + 2 * (ts % 2) + half
                        v_ = bb.op("tensor", lambda E, w=s_ap, cc=cc, ts=ts, half=half,
                              bank=bank, q=qdst: E.matmul(
                                  PS[bank][:, 0:512], q[:, ts * 128:(ts + 1) * 128],
                                  w[:, half * 512:(half + 1) * 512],
                                  start=(cc == 0), stop=(cc == 31)),
                              inc=(tsi == 1 and half == 1))
                ring_consumed(s_i)
                R[("fbT", par)] = v_

            if rnd == 0:
                ofhF = AA["of_h"].bitcast(F32)
                oflF = AA["of_l"].bitcast(F32)
                qd0 = AA["qhb"][:, 0:TILE]
                qd1 = AA["qhb"][:, TILE:2 * TILE]
                pre = (_ldB(0), _ldB(1))
                for pc in range(16):
                    cc0, cc1 = 2 * pc, 2 * pc + 1
                    (ldv0, w2s0), (ldv1, w2s1) = pre
                    bb.wait("vector", dHp[0], ldv0)
                    rwait("scalar", eT, ("fbT", 0))
                    rwait("scalar", dHb[0], ("fbH", 0))
                    bb.wait("vector", eV)
                    R[("fbV", 0)] = bb.op("vector", lambda E: E.tensor_tensor(
                        AA["qw"][:, 0:512], AA["ha"][:, 1:513], AA["srow"][:, 0:512],
                        OP.mult), inc=True)
                    bb.wait("vector", eV)
                    wv0 = bb.op("vector", lambda E: E.tensor_scalar_add(
                        AA["qw"][:, 0:512], AA["qw"][:, 0:512], 128.5), inc=True)
                    bb.wait("vector", dHp[1], ldv1)
                    rwait("scalar", eT, ("fbT", 1))
                    rwait("scalar", dHb[1], ("fbH", 1))
                    R[("fbV", 1)] = bb.op("vector", lambda E: E.tensor_tensor(
                        AA["tmp"][:, 0:512], AA["ha"][:, HW_ + 1:HW_ + 513],
                        AA["srow"][:, 0:512], OP.mult), inc=True)
                    bb.wait("vector", eV)
                    wv1 = bb.op("vector", lambda E: E.tensor_scalar_add(
                        AA["tmp"][:, 0:512], AA["tmp"][:, 0:512], 128.5), inc=True)
                    if pc < 15:
                        pre = (_ldB(cc0 + 2), _ldB(cc1 + 2))
                    fin = round2(512, [
                        (AA["qw"], AA["qr"], AA["qg"], qd0, wv0),
                        (AA["tmp"], ofhF, oflF, qd1, wv1)])
                    for par, (cc, qdst, w2s, fv) in enumerate(
                            ((cc0, qd0, w2s0, fin[0]), (cc1, qd1, w2s1, fin[1]))):
                        bb.wait("sync", eS, fv)
                        R[("fbH", par)] = bb.op(
                            "sync", lambda E, cc=cc, q=qdst: E.dma_start(
                                out=sc_qh[cc], in_=q), sem=dHb[par], inck=16)
                        bb.wait("tensor", eS, fv)
                        _w2mm(cc, qdst, w2s, par)
            else:
                pre = _ldB(0)
                for cc in range(32):
                    par = cc % 2
                    qdst = AA["qhb"][:, par * TILE:(par + 1) * TILE]
                    ldv, w2s = pre
                    if cc < 31:
                        pre = _ldB(cc + 1)
                    bb.wait("tensor", dHp[par], ldv)
                    _w2mm(cc, qdst, w2s, par)
            bb.wait("vector", eT)
            for ts in tss:
                for half in range(2):
                    bank = 4 + 2 * (ts % 2) + half
                    bb.op("vector", lambda E, ts=ts, half=half, bank=bank:
                          E.scalar_tensor_tensor(
                              AA["x1"][:, ts * D + half * 512: ts * D + (half + 1) * 512],
                              PS[bank][:, 0:512], ST[:, 44 + ts:45 + ts],
                              AA["x1"][:, ts * D + half * 512: ts * D + (half + 1) * 512],
                              OP.mult, OP.add), inc=True)
            bb.wait("tensor", eV)

        tap("x1", AA["x1"][:, 0:4 * D], (128, 4 * D))

        if stop_after == "L0":
            bb.waitc("sync", eV, eT)
            for g in range(4):
                bb.op("sync", lambda E, g=g: E.dma_start(
                    out=y_out[seq][T0 + g * 128: T0 + (g + 1) * 128, :],
                    in_=AA["x1"][:, g * D:(g + 1) * D]), sem=dO, inck=16)
            return
        do_tile_l1(seq, t)

    def do_tile_l1(seq, t):
        T0 = t * TILE

        bb.wait("vector", eV)
        bb.wait("vector", eT)
        bb.wait("scalar", eS)
        bb.wait("scalar", eV)
        bb.wait("scalar", eT)
        if t == 0:
            bb.wait("vector", eT)
            bb.op("vector", lambda E: E.memset(AA["S_sb"][:], 0.0), inc=True)
        for g in range(4):
            bb.op("vector", lambda E, g=g: E.tensor_reduce(
                ST[:, g:g + 1], AA["x1"][:, g * D:(g + 1) * D], AX.X, OP.add), inc=True)
            bb.op("scalar", lambda E, g=g: E.activation(
                AA["sq"][:, 0:D], AA["x1"][:, g * D:(g + 1) * D], AF.Square,
                accum_out=ST[:, 6 + g:7 + g]), inc=True)
            bb.wait("scalar", eS)
        bb.wait("vector", eS)
        bb.wait("vector", eV)
        bb.op("vector", lambda E: E.tensor_scalar_mul(ST[:, 0:4], ST[:, 0:4], 1.0 / D))
        bb.op("vector", lambda E: E.tensor_scalar_mul(ST[:, 6:10], ST[:, 6:10], 1.0 / D),
              inc=True)
        bb.wait("vector", eV)
        bb.op("vector", lambda E: E.tensor_tensor(ST[:, 11:15], ST[:, 0:4], ST[:, 0:4], OP.mult),
              inc=True)
        bb.wait("vector", eV)
        bb.op("vector", lambda E: E.tensor_tensor(ST[:, 11:15], ST[:, 6:10], ST[:, 11:15],
                                                  OP.subtract), inc=True)
        bb.wait("scalar", eV)
        bb.op("scalar", lambda E: E.activation(ST[:, 16:20], ST[:, 11:15], AF.Ln, bias=CEPS5),
              inc=True)
        bb.wait("scalar", eS)
        bb.op("scalar", lambda E: E.activation(ST[:, 16:20], ST[:, 16:20], AF.Exp, scale=-0.5),
              inc=True)
        bb.wait("vector", eS)
        for g in range(4):
            bb.op("vector", lambda E, g=g: E.tensor_scalar(
                AA["xn1"][:, g * D:(g + 1) * D], AA["x1"][:, g * D:(g + 1) * D],
                ST[:, g:g + 1], ST[:, 16 + g:17 + g], OP.subtract, OP.mult),
                inc=(g == 3))

        bb.wait("scalar", eV)
        for g in range(4):
            bb.op("scalar", lambda E, g=g: E.activation(
                AA["sq"][:, 0:D], AA["xn1"][:, g * D:(g + 1) * D], AF.Square,
                accum_out=ST[:, 21 + g:22 + g]), inc=True)
            bb.wait("scalar", eS)
        bb.op("scalar", lambda E: E.activation(ST[:, 26:30], ST[:, 21:25], AF.Ln, bias=CEPS5,
                                               scale=1.0 / D), inc=True)
        bb.wait("scalar", eS)
        bb.op("scalar", lambda E: E.activation(ST[:, 26:30], ST[:, 26:30], AF.Exp, scale=-0.5),
              inc=True)
        for g in range(4):
            bb.wait("vector", eS)
            bb.wait("vector", eT)
            bb.op("vector", lambda E, g=g: E.tensor_scalar(
                AA["tmp"][:], AA["xn1"][:, g * D:(g + 1) * D], ST[:, 26 + g:27 + g], None,
                OP.mult), inc=True)
            bb.wait("scalar", eV)
            bb.op("scalar", lambda E: E.activation(AA["tms_h"][:], AA["tmp"][:], AF.Copy),
                  inc=True)
            bb.wait("vector", eS)
            bb.op("vector", lambda E: E.tensor_tensor(
                AA["tms_l"][:], AA["tmp"][:], AA["tms_h"][:], OP.subtract), inc=True)
            bb.waitc("tensor", eV, eS)
            for dc in range(8):
                pv = PSB[dc % 2]
                for which, src_b in (("h", AA["tms_h"]), ("l", AA["tms_l"])):
                    dst = AA["h1f_h"] if which == "h" else AA["h1f_l"]
                    bb.op("tensor", lambda E, sb_=src_b, dc=dc, pv=pv: E.transpose(
                        pv[:, 0:128], sb_[:, dc * 128:(dc + 1) * 128], AA["c_idb"]), inc=True)
                    bb.wait("scalar", eT)
                    bb.op("scalar", lambda E, d=dst, dc=dc, g=g, pv=pv: E.copy(
                        d[:, dc * TILE + g * 128: dc * TILE + (g + 1) * 128], pv[:, 0:128]),
                        inc=True)
                    bb.wait("tensor", eS)

        for wwhich, scw, dsth, dstl in (("q", sc_wq, AA["q_fm_h"], AA["q_fm_l"]),
                                        ("k", sc_wk, AA["k_fm_h"], AA["k_fm_l"])):
            for fog in range(2):
                bb.waitc("tensor", eS, eV)
                for kc in range(8):
                    sh_i, sh_ap, sh_v, sh_s = ring_load(
                        [(scw[0][kc][:, fog * 512:(fog + 1) * 512], 512)])[0]
                    sl_i, sl_ap, sl_v, sl_s = ring_load(
                        [(scw[1][kc][:, fog * 512:(fog + 1) * 512], 512)])[0]
                    bb.wait("tensor", sh_s, sh_v)
                    bb.wait("tensor", sl_s, sl_v)
                    for fi in range(4):
                        for pi, (wap, xap) in enumerate((
                                (sh_ap, AA["h1f_h"]), (sh_ap, AA["h1f_l"]),
                                (sl_ap, AA["h1f_h"]))):
                            bb.op("tensor", lambda E, w=wap, x=xap, fi=fi, kc=kc, pi=pi:
                                  E.matmul(PS[fi][:, 0:512],
                                           w[:, fi * 128:(fi + 1) * 128],
                                           x[:, kc * TILE: (kc + 1) * TILE],
                                           start=(kc == 0 and pi == 0),
                                           stop=(kc == 7 and pi == 2)),
                                  inc=(fi == 3 and pi == 2))
                    ring_consumed(sh_i)
                    ring_consumed(sl_i)
                bb.wait("scalar", eT)
                for fi in range(4):
                    fo = fog * 4 + fi
                    bb.op("scalar", lambda E, d=dsth, fo=fo, fi=fi: E.copy(
                        d.bitcast(F32)[:, 0:1] if False else d[:, fo * TILE:(fo + 1) * TILE],
                        PS[fi][:, 0:512]), inc=True)
                bb.wait("vector", eS)
                for fi in range(4):
                    fo = fog * 4 + fi
                    bb.op("vector", lambda E, dh=dsth, dl=dstl, fo=fo, fi=fi: E.tensor_tensor(
                        dl[:, fo * TILE:(fo + 1) * TILE], PS[fi][:, 0:512],
                        dh[:, fo * TILE:(fo + 1) * TILE], OP.subtract), inc=(fi == 3))

        for wwhich, scw, dsth, dstl in (("kt", sc_wk, AA["k1t_h"], AA["k1t_l"]),
                                        ("vt", sc_wv, AA["v1t_h"], AA["v1t_l"])):
            bb.waitc("tensor", eS, eV)
            for kc in range(8):
                sh_i, sh_ap, sh_v, sh_s = ring_load([(scw[0][kc], D)])[0]
                sl_i, sl_ap, sl_v, sl_s = ring_load([(scw[1][kc], D)])[0]
                bb.wait("tensor", sh_s, sh_v)
                bb.wait("tensor", sl_s, sl_v)
                for ts in range(4):
                    for half in range(2):
                        bank = 2 * ts + half
                        for pi, (wap, xap) in enumerate((
                                (sh_ap, AA["h1f_h"]), (sh_ap, AA["h1f_l"]),
                                (sl_ap, AA["h1f_h"]))):
                            bb.op("tensor", lambda E, w=wap, x=xap, kc=kc, ts=ts, half=half,
                                  bank=bank, pi=pi:
                                  E.matmul(PS[bank][:, 0:512],
                                           x[:, kc * TILE + ts * 128: kc * TILE + (ts + 1) * 128],
                                           w[:, half * 512:(half + 1) * 512],
                                           start=(kc == 0 and pi == 0),
                                           stop=(kc == 7 and pi == 2)),
                                  inc=(ts == 3 and half == 1 and pi == 2))
                ring_consumed(sh_i)
                ring_consumed(sl_i)
            bb.wait("scalar", eT)
            for ts in range(4):
                for half in range(2):
                    bank = 2 * ts + half
                    bb.op("scalar", lambda E, d=dsth, ts=ts, half=half, bank=bank: E.copy(
                        d[:, ts * D + half * 512: ts * D + (half + 1) * 512],
                        PS[bank][:, 0:512]), inc=(ts == 3 and half == 1))
            bb.wait("vector", eS)
            for ts in range(4):
                for half in range(2):
                    bank = 2 * ts + half
                    bb.op("vector", lambda E, dh=dsth, dl=dstl, ts=ts, half=half, bank=bank:
                          E.tensor_tensor(
                              dl[:, ts * D + half * 512: ts * D + (half + 1) * 512],
                              PS[bank][:, 0:512],
                              dh[:, ts * D + half * 512: ts * D + (half + 1) * 512],
                              OP.subtract), inc=(ts == 3 and half == 1))

        rbase(eV, [("cmV", 0), ("cmV", 1), ("conV", 0), ("conV", 1), ("cx2V",),
                   ("amV", 0), ("amV", 1)]
              + [("csV", h) for h in range(NH1)] + [("cSl", h) for h in range(NH1)])
        rbase(eS, [("cnS", 0), ("cnS", 1)])
        rbase(eT, [("coT", 0), ("coT", 1), ("ctT", 0), ("ctT", 1)])

        def _cp(h):
            pu = h % 2
            return (pu, 6 + pu, 4 + pu, 31 if pu == 0 else 33,
                    AA["ssb"][:, pu * 140:pu * 140 + 128],
                    AA["pt_h"][:, pu * 128:(pu + 1) * 128],
                    AA["pt_l"][:, pu * 128:(pu + 1) * 128],
                    AA["et_h"][:, pu * 140:pu * 140 + 128],
                    AA["et_l"][:, pu * 140:pu * 140 + 128],
                    AA["sq"][:, pu * 128:(pu + 1) * 128])

        for c in range(4):
            cu = {}
            for i in range(10):
                if i < 8:
                    h = i
                    pu, ob_, tbk, stb2, AMF, amh_, aml_, onh_, onl_, sq_ = _cp(h)
                    rwait("tensor", eV, ("cmV", pu))
                    for pi, (ka, qa) in enumerate((
                            (AA["k_fm_h"], AA["q_fm_h"]), (AA["k_fm_h"], AA["q_fm_l"]),
                            (AA["k_fm_l"], AA["q_fm_h"]))):
                        v_ = bb.op("tensor", lambda E, ka=ka, qa=qa, h=h, c=c, pi=pi, pu=pu:
                              E.matmul(PS[pu][:, 0:128],
                                       ka[:, h * TILE + c * 128: h * TILE + (c + 1) * 128],
                                       qa[:, h * TILE + c * 128: h * TILE + (c + 1) * 128],
                                       start=(pi == 0), stop=(pi == 2)), inc=(pi == 2))
                    cu[h] = {}
                    bb.wait("vector", eT, v_)
                    R[("cmV", pu)] = bb.op(
                        "vector", lambda E, d=AMF, pu=pu: E.tensor_tensor(
                            d[:], PS[pu][:, 0:128], AA["c_tril"], OP.mult), inc=True)
                    bb.wait("vector", eV)
                    rwait("vector", eT, ("coT", pu))
                    bb.op("vector", lambda E, s=AMF, d=amh_: E.tensor_copy(d[:], s[:]),
                          inc=True)
                    bb.wait("vector", eV)
                    R[("amV", pu)] = cu[h]["am"] = bb.op(
                        "vector", lambda E, s=AMF, dh=amh_, dl=aml_: E.tensor_tensor(
                            dl[:], s[:], dh[:], OP.subtract), inc=True)
                if 1 <= i <= 8:
                    h = i - 1
                    pu, ob_, tbk, stb2, AMF, amh_, aml_, onh_, onl_, sq_ = _cp(h)
                    bb.wait("tensor", eV, cu[h]["am"])
                    rwait("tensor", eV, ("conV", pu))
                    rwait("tensor", eV, ("cSl", h))
                    for pi, (aa, va) in enumerate((
                            (amh_, AA["v1t_h"]), (amh_, AA["v1t_l"]),
                            (aml_, AA["v1t_h"]))):
                        bb.op("tensor", lambda E, aa=aa, va=va, h=h, c=c, pi=pi, ob=ob_:
                              E.matmul(PS[ob][:, 0:128], aa[:],
                                       va[:, c * D + h * 128: c * D + (h + 1) * 128],
                                       start=(pi == 0), stop=False))
                    if not (t == 0 and c == 0):
                        for pi, (qa, Sa) in enumerate((
                                (AA["q_fm_h"], AA["S_h"]), (AA["q_fm_h"], AA["S_l"]),
                                (AA["q_fm_l"], AA["S_h"]))):
                            v_ = bb.op("tensor", lambda E, qa=qa, Sa=Sa, h=h, c=c, pi=pi,
                                  ob=ob_:
                                  E.matmul(PS[ob][:, 0:128],
                                           qa[:, h * TILE + c * 128: h * TILE + (c + 1) * 128],
                                           Sa[:, h * 128:(h + 1) * 128],
                                           start=False, stop=(pi == 2)), inc=(pi == 2))
                    else:
                        v_ = bb.op("tensor", lambda E, h=h, c=c, ob=ob_, aa=aml_:
                              E.matmul(PS[ob][:, 0:128], aa[:],
                                       AA["v1t_l"][:, c * D + h * 128: c * D + (h + 1) * 128],
                                       start=False, stop=True), inc=True)
                    R[("coT", pu)] = cu[h]["o"] = v_
                    bb.wait("scalar", eT, cu[h]["o"])
                    bb.op("scalar", lambda E, d=sq_, ob=ob_, stb2=stb2: E.activation(
                        d[:], PS[ob][:, 0:128], AF.Square, accum_out=ST[:, stb2:stb2 + 1]),
                        inc=True)
                    bb.wait("scalar", eS)
                    rwait("scalar", eV, ("conV", pu))
                    bb.op("scalar", lambda E, stb2=stb2: E.activation(
                        ST[:, stb2 + 1:stb2 + 2], ST[:, stb2:stb2 + 1], AF.Ln,
                        bias=CEPS5, scale=1.0 / 128.0), inc=True)
                    bb.wait("scalar", eS)
                    cu[h]["exp"] = bb.op("scalar", lambda E, stb2=stb2: E.activation(
                        ST[:, stb2 + 1:stb2 + 2], ST[:, stb2 + 1:stb2 + 2], AF.Exp,
                        scale=-0.5), inc=True)
                    bb.wait("vector", eS, cu[h]["exp"])
                    rwait("vector", eT, ("ctT", pu))
                    bb.op("vector", lambda E, d=onh_, ob=ob_, stb2=stb2: E.tensor_scalar(
                        d[:], PS[ob][:, 0:128], ST[:, stb2 + 1:stb2 + 2], None, OP.mult),
                        inc=True)
                    bb.wait("vector", eV)
                    R[("conV", pu)] = cu[h]["onl"] = bb.op(
                        "vector", lambda E, dh=onh_, dl=onl_, ob=ob_, stb2=stb2:
                        E.scalar_tensor_tensor(
                            dl[:], PS[ob][:, 0:128], ST[:, stb2 + 1:stb2 + 2], dh[:],
                            OP.mult, OP.subtract), inc=True)
                if 2 <= i <= 9:
                    h = i - 2
                    pu, ob_, tbk, stb2, AMF, amh_, aml_, onh_, onl_, sq_ = _cp(h)
                    bb.wait("tensor", eV, cu[h]["onl"])
                    rwait("tensor", eS, ("cnS", pu))
                    bb.op("tensor", lambda E, s=onh_, tbk=tbk: E.transpose(
                        PSB[tbk][:, 0:128], s[:], AA["c_idb"]))
                    R[("ctT", pu)] = bb.op("tensor", lambda E, s=onl_, tbk=tbk: E.transpose(
                        PSB[tbk][:, 128:256], s[:], AA["c_idb"]), inc=True)
                    bb.wait("scalar", eT, R[("ctT", pu)])
                    bb.op("scalar", lambda E, h=h, tbk=tbk: E.copy(
                        AA["onT_h"][:, h * 128:(h + 1) * 128], PSB[tbk][:, 0:128]))
                    R[("cnS", pu)] = bb.op("scalar", lambda E, h=h, tbk=tbk: E.copy(
                        AA["onT_l"][:, h * 128:(h + 1) * 128], PSB[tbk][:, 128:256]),
                        inc=True)
                    sb_ = 2 + h // 4
                    so = (h % 4) * 128
                    rwait("tensor", eV, ("csV", h))
                    for pi, (ka, va) in enumerate((
                            (AA["k1t_h"], AA["v1t_h"]), (AA["k1t_h"], AA["v1t_l"]),
                            (AA["k1t_l"], AA["v1t_h"]))):
                        v_ = bb.op("tensor", lambda E, ka=ka, va=va, h=h, c=c, pi=pi,
                              sb_=sb_, so=so:
                              E.matmul(PS[sb_][:, so:so + 128],
                                       ka[:, c * D + h * 128: c * D + (h + 1) * 128],
                                       va[:, c * D + h * 128: c * D + (h + 1) * 128],
                                       start=(pi == 0), stop=(pi == 2)),
                              inc=(pi == 2))
                    bb.wait("vector", eT, v_)
                    R[("csV", h)] = bb.op("vector", lambda E, sb_=sb_, so=so, h=h:
                        E.tensor_tensor(
                            AA["S_sb"][:, h * 128:(h + 1) * 128],
                            AA["S_sb"][:, h * 128:(h + 1) * 128],
                            PS[sb_][:, so:so + 128], OP.add), inc=True)
                    bb.wait("vector", eV)
                    bb.op("vector", lambda E, h=h: E.tensor_copy(
                        AA["S_h"][:, h * 128:(h + 1) * 128],
                        AA["S_sb"][:, h * 128:(h + 1) * 128]), inc=True)
                    bb.wait("vector", eV)
                    R[("cSl", h)] = bb.op("vector", lambda E, h=h: E.tensor_tensor(
                        AA["S_l"][:, h * 128:(h + 1) * 128],
                        AA["S_sb"][:, h * 128:(h + 1) * 128],
                        AA["S_h"][:, h * 128:(h + 1) * 128], OP.subtract), inc=True)

            rwait("tensor", eS, ("cnS", 0), ("cnS", 1))
            rwait("tensor", eV, ("cx2V",))
            for kc in range(8):
                wh_i, wh_ap, wh_v, wh_s = ring_load([(sc_wo[0][kc], D)])[0]
                wl_i, wl_ap, wl_v, wl_s = ring_load([(sc_wo[1][kc], D)])[0]
                bb.wait("tensor", wh_s, wh_v)
                bb.wait("tensor", wl_s, wl_v)
                for half in range(2):
                    for pi, (oa, wa) in enumerate((
                            (AA["onT_h"], wh_ap), (AA["onT_h"], wl_ap), (AA["onT_l"], wh_ap))):
                        bb.op("tensor", lambda E, oa=oa, wa=wa, kc=kc, half=half, pi=pi:
                              E.matmul(PS[4 + half][:, 0:512],
                                       oa[:, kc * 128:(kc + 1) * 128],
                                       wa[:, half * 512:(half + 1) * 512],
                                       start=(kc == 0 and pi == 0),
                                       stop=(kc == 7 and pi == 2)),
                              inc=(half == 1 and pi == 2))
                ring_consumed(wh_i)
                ring_consumed(wl_i)
            bb.wait("vector", eT)
            for half in range(2):
                v_ = bb.op("vector", lambda E, c=c, half=half: E.tensor_tensor(
                    AA["x2"][:, c * D + half * 512: c * D + (half + 1) * 512],
                    PS[4 + half][:, 0:512],
                    AA["xn1"][:, c * D + half * 512: c * D + (half + 1) * 512], OP.add),
                    inc=(half == 1))
            R[("cx2V",)] = v_

        bb.wait("scalar", eV)
        for g in range(4):
            bb.op("scalar", lambda E, g=g: E.activation(
                AA["sq"][:, 0:D], AA["x2"][:, g * D:(g + 1) * D], AF.Square,
                accum_out=ST[:, 21 + g:22 + g]), inc=True)
            bb.wait("scalar", eS)
        bb.op("scalar", lambda E: E.activation(ST[:, 26:30], ST[:, 21:25], AF.Ln, bias=CEPS5,
                                               scale=1.0 / D), inc=True)
        bb.wait("scalar", eS)
        bb.op("scalar", lambda E: E.activation(ST[:, 26:30], ST[:, 26:30], AF.Exp, scale=-0.5),
              inc=True)
        for g in range(4):
            bb.wait("vector", eS)
            bb.wait("vector", eT)
            bb.op("vector", lambda E, g=g: E.tensor_scalar(
                AA["tmp"][:], AA["x2"][:, g * D:(g + 1) * D], ST[:, 26 + g:27 + g], None,
                OP.mult), inc=True)
            bb.wait("scalar", eV)
            bb.op("scalar", lambda E: E.activation(AA["tms_h"][:], AA["tmp"][:], AF.Copy),
                  inc=True)
            bb.wait("vector", eS)
            bb.op("vector", lambda E: E.tensor_tensor(
                AA["tms_l"][:], AA["tmp"][:], AA["tms_h"][:], OP.subtract), inc=True)
            bb.waitc("tensor", eV, eS)
            for dc in range(8):
                pv = PSB[dc % 2]
                for which, src_b in (("h", AA["tms_h"]), ("l", AA["tms_l"])):
                    dst = AA["h2f_h"] if which == "h" else AA["h2f_l"]
                    bb.op("tensor", lambda E, sb_=src_b, dc=dc, pv=pv: E.transpose(
                        pv[:, 0:128], sb_[:, dc * 128:(dc + 1) * 128], AA["c_idb"]), inc=True)
                    bb.wait("scalar", eT)
                    bb.op("scalar", lambda E, d=dst, dc=dc, g=g, pv=pv: E.copy(
                        d[:, dc * TILE + g * 128: dc * TILE + (g + 1) * 128], pv[:, 0:128]),
                        inc=True)
                    bb.wait("tensor", eS)

        for gp in range(2):
            bb.waitc("vector", eS, eT)
            wvs = []
            for gi in range(2):
                g = 2 * gp + gi
                c0 = 36 if gi == 0 else 44
                Wb = AA["qw"] if gi == 0 else AA["tmp"]
                bb.op("vector", lambda E, g=g, c0=c0: E.tensor_reduce(
                    ST[:, c0:c0 + 1], AA["xn1"][:, g * D:(g + 1) * D], AX.X, OP.max,
                    apply_absolute_value=True), inc=True)
                bb.wait("vector", eV)
                bb.op("vector", lambda E, c0=c0: E.tensor_scalar_add(
                    ST[:, c0:c0 + 1], ST[:, c0:c0 + 1], 1e-8), inc=True)
                bb.wait("vector", eV)
                bb.op("vector", lambda E, c0=c0: E.reciprocal(
                    ST[:, c0 + 1:c0 + 2], ST[:, c0:c0 + 1]))
                bb.op("vector", lambda E, g=g, c0=c0: E.tensor_tensor(
                    ST[:, 40 + g:41 + g], ST[:, c0:c0 + 1], AA["c_wm"][:, 2:3], OP.mult),
                    inc=True)
                bb.wait("vector", eV)
                bb.op("vector", lambda E, c0=c0: E.tensor_scalar_mul(
                    ST[:, c0 + 2:c0 + 3], ST[:, c0 + 1:c0 + 2], 127.0))
                bb.op("vector", lambda E, g=g: E.tensor_scalar_mul(
                    ST[:, 40 + g:41 + g], ST[:, 40 + g:41 + g], 1.0 / 127.0), inc=True)
                bb.wait("vector", eV)
                wvs.append(bb.op("vector", lambda E, g=g, c0=c0, W=Wb: E.tensor_scalar(
                    W[:], AA["xn1"][:, g * D:(g + 1) * D], ST[:, c0 + 2:c0 + 3], 128.5,
                    OP.mult, OP.add), inc=True))
            round2(D, [
                (AA["qw"], AA["qr"], AA["qg"],
                 AA["xn1q"][:, (2 * gp) * D:(2 * gp + 1) * D], wvs[0]),
                (AA["tmp"], AA["Bt"][:, 0:D], AA["Bt"][:, D:2 * D],
                 AA["xn1q"][:, (2 * gp + 1) * D:(2 * gp + 2) * D], wvs[1])])
            bb.waitc("tensor", eV, eS)
            for gi in range(2):
                g = 2 * gp + gi
                for dc in range(8):
                    pv = PSB[dc % 2]
                    bb.op("tensor", lambda E, dc=dc, g=g, pv=pv: E.transpose(
                        pv[:, 0:128], AA["xn1q"][:, g * D + dc * 128: g * D + (dc + 1) * 128],
                        AA["c_idb"]), inc=True)
                    bb.wait("scalar", eT)
                    bb.op("scalar", lambda E, dc=dc, g=g, pv=pv: E.copy(
                        AA["x1qf"][:, dc * TILE + g * 128: dc * TILE + (g + 1) * 128],
                        pv[:, 0:128]), inc=True)
                    bb.wait("tensor", eS)

        GUF = AA["ssb"]
        R[("obD", 0)] = dOp[0].n
        R[("obD", 1)] = dOp[1].n
        nout = 0
        for half in range(2):
            rbase(eV, [("muV", 0), ("muV", 1), ("mgV",)])
            rbase(eS, [("muS", 0), ("muS", 1)])
            rbase(eT, [("mdT",)])
            bb.waitc("tensor", eV, eS)
            mu = {}
            for it2 in range(33):
                if it2 < 32:
                    ic = it2
                    gpar = ic % 2
                    gb0 = 2 * gpar
                    gb1 = 2 * gpar + 1
                    gh_i, gh_ap, gh_v, gh_s = ring_load([(sc_wg[0][ic], D)])[0]
                    gl_i, gl_ap, gl_v, gl_s = ring_load([(sc_wg[1][ic], D)])[0]
                    uh_i, uh_ap, uh_v, uh_s = ring_load([(sc_wu[0][ic], D)])[0]
                    ul_i, ul_ap, ul_v, ul_s = ring_load([(sc_wu[1][ic], D)])[0]
                    rwait("tensor", eV, ("muV", gpar))
                    rwait("tensor", eS, ("muS", gpar))
                    bb.wait("tensor", gh_s, gh_v)
                    bb.wait("tensor", gl_s, gl_v)
                    bb.wait("tensor", uh_s, uh_v)
                    bb.wait("tensor", ul_s, ul_v)
                    for bank, wh, wl in ((gb0, gh_ap, gl_ap), (gb1, uh_ap, ul_ap)):
                        for pi, (wap, xap) in enumerate((
                                (wh, AA["h2f_h"]), (wh, AA["h2f_l"]), (wl, AA["h2f_h"]))):
                            for kc in range(8):
                                v_ = bb.op("tensor", lambda E, w=wap, x=xap, kc=kc,
                                      bank=bank, pi=pi, half=half:
                                      E.matmul(PS[bank][:, 0:256],
                                               w[:, kc * 128:(kc + 1) * 128],
                                               x[:, kc * TILE + half * 256: kc * TILE + half * 256 + 256],
                                               start=(kc == 0 and pi == 0),
                                               stop=(kc == 7 and pi == 2)),
                                      inc=(bank == gb1 and kc == 7 and pi == 2))
                    for si in (gh_i, gl_i, uh_i, ul_i):
                        ring_consumed(si)
                    mu[ic] = {"mm": v_}
                if it2 >= 1:
                    ic = it2 - 1
                    gpar = ic % 2
                    gb0 = 2 * gpar
                    gb1 = 2 * gpar + 1
                    bb.wait("scalar", eT, mu[ic]["mm"])
                    rwait("scalar", eV, ("mgV",))
                    R[("muS", gpar)] = mu[ic]["silu"] = bb.op(
                        "scalar", lambda E, gb0=gb0: E.activation(
                            AA["sg"][:, 0:256], PS[gb0][:, 0:256], AF.Silu), inc=True)
                    bb.wait("vector", eS, mu[ic]["silu"])
                    rwait("vector", eV, ("mguV",))
                    R[("muV", gpar)] = R[("mgV",)] = bb.op(
                        "vector", lambda E, gb1=gb1: E.tensor_tensor(
                            GUF[:, 0:256], AA["sg"][:, 0:256], PS[gb1][:, 0:256], OP.mult),
                        inc=True)
                    bb.wait("vector", eV)
                    rwait("vector", eT, ("mdT",))
                    bb.op("vector", lambda E: E.tensor_copy(
                        AA["gu_h"][:, 0:256], GUF[:, 0:256]), inc=True)
                    bb.wait("vector", eV)
                    guv = bb.op("vector", lambda E: E.tensor_tensor(
                        AA["gu_l"][:, 0:256], GUF[:, 0:256], AA["gu_h"][:, 0:256],
                        OP.subtract), inc=True)
                    R[("mguV",)] = guv
                    dh_i, dh_ap, dh_v, dh_s = ring_load([(sc_wd[0][ic], D)])[0]
                    dl_i, dl_ap, dl_v, dl_s = ring_load([(sc_wd[1][ic], D)])[0]
                    bb.wait("tensor", dh_s, dh_v)
                    bb.wait("tensor", dl_s, dl_v)
                    if ic == 0:
                        bb.wait("tensor", eV)
                    bb.wait("tensor", eV, guv)
                    for ts2 in range(2):
                        for dhf in range(2):
                            bank = 4 + 2 * ts2 + dhf
                            for pi, (ga, wa) in enumerate((
                                    (AA["gu_h"], dh_ap), (AA["gu_h"], dl_ap),
                                    (AA["gu_l"], dh_ap))):
                                v_ = bb.op("tensor", lambda E, ga=ga, wa=wa, ic=ic, ts2=ts2,
                                      dhf=dhf, bank=bank, pi=pi:
                                      E.matmul(PS[bank][:, 0:512],
                                               ga[:, ts2 * 128:(ts2 + 1) * 128],
                                               wa[:, dhf * 512:(dhf + 1) * 512],
                                               start=(ic == 0 and pi == 0),
                                               stop=(ic == 31 and pi == 2)),
                                      inc=(ts2 == 1 and dhf == 1 and pi == 2))
                    R[("mdT",)] = v_
                    ring_consumed(dh_i)
                    ring_consumed(dl_i)
            for ts2 in range(2):
                ts = half * 2 + ts2
                bb.waitc("tensor", eV)
                for kc in range(8):
                    fh_i, fh_ap, fh_v, fh_s = ring_load([(sc_l1f[kc], D)])[0]
                    bb.wait("tensor", fh_s, fh_v)
                    for dhf in range(2):
                        bb.op("tensor", lambda E, w=fh_ap, kc=kc, ts=ts, dhf=dhf:
                              E.matmul(PS[2 + dhf][:, 0:512],
                                       AA["x1qf"][:, kc * TILE + ts * 128: kc * TILE + (ts + 1) * 128],
                                       w[:, dhf * 512:(dhf + 1) * 512],
                                       start=(kc == 0), stop=(kc == 7)),
                              inc=(dhf == 1))
                    ring_consumed(fh_i)
                bb.wait("vector", eT)
                for dhf in range(2):
                    bank_d = 4 + 2 * ts2 + dhf
                    opar = nout % 2
                    nout += 1
                    ob2_ = AA["tmp"][:, opar * 512:(opar + 1) * 512]
                    rwait("vector", dOp[opar], ("obD", opar))
                    bb.op("vector", lambda E, ts=ts, dhf=dhf, d=ob2_: E.scalar_tensor_tensor(
                        d[:], PS[2 + dhf][:, 0:512], ST[:, 40 + ts:41 + ts],
                        AA["x1"][:, ts * D + dhf * 512: ts * D + (dhf + 1) * 512],
                        OP.mult, OP.add), inc=True)
                    bb.wait("vector", eV)
                    bb.op("vector", lambda E, ts=ts, dhf=dhf, d=ob2_: E.tensor_tensor(
                        d[:], d[:],
                        AA["x2"][:, ts * D + dhf * 512: ts * D + (dhf + 1) * 512], OP.add),
                        inc=True)
                    bb.wait("vector", eV)
                    bb.op("vector", lambda E, bd=bank_d, d=ob2_: E.tensor_tensor(
                        d[:], d[:], PS[bd][:, 0:512], OP.add), inc=True)
                    bb.wait("sync", eV)
                    R[("obD", opar)] = bb.op(
                        "sync", lambda E, ts=ts, dhf=dhf, T0=T0, d=ob2_: E.dma_start(
                            out=y_out[seq][T0 + ts * 128: T0 + (ts + 1) * 128,
                                           dhf * 512:(dhf + 1) * 512],
                            in_=d[:]), sem=dOp[opar], inck=16)

    if stop_after == "prep":
        for seq in range(n_seq):
            for t in range(NT):
                bb.waitc("sync", eV, eS, eG, eT)
                for g in range(4):
                    bb.op("sync", lambda E, g=g, seq=seq, t=t: E.dma_start(
                        out=y_out[seq][t * TILE + g * 128: t * TILE + (g + 1) * 128, :],
                        in_=x_in[seq][t * TILE + g * 128: t * TILE + (g + 1) * 128, :]),
                        sem=dO, inck=16)
    else:
        for seq in range(n_seq):
            for t in range(NT):
                do_tile(seq, t)

    bb.wait("sync", dO)
    bb.wait("sync", dOp[0])
    bb.wait("sync", dOp[1])
    bb.op("sync", lambda E: E.nop(nofuse=True), inc=True)

    with nc.Block() as block:
        bb.finish(block)
    return nc, dbg



def host_inputs(inp):
    f32 = np.float32
    out = {}
    out["w_qkvT"] = np.ascontiguousarray(inp["L0_attn_qkv_w"].T)
    out["w_projT"] = np.ascontiguousarray(inp["L0_attn_proj_w"].T)
    out["w_w1T"] = np.ascontiguousarray(inp["L0_ffn_w1"].T)
    out["w_w2T"] = np.ascontiguousarray(inp["L0_ffn_w2"].T)
    out["w_l1fT"] = np.ascontiguousarray(inp["L1_ffn_weight"].T)
    out["w_q1"] = np.ascontiguousarray(inp["L1_attn_Wq"])
    out["w_k1"] = np.ascontiguousarray(inp["L1_attn_Wk"])
    out["w_v1"] = np.ascontiguousarray(inp["L1_attn_Wv"])
    out["w_o1"] = np.ascontiguousarray(inp["L1_attn_Wo"])
    out["w_g1"] = np.ascontiguousarray(inp["L1_attn_Wgate"])
    out["w_u1"] = np.ascontiguousarray(inp["L1_attn_Wup"])
    out["w_d1"] = np.ascontiguousarray(inp["L1_attn_Wdown"])
    out["qkv_b_qk"] = np.ascontiguousarray(inp["L0_attn_qkv_b"][:2 * D].reshape(16, 128).T)
    out["qkv_b_vcol"] = np.ascontiguousarray(inp["L0_attn_qkv_b"][2 * D:].reshape(8, 128).T)
    out["proj_b_row"] = np.ascontiguousarray(inp["L0_attn_proj_b"].reshape(1, D))
    out["conv_w3"] = np.ascontiguousarray(
        inp["L0_ffn_conv_w"].reshape(32, 128, 3).transpose(2, 1, 0))
    out["conv_b_c"] = np.ascontiguousarray(inp["L0_ffn_conv_b"].reshape(32, 128).T)
    out["anorm_c"] = np.ascontiguousarray(inp["L1_attn_attn_norm_w"].reshape(8, 128).T)
    out["mnorm_c"] = np.ascontiguousarray(inp["L1_attn_mlp_norm_w"].reshape(8, 128).T)
    out["onorm_c"] = np.ascontiguousarray(inp["L1_attn_attn_out_norm_w"].reshape(128, 1))

    rpb = inp["L0_attn_rpb"]
    ii = np.arange(128)[:, None]
    jl = np.arange(JW)[None, :]
    out["brpb"] = np.ascontiguousarray(rpb[:, np.clip(6 + jl - ii, 0, 24)].astype(f32))
    r = jl - 6 - ii
    m_int = np.where((r >= -6) & (r <= 6), 0.0, -1e30).astype(f32)
    kloc = jl - 6
    ni_f = np.maximum(ii - 6, 0)
    m_first = np.where((kloc >= ni_f) & (kloc <= ni_f + 12), 0.0, -1e30).astype(f32)
    ni_l = np.minimum(ii - 6, 128 - 13)
    m_last = np.where((kloc >= ni_l) & (kloc <= ni_l + 12), 0.0, -1e30).astype(f32)
    out["bmask3"] = np.ascontiguousarray(np.stack([m_int, m_first, m_last]))
    jj = np.arange(128)
    out["tril_T"] = np.ascontiguousarray((jj[:, None] <= jj[None, :]).astype(f32))
    out["ident_f"] = np.eye(128, dtype=f32)
    import ml_dtypes
    out["ident_b"] = np.eye(128, dtype=ml_dtypes.bfloat16)
    out["ones_row"] = np.ones((1, 128), f32)
    out["ones_col"] = np.ones((128, 1), f32)
    cst = np.zeros((128, 8), f32)
    cst[:, 0] = -128.0
    cst[:, 1] = -0.5
    cst[:, 2] = 0.5
    cst[:, 3] = 1e-5
    cst[:, 4] = 1e-8
    cst[:, 6] = 1.0
    cst[:, 7] = 0.5
    out["consts_in"] = cst
    out["spc_in"] = np.zeros((128, 64), f32)
    return out


def kernel(**inputs):
    n_cores = 8
    Bf, Tf = inputs["L0_attn_x"].shape[0], inputs["L0_attn_x"].shape[1]
    per = Bf // n_cores
    nc, _ = build(n_seq=per, seq_len=Tf)
    common = host_inputs(inputs)
    x = np.ascontiguousarray(inputs["L0_attn_x"])
    in_maps = []
    for c in range(n_cores):
        m = dict(common)
        m["x_in"] = np.ascontiguousarray(x[c * per:(c + 1) * per])
        in_maps.append(m)
    res = run_bass_kernel_spmd(nc, in_maps, core_ids=list(range(n_cores)))
    return np.concatenate([r["y_out"] for r in res.results], axis=0)


# version 11
# speedup vs baseline: 1.0335x; 1.0230x over previous
import numpy as np
import concourse.bass as bass
import concourse.mybir as mybir
from concourse.bass_utils import run_bass_kernel_spmd

F32 = mybir.dt.float32
BF16 = mybir.dt.bfloat16
OP = mybir.AluOpType
AF = mybir.ActivationFunctionType
AX = mybir.AxisListType

D = 1024
NH0 = 16
HID = 4096
NH1 = 8
INTER = 4096

TILE = 512
NSUB = 4
KW = TILE + 12
XNW = TILE + 16
HW_ = TILE + 4
JW = 140



class Sem:
    def __init__(self, nc, name):
        self.h = nc.alloc_semaphore(name)
        self.n = 0


class B:
    ENGS = ("sync", "tensor", "vector", "scalar", "gpsimd")

    def __init__(self, nc):
        self.nc = nc
        self.q = {e: [] for e in self.ENGS}
        self.e = {e: Sem(nc, "e_" + e) for e in self.ENGS}
        self.dma = {}
        self._last_inc = {e: None for e in self.ENGS}
        self.spacer = None
        self._spc_rot = {e: 0 for e in self.ENGS}

    def dsem(self, name):
        if name not in self.dma:
            self.dma[name] = Sem(self.nc, "d_" + name)
        return self.dma[name]

    def op(self, eng, fn, inc=False, sem=None, inck=1):
        if inc or sem is not None:
            s = self.e[eng] if sem is None else sem
            if self._last_inc[eng] is s and self.spacer is not None:
                self.q[eng].append(self.spacer(eng))
            s.n += inck
            h, k = s.h, inck
            self.q[eng].append(lambda E: fn(E).then_inc(h, k))
            self._last_inc[eng] = s
            return s.n
        self.q[eng].append(lambda E: fn(E))
        self._last_inc[eng] = None
        return None

    def wait(self, eng, sem, val=None):
        v = sem.n if val is None else val
        if v <= 0:
            return
        h = sem.h
        self.q[eng].append(lambda E: E.wait_ge(h, v))

    def waitc(self, eng, *sems):
        for s in sems:
            self.wait(eng, s)

    def barrier(self):
        for e in self.ENGS:
            if self.spacer is not None and e in ("vector", "scalar", "gpsimd"):
                self.op(e, self.spacer(e), inc=True)
            else:
                self.op(e, lambda E: E.nop(nofuse=True), inc=True)
        for e in self.ENGS:
            for o in self.ENGS:
                if o != e:
                    self.wait(e, self.e[o])

    def finish(self, block):
        for e in self.ENGS:
            thunks = self.q[e]

            def body(E, thunks=thunks):
                for t in thunks:
                    t(E)
            getattr(block, e)(body)


def build(n_seq=2, seq_len=4096, taps=(), stop_after=None):
    NT = seq_len // TILE
    nc = bass.Bass("TRN2", target_bir_lowering=False, debug=False)

    x_in = nc.dram_tensor("x_in", [n_seq, seq_len, D], F32, kind="ExternalInput").ap()
    y_out = nc.dram_tensor("y_out", [n_seq, seq_len, D], F32, kind="ExternalOutput").ap()

    w_qkvT = nc.dram_tensor("w_qkvT", [D, 3 * D], F32, kind="ExternalInput").ap()
    w_projT = nc.dram_tensor("w_projT", [D, D], F32, kind="ExternalInput").ap()
    w_w1T = nc.dram_tensor("w_w1T", [D, HID], F32, kind="ExternalInput").ap()
    w_w2T = nc.dram_tensor("w_w2T", [HID, D], F32, kind="ExternalInput").ap()
    w_l1fT = nc.dram_tensor("w_l1fT", [D, D], F32, kind="ExternalInput").ap()
    w_q1 = nc.dram_tensor("w_q1", [D, D], F32, kind="ExternalInput").ap()
    w_k1 = nc.dram_tensor("w_k1", [D, D], F32, kind="ExternalInput").ap()
    w_v1 = nc.dram_tensor("w_v1", [D, D], F32, kind="ExternalInput").ap()
    w_o1 = nc.dram_tensor("w_o1", [D, D], F32, kind="ExternalInput").ap()
    w_g1 = nc.dram_tensor("w_g1", [D, INTER], F32, kind="ExternalInput").ap()
    w_u1 = nc.dram_tensor("w_u1", [D, INTER], F32, kind="ExternalInput").ap()
    w_d1 = nc.dram_tensor("w_d1", [INTER, D], F32, kind="ExternalInput").ap()

    qkv_b_qk = nc.dram_tensor("qkv_b_qk", [128, 16], F32, kind="ExternalInput").ap()
    qkv_b_vcol = nc.dram_tensor("qkv_b_vcol", [128, 8], F32, kind="ExternalInput").ap()
    proj_b_row = nc.dram_tensor("proj_b_row", [1, D], F32, kind="ExternalInput").ap()
    conv_w3 = nc.dram_tensor("conv_w3", [3, 128, 32], F32, kind="ExternalInput").ap()
    conv_b_c = nc.dram_tensor("conv_b_c", [128, 32], F32, kind="ExternalInput").ap()
    anorm_c = nc.dram_tensor("anorm_c", [128, 8], F32, kind="ExternalInput").ap()
    mnorm_c = nc.dram_tensor("mnorm_c", [128, 8], F32, kind="ExternalInput").ap()
    onorm_c = nc.dram_tensor("onorm_c", [128, 1], F32, kind="ExternalInput").ap()
    brpb = nc.dram_tensor("brpb", [NH0, 128, JW], F32, kind="ExternalInput").ap()
    bmask3 = nc.dram_tensor("bmask3", [3, 128, JW], F32, kind="ExternalInput").ap()
    tril_T = nc.dram_tensor("tril_T", [128, 128], F32, kind="ExternalInput").ap()
    ident_f = nc.dram_tensor("ident_f", [128, 128], F32, kind="ExternalInput").ap()
    ident_b = nc.dram_tensor("ident_b", [128, 128], BF16, kind="ExternalInput").ap()
    ones_row = nc.dram_tensor("ones_row", [1, 128], F32, kind="ExternalInput").ap()
    ones_col = nc.dram_tensor("ones_col", [128, 1], F32, kind="ExternalInput").ap()
    consts_in = nc.dram_tensor("consts_in", [128, 8], F32, kind="ExternalInput").ap()
    spc_in = nc.dram_tensor("spc_in", [128, 64], F32, kind="ExternalInput").ap()

    sc_qkvT = [nc.dram_tensor(f"s_qkvT{h}", [8, 128, 3 * D], BF16).ap() for h in range(2)]
    sc_projT = [nc.dram_tensor(f"s_projT{h}", [8, 128, D], BF16).ap() for h in range(2)]
    sc_w1q = nc.dram_tensor("s_w1q", [32, 128, D], BF16).ap()
    sc_w2q = nc.dram_tensor("s_w2q", [32, 128, D], BF16).ap()
    sc_l1f = nc.dram_tensor("s_l1f", [8, 128, D], BF16).ap()
    sc_wq = [nc.dram_tensor(f"s_wq{h}", [8, 128, D], BF16).ap() for h in range(2)]
    sc_wk = [nc.dram_tensor(f"s_wk{h}", [8, 128, D], BF16).ap() for h in range(2)]
    sc_wv = [nc.dram_tensor(f"s_wv{h}", [8, 128, D], BF16).ap() for h in range(2)]
    sc_wo = [nc.dram_tensor(f"s_wo{h}", [8, 128, D], BF16).ap() for h in range(2)]
    sc_wg = [nc.dram_tensor(f"s_wg{h}", [32, 128, D], BF16).ap() for h in range(2)]
    sc_wu = [nc.dram_tensor(f"s_wu{h}", [32, 128, D], BF16).ap() for h in range(2)]
    sc_wd = [nc.dram_tensor(f"s_wd{h}", [32, 128, D], BF16).ap() for h in range(2)]
    sc_B = nc.dram_tensor("s_B", [3, 128, NH0 * JW], F32).ap()
    sc_hact = nc.dram_tensor("s_hact", [32, 128, HW_], F32).ap()
    sc_qh = nc.dram_tensor("s_qh", [32, 128, TILE], BF16).ap()

    AA = {}

    def sb(name, shape, dt=F32):
        AA[name] = nc.alloc_sbuf_tensor(name, list(shape), dt).ap()
        return AA[name]

    C = sb("c_c", [128, 8])
    sb("c_convw", [128, 96])
    sb("c_convb", [128, 32])
    sb("c_qkvb", [128, 16])
    sb("c_anorm", [128, 8])
    sb("c_mnorm", [128, 8])
    sb("c_onorm", [128, 1])
    sb("c_pbrow", [128, D])
    sb("c_tril", [128, 128])
    sb("c_idf", [128, 128])
    sb("c_idb", [128, 128], BF16)
    sb("c_or", [1, 128])
    sb("c_oc", [128, 1])
    sb("c_wm", [128, 8])
    ST = sb("c_stat", [128, 64])
    sb("S_sb", [128, NH1 * 128])
    sb("S_h", [128, NH1 * 128], BF16)
    sb("S_l", [128, NH1 * 128], BF16)
    NRING = 6
    for i in range(NRING):
        sb(f"ring{i}", [128, 1024], BF16)
    sb("v1t_l", [128, 4 * D], BF16)
    sb("Bt", [128, NH0 * JW])

    sb("xg", [128, 4 * D])
    sb("xh", [128, D])
    sb("tmp", [128, D])
    sb("xnf_h", [128, 8 * XNW], BF16)
    sb("xnf_l", [128, 8 * XNW], BF16)
    sb("xnq", [128, 4 * D], BF16)
    sb("xnqh", [128, D], BF16)
    sb("xnq_fm", [128, 8 * XNW], BF16)
    sb("q_fm_h", [128, 8 * TILE], BF16)
    sb("q_fm_l", [128, 8 * TILE], BF16)
    sb("k_fm_h", [128, 8 * KW], BF16)
    sb("k_fm_l", [128, 8 * KW], BF16)
    sb("vt_h", [128, 1024], BF16)
    sb("vt_l", [128, 1024], BF16)
    sb("vtm_h", [128, 1024], BF16)
    sb("vtm_l", [128, 1024], BF16)
    sb("ssb", [128, 4 * JW])
    sb("et_h", [128, 4 * JW], BF16)
    sb("et_l", [128, 4 * JW], BF16)
    sb("pt_h", [128, 256], BF16)
    sb("pt_l", [128, 256], BF16)
    sb("ptm_h", [128, 256], BF16)
    sb("ptm_l", [128, 256], BF16)
    sb("of_h", [128, 8 * 128], BF16)
    sb("of_l", [128, 8 * 128], BF16)
    sb("x1", [128, 4 * D])
    sb("qw", [128, D])
    sb("qr", [128, D])
    sb("qg", [128, D])
    AA["sq"] = AA["qg"]
    sb("ha", [128, 2 * HW_])
    sb("qhb", [128, 2 * TILE], BF16)
    sb("dqrow", [128, HW_])
    AA["srow"] = AA["dqrow"][:, 0:TILE]
    AA["hacc"] = AA["qhb"].bitcast(F32)[:, 0:TILE]
    AA["hacc2"] = AA["of_h"].bitcast(F32)[:, 0:TILE]
    AA["tms_h"] = AA["qhb"][:, 0:D]
    AA["tms_l"] = AA["dqrow"].bitcast(BF16)[:, 0:D]
    sb("xn1", [128, 4 * D])
    AA["ob2"] = AA["tmp"][:, 0:512]
    AA["h1f_h"] = AA["xnf_h"][:, 0:8 * TILE]
    AA["h1f_l"] = AA["xnf_l"][:, 0:8 * TILE]
    AA["k1t_h"] = AA["xnq"][:, 0:4 * D]
    AA["k1t_l"] = AA["xnq_fm"][:, 0:4 * D]
    AA["v1t_h"] = AA["Bt"].bitcast(BF16)[:, 0:4 * D]

    AA["onT_h"] = AA["of_h"][:, 0:NH1 * 128]
    AA["onT_l"] = AA["of_l"][:, 0:NH1 * 128]
    AA["am_h"] = AA["pt_h"]
    AA["am_l"] = AA["pt_l"]
    AA["h2f_h"] = AA["xnf_h"][:, 0:8 * TILE]
    AA["h2f_l"] = AA["xnf_l"][:, 0:8 * TILE]
    AA["xn1q"] = AA["xnq"][:, 0:4 * D]
    AA["x1qf"] = AA["xnq_fm"][:, 0:8 * TILE]
    AA["sg"] = AA["vt_h"].bitcast(F32)[:, 0:256]
    AA["x2"] = AA["xg"][:, 0:4 * D]
    AA["on_h"] = AA["et_h"][:, 0:128]
    AA["on_l"] = AA["et_l"][:, 0:128]
    AA["gu_h"] = AA["vtm_h"][:, 0:256]
    AA["gu_l"] = AA["vtm_l"][:, 0:256]

    sb("c_spc", [128, 64])

    PS = [nc.alloc_psum_tensor(f"ps{i}", [128, 512], F32).ap() for i in range(8)]
    PSB = [p.bitcast(BF16) for p in PS]

    bb = B(nc)

    def _mk_spacer(eng):
        base = {"vector": 0, "scalar": 21, "gpsimd": 42, "tensor": 0, "sync": 0}[eng]
        if eng in ("tensor", "sync"):
            return lambda E: E.nop(nofuse=True)
        rot = bb._spc_rot[eng] % 20
        bb._spc_rot[eng] += 1
        dst = AA["c_spc"][:, base + rot:base + rot + 1]
        src = AA["c_spc"][:, base + 20:base + 21]
        if eng == "scalar":
            return lambda E: E.copy(dst, src)
        return lambda E: E.tensor_copy(dst, src)
    bb.spacer = _mk_spacer
    dW = bb.dsem("w")
    dX = bb.dsem("x")
    dO = bb.dsem("o")
    dOp = [bb.dsem("o0"), bb.dsem("o1")]
    dP = bb.dsem("p")
    dH = bb.dsem("h")
    dHa = [bb.dsem("ha0"), bb.dsem("ha1")]
    dHb = [bb.dsem("hb0"), bb.dsem("hb1")]
    dHl = bb.dsem("hl")
    dHp = [bb.dsem("hl0"), bb.dsem("hl1")]
    dBt = bb.dsem("b")
    eT, eV, eS, eG, eY = (bb.e[k] for k in ("tensor", "vector", "scalar", "gpsimd", "sync"))

    CM128 = C[:, 0:1]
    CMH = C[:, 1:2]
    CPH = C[:, 2:3]
    CEPS5 = C[:, 3:4]
    CEPS8 = C[:, 4:5]
    for val, ap in ((-128.0, CM128), (-0.5, CMH), (0.5, CPH), (1e-5, CEPS5),
                    (1e-8, CEPS8), (0.0, C[:, 5:6]), (1.0, C[:, 6:7])):
        nc.const_aps.register(F32, val, ap)

    dbg = {}

    def tap(name, src_ap, shape):
        if name not in taps:
            return
        d = nc.dram_tensor("dbg_" + name, list(shape), F32, kind="ExternalOutput").ap()
        dbg[name] = d
        for e in ("vector", "scalar", "tensor", "gpsimd"):
            bb.wait("sync", bb.e[e])
        sv = bb.op("sync", lambda E: E.dma_start(out=d[:], in_=src_ap),
                   sem=bb.dsem("dbg"), inck=16)
        for e in B.ENGS:
            bb.wait(e, bb.dsem("dbg"), sv)

    def round_from_w(Fc, dst_ap, qw_, qr_, qg_, stt="vector"):
        se = eV if stt == "vector" else eG
        bb.wait(stt, se)
        bb.wait("scalar", eS)
        bb.op(stt, lambda E: E.tensor_copy(qr_[:, 0:Fc], qw_[:, 0:Fc]), inc=True, sem=se)
        for Bv in (128.0, 64.0, 32.0, 16.0, 8.0, 4.0, 2.0, 1.0):
            bb.wait("scalar", se)
            bb.op("scalar", lambda E: E.activation(qg_[:, 0:Fc], qr_[:, 0:Fc], AF.Sign,
                                                   bias=CM128), inc=True)
            bb.wait(stt, eS)
            bb.op(stt, lambda E, c=-Bv / 2.0: E.scalar_tensor_tensor(
                qr_[:, 0:Fc], qg_[:, 0:Fc], c, qr_[:, 0:Fc], OP.mult, OP.add),
                inc=True, sem=se)
        bb.wait(stt, se)
        bb.op(stt, lambda E: E.scalar_tensor_tensor(
            qg_[:, 0:Fc], qr_[:, 0:Fc], -1.0, qw_[:, 0:Fc], OP.mult, OP.add),
            inc=True, sem=se)
        bb.wait("scalar", se)
        bb.op("scalar", lambda E: E.activation(dst_ap, qg_[:, 0:Fc], AF.Copy, bias=-0.5),
              inc=True)

    def round2(Fc, chains):
        rv = {}
        sv = {}
        for k, (Wb, Rb, Gb, dst, wv) in enumerate(chains):
            bb.wait("vector", eV, wv)
            bb.wait("vector", eS)
            rv[k] = bb.op("vector", lambda E, W=Wb, Rb=Rb, Fc=Fc: E.tensor_copy(
                Rb[:, 0:Fc], W[:, 0:Fc]), inc=True)
        for Bv in (128.0, 64.0, 32.0, 16.0, 8.0, 4.0, 2.0, 1.0):
            for k, (Wb, Rb, Gb, dst, wv) in enumerate(chains):
                bb.wait("scalar", eV, rv[k])
                sv[k] = bb.op("scalar", lambda E, Rb=Rb, Gb=Gb, Fc=Fc: E.activation(
                    Gb[:, 0:Fc], Rb[:, 0:Fc], AF.Sign, bias=CM128), inc=True)
            for k, (Wb, Rb, Gb, dst, wv) in enumerate(chains):
                bb.wait("vector", eS, sv[k])
                rv[k] = bb.op("vector", lambda E, Rb=Rb, Gb=Gb, Fc=Fc, c=-Bv / 2.0:
                              E.scalar_tensor_tensor(
                                  Rb[:, 0:Fc], Gb[:, 0:Fc], c, Rb[:, 0:Fc],
                                  OP.mult, OP.add), inc=True)
        for k, (Wb, Rb, Gb, dst, wv) in enumerate(chains):
            bb.wait("vector", eV, rv[k])
            rv[k] = bb.op("vector", lambda E, W=Wb, Rb=Rb, Gb=Gb, Fc=Fc:
                          E.scalar_tensor_tensor(
                              Gb[:, 0:Fc], Rb[:, 0:Fc], -1.0, W[:, 0:Fc],
                              OP.mult, OP.add), inc=True)
        out = []
        for k, (Wb, Rb, Gb, dst, wv) in enumerate(chains):
            bb.wait("scalar", eV, rv[k])
            out.append(bb.op("scalar", lambda E, Gb=Gb, dst=dst, Fc=Fc: E.activation(
                dst, Gb[:, 0:Fc], AF.Copy, bias=-0.5), inc=True))
        return out

    def prep():
        small = [
            (AA["c_spc"][:], spc_in[:]),
            (C[:], consts_in[:]),
            (AA["c_convw"][:, 0:32], conv_w3[0]),
            (AA["c_convw"][:, 32:64], conv_w3[1]),
            (AA["c_convw"][:, 64:96], conv_w3[2]),
            (AA["c_convb"][:], conv_b_c[:]),
            (AA["c_qkvb"][:], qkv_b_qk[:]),
            (AA["c_anorm"][:], anorm_c[:]),
            (AA["c_mnorm"][:], mnorm_c[:]),
            (AA["c_onorm"][:], onorm_c[:]),
            (AA["c_tril"][:], tril_T[:]),
            (AA["c_idf"][:], ident_f[:]),
            (AA["c_idb"][:], ident_b[:]),
            (AA["c_or"][:], ones_row[:]),
            (AA["c_oc"][:], ones_col[:]),
        ]
        for dst, src in small:
            bb.op("sync", lambda E, d=dst, s=src: E.dma_start(out=d, in_=s), sem=dP, inck=16)
        for e in ("vector", "scalar", "tensor", "gpsimd"):
            bb.wait(e, dP)
        bb.op("vector", lambda E: E.tensor_scalar_mul(
            AA["c_qkvb"][:, 0:8], AA["c_qkvb"][:, 0:8], 0.125), inc=True)

        for v in range(3):
            bb.waitc("sync", eV)
            bb.wait("sync", dP)
            bb.op("sync", lambda E, v=v: E.dma_start(out=AA["tmp"][:, 0:JW], in_=bmask3[v]),
                  sem=dP, inck=16)
            for h in range(NH0):
                bb.op("sync", lambda E, h=h: E.dma_start(
                    out=AA["Bt"][:, h * JW:(h + 1) * JW], in_=brpb[h]), sem=dP, inck=16)
            bb.wait("vector", dP)
            for h in range(NH0):
                bb.op("vector", lambda E, h=h: E.tensor_tensor(
                    AA["Bt"][:, h * JW:(h + 1) * JW], AA["Bt"][:, h * JW:(h + 1) * JW],
                    AA["tmp"][:, 0:JW], OP.add), inc=(h == NH0 - 1))
            bb.wait("sync", eV)
            bb.op("sync", lambda E, v=v: E.dma_start(out=sc_B[v], in_=AA["Bt"][:]),
                  sem=dP, inck=16)

        def mean_pass(src, nk, F, acc_col):
            for k in range(nk):
                buf = (AA["xg"] if k % 2 == 0 else AA["xn1"])[:, 0:F]
                bb.wait("sync", eV)
                bb.op("sync", lambda E, b=buf, s=src, k=k: E.dma_start(
                    out=b, in_=s[k * 128:(k + 1) * 128, :]), sem=dP, inck=16)
                bb.wait("vector", dP)
                col = acc_col if k == 0 else 62
                bb.op("vector", lambda E, b=buf, c=col: E.tensor_reduce(
                    ST[:, c:c + 1], b, AX.X, OP.add, apply_absolute_value=True), inc=True)
                if k > 0:
                    bb.wait("vector", eV)
                    bb.op("vector", lambda E, c=acc_col: E.tensor_tensor(
                        ST[:, c:c + 1], ST[:, c:c + 1], ST[:, 62:63], OP.add), inc=True)

        mean_pass(w_w1T, 8, HID, 56)
        mean_pass(w_w2T, 32, D, 57)
        mean_pass(w_l1fT, 8, D, 58)
        bb.wait("tensor", eV)
        for i, c in enumerate((56, 57, 58)):
            bb.op("tensor", lambda E, c=c, i=i: E.matmul(
                PS[6][0:1, 4 * i:4 * i + 1], AA["c_oc"][:, 0:1], ST[:, c:c + 1],
                start=True, stop=True), inc=(i == 2))
        bb.wait("vector", eT)
        bb.op("vector", lambda E: E.tensor_scalar_mul(PS[6][0:1, 16:17], PS[6][0:1, 0:1], 1.0 / (D * HID)))
        bb.op("vector", lambda E: E.tensor_scalar_mul(PS[6][0:1, 17:18], PS[6][0:1, 4:5], 1.0 / (HID * D)))
        bb.op("vector", lambda E: E.tensor_scalar_mul(PS[6][0:1, 18:19], PS[6][0:1, 8:9], 1.0 / (D * D)),
              inc=True)
        bb.wait("vector", eV)
        bb.op("vector", lambda E: E.tensor_scalar_max(PS[6][0:1, 16:17], PS[6][0:1, 16:17], 1e-5))
        bb.op("vector", lambda E: E.tensor_scalar_max(PS[6][0:1, 17:18], PS[6][0:1, 17:18], 1e-5))
        bb.op("vector", lambda E: E.tensor_scalar_add(PS[6][0:1, 18:19], PS[6][0:1, 18:19], 1e-8),
              inc=True)
        bb.wait("vector", eV)
        bb.op("vector", lambda E: E.tensor_copy(ST[0:1, 59:62], PS[6][0:1, 16:19]), inc=True)
        bb.wait("tensor", eV)
        bb.op("tensor", lambda E: E.matmul(PS[7][:, 0:3], AA["c_or"][0:1, :], ST[0:1, 59:62],
                                           start=True, stop=True), inc=True)
        bb.wait("vector", eT)
        bb.op("vector", lambda E: E.tensor_copy(AA["c_wm"][:, 0:3], PS[7][:, 0:3]), inc=True)
        bb.wait("vector", eV)
        bb.op("vector", lambda E: E.reciprocal(AA["c_wm"][:, 3:4], AA["c_wm"][:, 2:3]))
        bb.op("vector", lambda E: E.reciprocal(AA["c_wm"][:, 4:5], AA["c_wm"][:, 0:1]))
        bb.op("vector", lambda E: E.reciprocal(AA["c_wm"][:, 5:6], AA["c_wm"][:, 1:2]), inc=True)

        def hi_lo(src, F, nk, dh, dl, scale_col=None, col_per_k=False, qcols=None,
                  ccmajor=False):
            for k in range(nk):
                buf = (AA["xg"] if k % 2 == 0 else AA["xn1"])[:, 0:F]
                hib = AA["x1"].bitcast(BF16)[:, 0:F]
                lob = AA["Bt"].bitcast(BF16)[:, 0:F]
                bb.waitc("sync", eV, eS)
                bb.wait("sync", dP)
                bb.op("sync", lambda E, b=buf, s=src, k=k: E.dma_start(
                    out=b, in_=s[k * 128:(k + 1) * 128, :]), sem=dP, inck=16)
                bb.wait("scalar", dP)
                bb.wait("vector", dP)
                segs = [(0, F, None)] if qcols is None else [(0, qcols, 0.125), (qcols, F, None)]
                for a0, a1, imm in segs:
                    scap = ((scale_col[:, k:k + 1] if col_per_k else scale_col)
                            if scale_col is not None else (1.0 if imm is None else imm))
                    bb.op("vector", lambda E, b=buf, hb=hib, a0=a0, a1=a1, sc=scap:
                          E.tensor_scalar(hb[:, a0:a1], b[:, a0:a1], sc, None, OP.mult),
                          inc=True)
                    bb.wait("vector", eV)
                    bb.op("vector", lambda E, b=buf, hb=hib, lb=lob, a0=a0, a1=a1, sc=scap:
                          E.scalar_tensor_tensor(lb[:, a0:a1], b[:, a0:a1], sc,
                                                 hb[:, a0:a1], OP.mult, OP.subtract),
                          inc=True)
                bb.wait("sync", eV)
                if ccmajor:
                    for ci in range(F // 128):
                        bb.op("sync", lambda E, hb=hib, d=dh, k=k, ci=ci: E.dma_start(
                            out=d[ci][:, k * 128:(k + 1) * 128],
                            in_=hb[:, ci * 128:(ci + 1) * 128]), sem=dP, inck=16)
                        bb.op("sync", lambda E, lb=lob, d=dl, k=k, ci=ci: E.dma_start(
                            out=d[ci][:, k * 128:(k + 1) * 128],
                            in_=lb[:, ci * 128:(ci + 1) * 128]), sem=dP, inck=16)
                else:
                    bb.op("sync", lambda E, hb=hib, d=dh, k=k: E.dma_start(out=d[k], in_=hb),
                          sem=dP, inck=16)
                    bb.op("sync", lambda E, lb=lob, d=dl, k=k: E.dma_start(out=d[k], in_=lb),
                          sem=dP, inck=16)

        hi_lo(w_qkvT, 3 * D, 8, sc_qkvT[0], sc_qkvT[1], qcols=D)
        hi_lo(w_projT, D, 8, sc_projT[0], sc_projT[1])
        hi_lo(w_q1, D, 8, sc_wq[0], sc_wq[1], scale_col=AA["c_anorm"], col_per_k=True)
        hi_lo(w_k1, D, 8, sc_wk[0], sc_wk[1], scale_col=AA["c_anorm"], col_per_k=True)
        hi_lo(w_v1, D, 8, sc_wv[0], sc_wv[1], scale_col=AA["c_anorm"], col_per_k=True)
        hi_lo(w_o1, D, 8, sc_wo[0], sc_wo[1], scale_col=AA["c_onorm"])
        hi_lo(w_g1, INTER, 8, sc_wg[0], sc_wg[1], scale_col=AA["c_mnorm"],
              col_per_k=True, ccmajor=True)
        hi_lo(w_u1, INTER, 8, sc_wu[0], sc_wu[1], scale_col=AA["c_mnorm"],
              col_per_k=True, ccmajor=True)
        hi_lo(w_d1, D, 32, sc_wd[0], sc_wd[1])

        def ternary(src, F, nk, dst, recip_col, ccmajor=False):
            for k in range(nk):
                buf = (AA["xg"] if k % 2 == 0 else AA["xn1"])[:, 0:F]
                qb = AA["x1"].bitcast(BF16)[:, 0:F]
                bb.waitc("sync", eV, eS)
                bb.wait("sync", dP)
                bb.op("sync", lambda E, b=buf, s=src, k=k: E.dma_start(
                    out=b, in_=s[k * 128:(k + 1) * 128, :]), sem=dP, inck=16)
                bb.wait("vector", dP)
                for a0 in range(0, F, 2048):
                    a1 = min(a0 + 2048, F)
                    Fs = a1 - a0
                    bseg = buf[:, a0:a1]
                    qseg = qb[:, a0:a1]
                    s1 = AA["xnf_h"].bitcast(F32)[:, 0:Fs]
                    s2 = AA["xnf_l"].bitcast(F32)[:, 0:Fs]
                    bb.op("vector", lambda E, b=bseg, s1=s1, rc=recip_col: E.tensor_scalar(
                        s1, b, rc, 1.4990234375, OP.mult, OP.min), inc=True)
                    bb.wait("vector", eV)
                    bb.op("vector", lambda E, s1=s1: E.tensor_scalar_max(s1, s1, -1.4990234375),
                          inc=True)
                    bb.wait("scalar", eV)
                    bb.op("scalar", lambda E, s1=s1, s2=s2: E.activation(s2, s1, AF.Sign, bias=CPH),
                          inc=True)
                    bb.wait("scalar", eS)
                    bb.op("scalar", lambda E, s1=s1: E.activation(s1, s1, AF.Sign, bias=CMH),
                          inc=True)
                    bb.wait("vector", eS)
                    bb.op("vector", lambda E, s1=s1, s2=s2: E.tensor_tensor(s2, s2, s1, OP.add),
                          inc=True)
                    bb.wait("vector", eV)
                    bb.op("vector", lambda E, qs=qseg, s2=s2: E.tensor_scalar_mul(qs, s2, 0.5),
                          inc=True)
                bb.wait("sync", eV)
                if ccmajor:
                    for ci in range(F // 128):
                        bb.op("sync", lambda E, qb=qb, d=dst, k=k, ci=ci: E.dma_start(
                            out=d[ci][:, k * 128:(k + 1) * 128],
                            in_=qb[:, ci * 128:(ci + 1) * 128]), sem=dP, inck=16)
                else:
                    bb.op("sync", lambda E, qb=qb, d=dst, k=k: E.dma_start(out=d[k], in_=qb),
                          sem=dP, inck=16)

        ternary(w_w1T, HID, 8, sc_w1q, AA["c_wm"][:, 4:5], ccmajor=True)
        ternary(w_w2T, D, 32, sc_w2q, AA["c_wm"][:, 5:6])
        ternary(w_l1fT, D, 8, sc_l1f, AA["c_wm"][:, 3:4])

        bb.waitc("sync", eV, eS)
        bb.op("sync", lambda E: E.dma_start(out=ST[:, 48:56], in_=qkv_b_vcol[:]),
            sem=dP, inck=16)
        bb.op("sync", lambda E: E.dma_start(out=AA["tmp"][0:1, 0:D], in_=proj_b_row[:]),
              sem=dP, inck=16)
        for k in range(8):
            buf = AA["xg"][:, 0:D]
            if k > 0:
                bb.wait("sync", eT)
            bb.op("sync", lambda E, b=buf, k=k: E.dma_start(
                out=b, in_=w_projT[k * 128:(k + 1) * 128, :]), sem=dP, inck=16)
            bb.wait("tensor", dP)
            for half in range(2):
                bb.op("tensor", lambda E, b=buf, k=k, h=half: E.matmul(
                    PS[4 + h][0:1, 0:512], ST[:, 48 + k:49 + k],
                    b[:, h * 512:(h + 1) * 512], start=(k == 0), stop=(k == 7)),
                    inc=(half == 1))
        bb.wait("vector", eT)
        bb.wait("vector", dP)
        for half in range(2):
            bb.op("vector", lambda E, h=half: E.tensor_tensor(
                AA["tmp"][0:1, h * 512:(h + 1) * 512], AA["tmp"][0:1, h * 512:(h + 1) * 512],
                PS[4 + h][0:1, 0:512], OP.add), inc=(half == 1))
        for half in range(2):
            bb.wait("tensor", eV)
            bb.op("tensor", lambda E, h=half: E.matmul(
                PS[6][:, 0:512], AA["c_or"][0:1, :], AA["tmp"][0:1, h * 512:(h + 1) * 512],
                start=True, stop=True), inc=True)
            bb.wait("vector", eT)
            bb.op("vector", lambda E, h=half: E.tensor_copy(
                AA["c_pbrow"][:, h * 512:(h + 1) * 512], PS[6][:, 0:512]), inc=True)
            bb.wait("tensor", eV)

        bb.wait("sync", dP)
        bb.barrier()

    prep()
    tap("wm", AA["c_wm"][:, 0:8], (128, 8))

    def rg_dummy(b=7, col=480):
        bb.op("tensor", lambda E, b=b, col=col: E.matmul(
            PS[b][:, col:col + 1], AA["c_idb"], AA["c_idb"][:, 0:1],
            start=True, stop=True))

    rs = {"idx": 0, "free_at": [0] * NRING}
    dWs = [bb.dsem(f"w{i}") for i in range(NRING)]

    def ring_load(chunks):
        out = []
        for (src, ncol) in chunks:
            i = rs["idx"] % NRING
            rs["idx"] += 1
            eng = "sync" if i % 2 == 0 else "gpsimd"
            if rs["free_at"][i] > 0:
                bb.wait(eng, eT, rs["free_at"][i])
            v = bb.op(eng, lambda E, i=i, s=src, n=ncol: E.dma_start(
                out=AA[f"ring{i}"][:, 0:n], in_=s), sem=dWs[i], inck=16)
            out.append((i, AA[f"ring{i}"], v, dWs[i]))
        return out

    def ring_consumed(slot):
        rs["free_at"][slot] = eT.n

    R = {}

    def rwait(eng, sem, *keys):
        for k in keys:
            v = R.get(k, 0)
            if v:
                bb.wait(eng, sem, v)

    def rbase(sem, keys):
        for k in keys:
            R[k] = sem.n

    def do_tile(seq, t):
        T0 = t * TILE
        first = (t == 0)
        last = (t == NT - 1)
        xrow = x_in[seq]

        bb.waitc("sync", eV, eS, eG, eT)
        for g in range(4):
            bb.op("sync", lambda E, g=g: E.dma_start(
                out=AA["xg"][:, g * D:(g + 1) * D],
                in_=xrow[T0 + g * 128: T0 + (g + 1) * 128, :]), sem=dX, inck=16)
        bb.wait("vector", eS)
        if first or last:
            bb.op("vector", lambda E: E.memset(AA["xh"][:], 0.0), inc=True)
            bb.wait("sync", eV)
        if not first:
            bb.op("sync", lambda E: E.dma_start(
                out=AA["xh"][120:128, :], in_=xrow[T0 - 8:T0, :]), sem=dX, inck=16)
        if not last:
            bb.op("sync", lambda E: E.dma_start(
                out=AA["xh"][0:8, :], in_=xrow[T0 + TILE:T0 + TILE + 8, :]), sem=dX, inck=16)

        bb.wait("vector", dX)
        bb.wait("scalar", dX)
        bb.wait("vector", eV)
        for g in range(4):
            bb.op("vector", lambda E, g=g: E.tensor_reduce(
                ST[:, g:g + 1], AA["xg"][:, g * D:(g + 1) * D], AX.X, OP.add), inc=True)
            bb.op("scalar", lambda E, g=g: E.activation(
                AA["sq"][:, 0:D], AA["xg"][:, g * D:(g + 1) * D], AF.Square,
                accum_out=ST[:, 6 + g:7 + g]), inc=True)
            bb.wait("scalar", eS)
        bb.op("vector", lambda E: E.tensor_reduce(ST[:, 5:6], AA["xh"][:], AX.X, OP.add),
              inc=True)
        bb.op("scalar", lambda E: E.activation(AA["sq"][:, 0:D], AA["xh"][:], AF.Square,
                                               accum_out=ST[:, 10:11]), inc=True)
        bb.wait("vector", eS)
        bb.wait("vector", eV)
        bb.op("vector", lambda E: E.tensor_scalar_mul(ST[:, 0:4], ST[:, 0:4], 1.0 / D))
        bb.op("vector", lambda E: E.tensor_scalar_mul(ST[:, 5:6], ST[:, 5:6], 1.0 / D))
        bb.op("vector", lambda E: E.tensor_scalar_mul(ST[:, 6:11], ST[:, 6:11], 1.0 / D),
              inc=True)
        bb.wait("vector", eV)
        bb.op("vector", lambda E: E.tensor_tensor(ST[:, 11:15], ST[:, 0:4], ST[:, 0:4], OP.mult))
        bb.op("vector", lambda E: E.tensor_tensor(ST[:, 15:16], ST[:, 5:6], ST[:, 5:6], OP.mult),
              inc=True)
        bb.wait("vector", eV)
        bb.op("vector", lambda E: E.tensor_tensor(ST[:, 11:15], ST[:, 6:10], ST[:, 11:15], OP.subtract))
        bb.op("vector", lambda E: E.tensor_tensor(ST[:, 15:16], ST[:, 10:11], ST[:, 15:16], OP.subtract),
              inc=True)
        bb.wait("scalar", eV)
        bb.op("scalar", lambda E: E.activation(ST[:, 16:21], ST[:, 11:16], AF.Ln, bias=CEPS5),
              inc=True)
        bb.wait("scalar", eS)
        bb.op("scalar", lambda E: E.activation(ST[:, 16:21], ST[:, 16:21], AF.Exp, scale=-0.5),
              inc=True)

        rwait("vector", dOp[0], ("obD", 0))
        rwait("vector", dOp[1], ("obD", 1))

        def _xn_stage(g, Wb):
            halo = (g == 4)
            xsrc = AA["xh"][:] if halo else AA["xg"][:, g * D:(g + 1) * D]
            mcol = ST[:, 5:6] if halo else ST[:, g:g + 1]
            rcol = ST[:, 20:21] if halo else ST[:, 16 + g:17 + g]
            acol = 25 if halo else 21 + g
            scol = 30 if halo else 26 + g
            dcol = 35 if halo else 31 + g
            tcol = 54 if g % 2 == 0 else 55
            bb.wait("vector", eS)
            bb.op("vector", lambda E, xs=xsrc, m=mcol, r=rcol: E.tensor_scalar(
                AA["tmp"][:], xs, m, r, OP.subtract, OP.mult), inc=True)
            bb.wait("scalar", eV)
            bb.op("scalar", lambda E: E.activation(AA["tms_h"][:], AA["tmp"][:], AF.Copy),
                  inc=True)
            bb.wait("vector", eS)
            bb.op("vector", lambda E: E.tensor_tensor(
                AA["tms_l"][:], AA["tmp"][:], AA["tms_h"][:], OP.subtract))
            bb.op("vector", lambda E, a=acol: E.tensor_reduce(
                ST[:, a:a + 1], AA["tmp"][:], AX.X, OP.max, apply_absolute_value=True),
                inc=True)
            bb.wait("vector", eV)
            bb.op("vector", lambda E, a=acol: E.tensor_scalar_max(
                ST[:, a:a + 1], ST[:, a:a + 1], 1e-5), inc=True)
            bb.wait("vector", eV)
            bb.op("vector", lambda E, a=acol, tc=tcol: E.reciprocal(
                ST[:, tc:tc + 1], ST[:, a:a + 1]))
            bb.op("vector", lambda E, a=acol, d=dcol: E.tensor_tensor(
                ST[:, d:d + 1], ST[:, a:a + 1], AA["c_wm"][:, 0:1], OP.mult), inc=True)
            bb.wait("vector", eV)
            bb.op("vector", lambda E, s=scol, tc=tcol: E.tensor_scalar_mul(
                ST[:, s:s + 1], ST[:, tc:tc + 1], 127.0))
            bb.op("vector", lambda E, d=dcol: E.tensor_scalar_mul(
                ST[:, d:d + 1], ST[:, d:d + 1], 1.0 / 127.0), inc=True)
            bb.wait("vector", eV)
            return bb.op("vector", lambda E, s=scol, W=Wb: E.tensor_scalar(
                W[:, 0:D], AA["tmp"][:], ST[:, s:s + 1], 128.5, OP.mult, OP.add),
                inc=True)

        def _trans(g, specs):
            halo = (g == 4)
            bb.waitc("tensor", eV, eS)
            for dc in range(8):
                pv = PSB[dc % 2]
                for src_b, which in specs:
                    dst_fm = {"h": AA["xnf_h"], "l": AA["xnf_l"], "q": AA["xnq_fm"]}[which]
                    bb.op("tensor", lambda E, sb_=src_b, dc=dc, pv=pv:
                          E.transpose(pv[:, 0:128], sb_[:, dc * 128:(dc + 1) * 128],
                                      AA["c_idb"]), inc=True)
                    bb.wait("scalar", eT)
                    if halo:
                        bb.op("scalar", lambda E, d=dst_fm, dc=dc, pv=pv: E.copy(
                            d[:, dc * XNW + 0: dc * XNW + 8], pv[:, 120:128]))
                        bb.op("scalar", lambda E, d=dst_fm, dc=dc, pv=pv: E.copy(
                            d[:, dc * XNW + 520: dc * XNW + 528], pv[:, 0:8]), inc=True)
                    else:
                        bb.op("scalar", lambda E, d=dst_fm, dc=dc, pv=pv, g=g: E.copy(
                            d[:, dc * XNW + 8 + g * 128: dc * XNW + 8 + (g + 1) * 128],
                            pv[:, 0:128]), inc=True)
                    bb.wait("tensor", eS)

        for ga, gb in ((0, 1), (2, 3)):
            qda = AA["xnq"][:, ga * D:(ga + 1) * D]
            qdb = AA["xnq"][:, gb * D:(gb + 1) * D]
            wva = _xn_stage(ga, AA["qw"])
            _trans(ga, ((AA["tms_h"], "h"), (AA["tms_l"], "l")))
            wvb = _xn_stage(gb, AA["Bt"])
            _trans(gb, ((AA["tms_h"], "h"), (AA["tms_l"], "l")))
            round2(D, [(AA["qw"], AA["qr"], AA["qg"], qda, wva),
                       (AA["Bt"], AA["Bt"][:, D:2 * D], AA["x1"], qdb, wvb)])
            _trans(ga, ((qda, "q"),))
            _trans(gb, ((qdb, "q"),))
        qdh = AA["xnqh"][:]
        wvh = _xn_stage(4, AA["qw"])
        _trans(4, ((AA["tms_h"], "h"), (AA["tms_l"], "l")))
        round2(D, [(AA["qw"], AA["qr"], AA["qg"], qdh, wvh)])
        _trans(4, ((qdh, "q"),))

        tap("xn_t", AA["tmp"][:], (128, D))
        tap("xnq", AA["xnq"].bitcast(F32)[:, 0:2 * D], (128, 2 * D))

        def _passthrough():
            bb.waitc("sync", eV, eS, eT, eG)
            for g in range(4):
                bb.op("sync", lambda E, g=g: E.dma_start(
                    out=y_out[seq][T0 + g * 128: T0 + (g + 1) * 128, :],
                    in_=AA["xg"][:, g * D:(g + 1) * D]), sem=dO, inck=16)
        if stop_after == "ln":
            _passthrough()
            return

        for fog in range(4):
            isq = fog < 2
            base = 8 if isq else 2
            width = TILE if isq else KW
            bb.waitc("tensor", eS, eV)
            for kc in range(8):
                sh_i, sh_ap, sh_v, sh_s = ring_load(
                    [(sc_qkvT[0][kc][:, fog * 512:(fog + 1) * 512], 512)])[0]
                sl_i, sl_ap, sl_v, sl_s = ring_load(
                    [(sc_qkvT[1][kc][:, fog * 512:(fog + 1) * 512], 512)])[0]
                bb.wait("tensor", sh_s, sh_v)
                bb.wait("tensor", sl_s, sl_v)
                ngi = 1 if isq else 2
                for fi in range(4):
                    for gi, (n0, n1) in enumerate(((0, 512),) if isq else ((0, 512), (512, KW))):
                        pb = fi if gi == 0 else 6
                        po = 0 if gi == 0 else fi * 128
                        for pi, (wap, xap) in enumerate((
                                (sh_ap, AA["xnf_h"]), (sh_ap, AA["xnf_l"]), (sl_ap, AA["xnf_h"]))):
                            if gi == 0:
                                st = (kc == 0 and pi == 0)
                                sp = (kc == 7 and pi == 2)
                            else:
                                st = (kc == 0 and fi == 0 and pi == 0)
                                sp = (kc == 7 and fi == 3 and pi == 2)
                            bb.op("tensor", lambda E, w=wap, x=xap, fi=fi, kc=kc, n0=n0, n1=n1,
                                  pb=pb, po=po, base=base, pi=pi, st=st, sp=sp:
                                  E.matmul(PS[pb][:, po:po + n1 - n0],
                                           w[:, fi * 128:(fi + 1) * 128],
                                           x[:, kc * XNW + base + n0: kc * XNW + base + n1],
                                           start=st, stop=sp),
                                  inc=(fi == 3 and gi == ngi - 1 and pi == 2))
                ring_consumed(sh_i)
                ring_consumed(sl_i)
            bb.wait("scalar", eT)
            bb.wait("vector", eT)
            dsth = AA["q_fm_h"] if isq else AA["k_fm_h"]
            dstl = AA["q_fm_l"] if isq else AA["k_fm_l"]
            for phase in range(2):
                for fi in range(4):
                    fo = fog * 4 + fi
                    foc = fo if isq else fo - 8
                    bcol = AA["c_qkvb"][:, fo:fo + 1]
                    for gi, (n0, n1) in enumerate(((0, 512),) if isq else ((0, 512), (512, KW))):
                        pb = fi if gi == 0 else 6
                        po = 0 if gi == 0 else fi * 128
                        if phase == 0:
                            bb.op("vector", lambda E, d=dsth, foc=foc, w=width, pb=pb, po=po,
                                  bc=bcol, n0=n0, n1=n1: E.tensor_scalar(
                                      d[:, foc * w + n0: foc * w + n1],
                                      PS[pb][:, po:po + n1 - n0], bc, None, OP.add),
                                  inc=(fi == 3 and gi == (0 if isq else 1)))
                        else:
                            bb.op("vector", lambda E, dh=dsth, dl=dstl, foc=foc, w=width, pb=pb,
                                  po=po, bc=bcol, n0=n0, n1=n1: E.scalar_tensor_tensor(
                                      dl[:, foc * w + n0: foc * w + n1],
                                      PS[pb][:, po:po + n1 - n0], bc,
                                      dh[:, foc * w + n0: foc * w + n1], OP.add, OP.subtract),
                                  inc=True)
                if phase == 0:
                    bb.wait("vector", eV)

        tap("k_fm_h", AA["k_fm_h"].bitcast(F32)[:, 0:4 * KW], (128, 4 * KW))
        if stop_after == "qkv":
            _passthrough()
            return

        rbase(eV, [("sc_bias", 0), ("sc_bias", 1), ("x1",), ("vV",), ("bias_last",)])
        rbase(eS, [("ssbS", 0), ("ssbS", 1), ("ptS", 0), ("ptS", 1), ("ofhS",),
                   ("vS",)])
        rbase(eT, [("etT", 0), ("etT", 1), ("avT", 0), ("avT", 1), ("av_last",)])
        for sub in range(NSUB):
            i0 = sub * 128
            var = 1 if (first and sub == 0) else (2 if (last and sub == NSUB - 1) else 0)
            rwait("sync", eV, ("bias_last",))
            bv = bb.op("sync", lambda E, v=var: E.dma_start(out=AA["Bt"][:], in_=sc_B[v]),
                       sem=dBt, inck=16)

            rwait("tensor", eV, ("x1",))
            rwait("tensor", eS, ("ofhS",))
            for kc in range(8):
                vh_i, vh_ap, vh_v, vh_s = ring_load([(sc_qkvT[0][kc][:, 2 * D:3 * D], D)])[0]
                vl_i, vl_ap, vl_v, vl_s = ring_load([(sc_qkvT[1][kc][:, 2 * D:3 * D], D)])[0]
                bb.wait("tensor", vh_s, vh_v)
                bb.wait("tensor", vl_s, vl_v)
                for half in range(2):
                    for pi, (wap, xap) in enumerate((
                            (vh_ap, AA["xnf_h"]), (vh_ap, AA["xnf_l"]), (vl_ap, AA["xnf_h"]))):
                        bb.op("tensor", lambda E, w=wap, x=xap, kc=kc, half=half, i0=i0, pi=pi:
                              E.matmul(PS[4 + half][:, 0:512],
                                       x[:, kc * XNW + i0 + 2: kc * XNW + i0 + 130],
                                       w[:, half * 512:(half + 1) * 512],
                                       start=(kc == 0 and pi == 0),
                                       stop=(kc == 7 and pi == 2)))
                        bb.op("tensor", lambda E, w=wap, x=xap, kc=kc, half=half, i0=i0, pi=pi:
                              E.matmul(PS[6 + half][0:12, 0:512],
                                       x[:, kc * XNW + i0 + 130: kc * XNW + i0 + 142],
                                       w[:, half * 512:(half + 1) * 512],
                                       start=(kc == 0 and pi == 0),
                                       stop=(kc == 7 and pi == 2)),
                              inc=(half == 1 and pi == 2))
                ring_consumed(vh_i)
                ring_consumed(vl_i)
            bb.wait("scalar", eT)
            for half in range(2):
                bb.op("scalar", lambda E, h=half: E.copy(
                    AA["vt_h"][:, h * 512:(h + 1) * 512], PS[4 + h][:, 0:512]))
                v_ = bb.op("scalar", lambda E, h=half: E.copy(
                    AA["vtm_h"][0:12, h * 512:(h + 1) * 512], PS[6 + h][0:12, 0:512]),
                    inc=(half == 1))
            R[("vS",)] = v_
            bb.wait("vector", eS)
            for half in range(2):
                bb.op("vector", lambda E, h=half: E.tensor_tensor(
                    AA["vt_l"][:, h * 512:(h + 1) * 512], PS[4 + h][:, 0:512],
                    AA["vt_h"][:, h * 512:(h + 1) * 512], OP.subtract))
                v_ = bb.op("vector", lambda E, h=half: E.tensor_tensor(
                    AA["vtm_l"][0:12, h * 512:(h + 1) * 512], PS[6 + h][0:12, 0:512],
                    AA["vtm_h"][0:12, h * 512:(h + 1) * 512], OP.subtract), inc=(half == 1))
            R[("vV",)] = v_

            if stop_after == "att1" and sub == 0:
                _passthrough()
                return
            bb.wait("vector", dBt, bv)
            at_ = {}

            def _ap(hp):
                par = hp % 2
                return (par, (2, 3) if par == 0 else (6, 7), 48 if par == 0 else 58,
                        AA["ssb"][:, par * 2 * JW:(par + 1) * 2 * JW],
                        AA["et_h"][:, par * 2 * JW:(par + 1) * 2 * JW],
                        AA["et_l"][:, par * 2 * JW:(par + 1) * 2 * JW],
                        AA["pt_h"][:, par * 128:(par + 1) * 128],
                        AA["pt_l"][:, par * 128:(par + 1) * 128],
                        AA["ptm_h"][:, par * 128:(par + 1) * 128],
                        AA["ptm_l"][:, par * 128:(par + 1) * 128])

            for it in range(9):
                if it < 8:
                    hp = it
                    par, tb, stb, ssb_, eth_, etl_, pth_, ptl_, ptmh_, ptml_ = _ap(hp)
                    sbank = par
                    rwait("tensor", eV, ("sc_bias", par))
                    if hp == 1:
                        rwait("tensor", eV, ("vV",))
                        rwait("tensor", eS, ("vS",))
                    scv = None
                    for hh in range(2):
                        h = 2 * hp + hh
                        fo = h // 2
                        pr = (h % 2) * 64
                        rg_dummy(sbank, 511)
                        for pi, (qa, ka) in enumerate((
                                (AA["q_fm_h"], AA["k_fm_h"]), (AA["q_fm_h"], AA["k_fm_l"]),
                                (AA["q_fm_l"], AA["k_fm_h"]))):
                            scv = bb.op("tensor", lambda E, qa=qa, ka=ka, fo=fo, pr=pr,
                                  i0=i0, sb_=sbank, hh=hh, pi=pi:
                                  E.matmul(PS[sb_][:, hh * JW:(hh + 1) * JW],
                                           qa[pr:pr + 64, fo * TILE + i0: fo * TILE + i0 + 128],
                                           ka[pr:pr + 64, fo * KW + i0: fo * KW + i0 + JW],
                                           start=(pi == 0), stop=(pi == 2)),
                                  inc=(hh == 1 and pi == 2))
                    rg_dummy(sbank, 511)
                    rwait("vector", eS, ("ssbS", par))
                    bb.wait("vector", eT, scv)
                    R[("sc_bias", par)] = bb.op(
                        "vector", lambda E, sb_=sbank, hp=hp, d=ssb_: E.tensor_tensor(
                            d[:], PS[sb_][:, 0:2 * JW],
                            AA["Bt"][:, hp * 2 * JW:(hp + 1) * 2 * JW], OP.add), inc=True)
                    if hp == 7:
                        R[("bias_last",)] = R[("sc_bias", par)]
                    bb.wait("vector", eV)
                    bb.op("vector", lambda E, d=ssb_, stb=stb: E.tensor_reduce(
                        ST[:, stb:stb + 2], d.rearrange("p (h j) -> p h j", j=JW), AX.X,
                        OP.max), inc=True)
                    bb.wait("vector", eV)
                    at_[hp] = {"neg": bb.op("vector", lambda E, stb=stb: E.tensor_scalar_mul(
                        ST[:, stb + 2:stb + 4], ST[:, stb:stb + 2], -1.0), inc=True)}
                if it >= 1:
                    hp = it - 1
                    par, tb, stb, ssb_, eth_, etl_, pth_, ptl_, ptmh_, ptml_ = _ap(hp)
                    bb.wait("vector", eS, at_[hp]["exp"])
                    bb.op("vector", lambda E, d=ssb_, stb=stb: E.tensor_reduce(
                        ST[:, stb:stb + 2], d.rearrange("p (h j) -> p h j", j=JW), AX.X,
                        OP.add), inc=True)
                    bb.wait("vector", eV)
                    bb.op("vector", lambda E, stb=stb: E.reciprocal(
                        ST[:, stb + 4:stb + 6], ST[:, stb:stb + 2]), inc=True)
                    bb.wait("vector", eV)
                    for hh in range(2):
                        v_ = bb.op("vector", lambda E, hh=hh, d=ssb_, stb=stb: E.tensor_scalar(
                            d[:, hh * JW:(hh + 1) * JW], d[:, hh * JW:(hh + 1) * JW],
                            ST[:, stb + 4 + hh:stb + 5 + hh], None, OP.mult), inc=(hh == 1))
                    at_[hp]["norm"] = v_
                    if stop_after == "att2" and sub == 0 and hp == 0:
                        _passthrough()
                        return
                if it < 8:
                    hp = it
                    par, tb, stb, ssb_, eth_, etl_, pth_, ptl_, ptmh_, ptml_ = _ap(hp)
                    bb.wait("scalar", eV, at_[hp]["neg"])
                    for hh in range(2):
                        v_ = bb.op("scalar", lambda E, hh=hh, d=ssb_, stb=stb: E.activation(
                            d[:, hh * JW:(hh + 1) * JW], d[:, hh * JW:(hh + 1) * JW],
                            AF.Exp, bias=ST[:, stb + 2 + hh:stb + 3 + hh]), inc=(hh == 1))
                    at_[hp]["exp"] = v_
                if it >= 1:
                    hp = it - 1
                    par, tb, stb, ssb_, eth_, etl_, pth_, ptl_, ptmh_, ptml_ = _ap(hp)
                    sbank = par
                    rwait("scalar", eT, ("etT", par))
                    bb.wait("scalar", eV, at_[hp]["norm"])
                    R[("ssbS", par)] = bb.op(
                        "scalar", lambda E, d=ssb_, e=eth_: E.activation(e[:], d[:], AF.Copy),
                        inc=True)
                    rwait("vector", eT, ("etT", par))
                    bb.wait("vector", eS, R[("ssbS", par)])
                    at_[hp]["etl"] = bb.op(
                        "vector", lambda E, d=ssb_, eh=eth_, el=etl_: E.tensor_tensor(
                            el[:], d[:], eh[:], OP.subtract), inc=True)
                    bb.wait("tensor", eV, at_[hp]["etl"])
                    rwait("tensor", eS, ("ptS", par))
                    for hh in range(2):
                        h = 2 * hp + hh
                        e_h = eth_[:, hh * JW:(hh + 1) * JW]
                        e_l = etl_[:, hh * JW:(hh + 1) * JW]
                        if hp == 0 and hh == 0:
                            rwait("tensor", eV, ("vV",))
                            rwait("tensor", eS, ("vS",))
                        bb.op("tensor", lambda E, e=e_h, tb=tb: E.transpose(
                            PSB[tb[0]][:, 0:128], e[:, 0:128], AA["c_idb"]))
                        bb.op("tensor", lambda E, e=e_l, tb=tb: E.transpose(
                            PSB[tb[1]][:, 0:128], e[:, 0:128], AA["c_idb"]))
                        bb.op("tensor", lambda E, e=e_h, tb=tb: E.transpose(
                            PSB[tb[0]][0:12, 128:256], e[:, 128:140], AA["c_idb"]))
                        v_ = bb.op("tensor", lambda E, e=e_l, tb=tb: E.transpose(
                            PSB[tb[1]][0:12, 128:256], e[:, 128:140], AA["c_idb"]), inc=True)
                        if hh == 1:
                            R[("etT", par)] = v_
                        bb.wait("scalar", eT)
                        bb.op("scalar", lambda E, tb=tb, d=pth_: E.copy(
                            d[:], PSB[tb[0]][:, 0:128]))
                        bb.op("scalar", lambda E, tb=tb, d=ptl_: E.copy(
                            d[:], PSB[tb[1]][:, 0:128]))
                        bb.op("scalar", lambda E, tb=tb, d=ptmh_: E.copy(
                            d[0:12, :], PSB[tb[0]][0:12, 128:256]))
                        v_ = bb.op("scalar", lambda E, tb=tb, d=ptml_: E.copy(
                            d[0:12, :], PSB[tb[1]][0:12, 128:256]), inc=True)
                        if hh == 1:
                            R[("ptS", par)] = v_
                        bb.wait("tensor", eS)
                        ob = 4 + (h // 8)
                        osl = (h % 8) // 2
                        opr = (h % 2) * 64
                        vc = h * 64
                        for pi, (pa, va) in enumerate((
                                (pth_, AA["vt_h"]), (pth_, AA["vt_l"]),
                                (ptl_, AA["vt_h"]))):
                            bb.op("tensor", lambda E, pa=pa, va=va, ob=ob, osl=osl, opr=opr,
                                  vc=vc, pi=pi:
                                  E.matmul(PS[ob][opr:opr + 64, osl * 128:(osl + 1) * 128],
                                           va[:, vc:vc + 64], pa[:],
                                           start=(pi == 0), stop=False))
                        rg_dummy(sbank, 511)
                        for pi, (ma, mv) in enumerate((
                                (ptmh_, AA["vtm_h"]), (ptmh_, AA["vtm_l"]),
                                (ptml_, AA["vtm_h"]))):
                            v_ = bb.op("tensor", lambda E, ma=ma, mv=mv, ob=ob, osl=osl,
                                  opr=opr, vc=vc, pi=pi:
                                  E.matmul(PS[ob][opr:opr + 64, osl * 128:(osl + 1) * 128],
                                           mv[0:12, vc:vc + 64], ma[0:12, :],
                                           start=False, stop=(pi == 2)),
                                  inc=(pi == 2))
                        if hh == 1:
                            R[("avT", par)] = v_
                            if hp == 7:
                                R[("av_last",)] = v_
                        rg_dummy(sbank, 511)

            if stop_after == "att3" and sub == 0:
                _passthrough()
                return
            bb.wait("scalar", eT)
            for hp in range(8):
                ob = 4 + hp // 4
                osl = hp % 4
                v_ = bb.op("scalar", lambda E, ob=ob, osl=osl, hp=hp: E.copy(
                    AA["of_h"][:, hp * 128:(hp + 1) * 128],
                    PS[ob][:, osl * 128:(osl + 1) * 128]), inc=(hp == 7))
            R[("ofhS",)] = v_
            bb.wait("vector", eS)
            for hp in range(8):
                ob = 4 + hp // 4
                osl = hp % 4
                bb.op("vector", lambda E, ob=ob, osl=osl, hp=hp: E.tensor_tensor(
                    AA["of_l"][:, hp * 128:(hp + 1) * 128],
                    PS[ob][:, osl * 128:(osl + 1) * 128],
                    AA["of_h"][:, hp * 128:(hp + 1) * 128], OP.subtract), inc=(hp == 7))

            bb.wait("tensor", eV)
            for kc in range(8):
                ph_i, ph_ap, ph_v, ph_s = ring_load([(sc_projT[0][kc], D)])[0]
                pl_i, pl_ap, pl_v, pl_s = ring_load([(sc_projT[1][kc], D)])[0]
                bb.wait("tensor", ph_s, ph_v)
                bb.wait("tensor", pl_s, pl_v)
                for half in range(2):
                    for pi, (oa, wa) in enumerate((
                            (AA["of_h"], ph_ap), (AA["of_h"], pl_ap), (AA["of_l"], ph_ap))):
                        bb.op("tensor", lambda E, oa=oa, wa=wa, kc=kc, half=half, pi=pi:
                              E.matmul(PS[6 + half][:, 0:512],
                                       oa[:, kc * 128:(kc + 1) * 128],
                                       wa[:, half * 512:(half + 1) * 512],
                                       start=(kc == 0 and pi == 0),
                                       stop=(kc == 7 and pi == 2)),
                              inc=(half == 1 and pi == 2))
                ring_consumed(ph_i)
                ring_consumed(pl_i)
            bb.wait("vector", eT)
            for half in range(2):
                bb.op("vector", lambda E, h=half, sub=sub: E.tensor_tensor(
                    AA["x1"][:, sub * D + h * 512: sub * D + (h + 1) * 512],
                    PS[6 + h][:, 0:512], AA["c_pbrow"][:, h * 512:(h + 1) * 512], OP.add),
                    inc=(half == 1))
            bb.wait("vector", eV)
            for half in range(2):
                v_ = bb.op("vector", lambda E, h=half, sub=sub: E.tensor_tensor(
                    AA["x1"][:, sub * D + h * 512: sub * D + (h + 1) * 512],
                    AA["x1"][:, sub * D + h * 512: sub * D + (h + 1) * 512],
                    AA["xg"][:, sub * D + h * 512: sub * D + (h + 1) * 512], OP.add),
                    inc=(half == 1))
            R[("x1",)] = v_

        tap("x1a", AA["x1"][:, 0:4 * D], (128, 4 * D))
        if stop_after == "attn":
            _passthrough()
            return

        bb.waitc("tensor", eV, eS)
        for gi in range(4):
            bb.op("tensor", lambda E, gi=gi: E.transpose(
                PS[4][0:1, gi * 128:(gi + 1) * 128], ST[:, 31 + gi:32 + gi], AA["c_idf"]))
        bb.op("tensor", lambda E: E.transpose(
            PS[6][0:1, 0:128], ST[:, 35:36], AA["c_idf"]), inc=True)
        bb.wait("vector", eT)
        bb.op("vector", lambda E: E.tensor_copy(AA["qw"][0:1, 0:512], PS[4][0:1, 0:512]))
        bb.op("vector", lambda E: E.tensor_copy(AA["qw"][0:1, 512:640], PS[6][0:1, 0:128]),
              inc=True)
        bb.wait("tensor", eV)
        for g in range(4):
            bb.op("tensor", lambda E, g=g: E.matmul(
                PS[5][:, g * 128:(g + 1) * 128], AA["c_or"][0:1, :],
                AA["qw"][0:1, g * 128:(g + 1) * 128], start=True, stop=True))
        bb.op("tensor", lambda E: E.matmul(PS[6][:, 0:2], AA["c_or"][0:1, :],
                                           AA["qw"][0:1, 638:640], start=True, stop=True))
        bb.op("tensor", lambda E: E.matmul(PS[6][:, 2:4], AA["c_or"][0:1, :],
                                           AA["qw"][0:1, 512:514], start=True, stop=True),
              inc=True)
        bb.wait("vector", eT)
        bb.op("vector", lambda E: E.tensor_copy(AA["dqrow"][:, 2:514], PS[5][:, 0:512]))
        bb.op("vector", lambda E: E.tensor_copy(AA["dqrow"][:, 0:2], PS[6][:, 0:2]))
        bb.op("vector", lambda E: E.tensor_copy(AA["dqrow"][:, 514:516], PS[6][:, 2:4]))
        bb.op("vector", lambda E: E.memset(AA["hacc"][:], -3.0e38))
        bb.op("vector", lambda E: E.memset(AA["hacc2"][:], 3.0e38), inc=True)
        bb.wait("tensor", eV)

        rbase(eV, [("faV", 0), ("faV", 1)])
        rbase(eS, [("faS", 0), ("faS", 1)])
        R[("faH", 0)] = dHa[0].n
        R[("faH", 1)] = dHa[1].n
        w1pre = ring_load([(sc_w1q[0], D)])[0]
        fa = {}
        for it in range(34):
            if it >= 2:
                cc = it - 2
                par = cc % 2
                hof = par * HW_
                bb.wait("vector", eS, fa[cc]["silu"])
                if cc >= 1:
                    bb.wait("vector", eV, fa[cc - 1]["folds"])
                bb.op("vector", lambda E, hof=hof: E.tensor_tensor(
                    AA["hacc"][:], AA["hacc"][:], AA["ha"][:, hof + 1:hof + 513], OP.max))
                fa[cc]["folds"] = bb.op("vector", lambda E, hof=hof: E.tensor_tensor(
                    AA["hacc2"][:], AA["hacc2"][:], AA["ha"][:, hof + 1:hof + 513],
                    OP.min), inc=True)
                bb.wait("sync", eS, fa[cc]["silu"])
                R[("faH", par)] = bb.op("sync", lambda E, cc=cc, hof=hof: E.dma_start(
                    out=sc_hact[cc][:, 0:514], in_=AA["ha"][:, hof:hof + 514]),
                    sem=dHa[par], inck=16)
            if 1 <= it <= 32:
                cc = it - 1
                par = cc % 2
                hof = par * HW_
                bufW = AA["qw"] if par == 0 else AA["tmp"]
                bufR = AA["qr"] if par == 0 else AA["qg"]
                w1c = AA["c_convw"][:, 32 + cc:33 + cc]
                w2c = AA["c_convw"][:, 64 + cc:65 + cc]
                bbc = AA["c_convb"][:, cc:cc + 1]
                bb.wait("vector", eS, fa[cc]["w0"])
                bb.op("vector", lambda E, hof=hof, w2=w2c, bw=bufW, br=bufR:
                      E.scalar_tensor_tensor(
                          br[:, 0:514], AA["ha"][:, hof + 2:hof + 516], w2, bw[:, 0:514],
                          OP.mult, OP.add), inc=True)
                bb.wait("vector", eV)
                fa[cc]["stt1"] = bb.op(
                    "vector", lambda E, hof=hof, w1=w1c, br=bufR: E.scalar_tensor_tensor(
                        br[:, 0:514], AA["ha"][:, hof + 1:hof + 515], w1, br[:, 0:514],
                        OP.mult, OP.add), inc=True)
                bb.wait("scalar", eV, fa[cc]["stt1"])
                fa[cc]["silu"] = bb.op(
                    "scalar", lambda E, hof=hof, b_=bbc, br=bufR: E.activation(
                        AA["ha"][:, hof:hof + 514], br[:, 0:514], AF.Silu, bias=b_),
                    inc=True)
                R[("faS", par)] = fa[cc]["silu"]
            if it < 32:
                cc = it
                par = cc % 2
                hof = par * HW_
                pb0 = 0 + 2 * par
                pb1 = 1 + 2 * par
                s_i, s_ap, s_v, s_s = w1pre
                if cc < 31:
                    w1pre = ring_load([(sc_w1q[cc + 1], D)])[0]
                rwait("tensor", eV, ("faV", par))
                bb.wait("tensor", s_s, s_v)
                for kc in range(8):
                    bb.op("tensor", lambda E, w=s_ap, kc=kc, cc=cc, pb=pb0:
                          E.matmul(PS[pb][:, 0:512], w[:, kc * 128:(kc + 1) * 128],
                                   AA["xnq_fm"][:, kc * XNW + 6: kc * XNW + 518],
                                   start=(kc == 0), stop=(kc == 7)))
                    bb.op("tensor", lambda E, w=s_ap, kc=kc, cc=cc, pb=pb1:
                          E.matmul(PS[pb][:, 0:4], w[:, kc * 128:(kc + 1) * 128],
                                   AA["xnq_fm"][:, kc * XNW + 518: kc * XNW + 522],
                                   start=(kc == 0), stop=(kc == 7)),
                          inc=(kc == 7))
                ring_consumed(s_i)
                bb.wait("vector", eT)
                rwait("vector", eS, ("faS", par))
                rwait("vector", dHa[par], ("faH", par))
                bb.op("vector", lambda E, hof=hof, pb=pb0: E.tensor_tensor(
                    AA["ha"][:, hof:hof + 512], PS[pb][:, 0:512], AA["dqrow"][:, 0:512],
                    OP.mult))
                vq = bb.op("vector", lambda E, hof=hof, pb=pb1: E.tensor_tensor(
                    AA["ha"][:, hof + 512:hof + 516], PS[pb][:, 0:4],
                    AA["dqrow"][:, 512:516], OP.mult), inc=True)
                R[("faV", par)] = vq
                fa[cc] = {"dq": vq}
                bb.wait("scalar", eV, vq)
                fa[cc]["w0"] = bb.op(
                    "scalar", lambda E, hof=hof, w0=AA["c_convw"][:, cc:cc + 1],
                    bw=(AA["qw"] if par == 0 else AA["tmp"]): E.activation(
                        bw[:, 0:514], AA["ha"][:, hof:hof + 514], AF.Copy, scale=w0),
                    inc=True)
        bb.waitc("tensor", eV, eS)
        for g in range(4):
            bb.op("tensor", lambda E, g=g: E.transpose(
                PS[4][:, g * 128:(g + 1) * 128], AA["hacc"][:, g * 128:(g + 1) * 128],
                AA["c_idf"]))
            bb.op("tensor", lambda E, g=g: E.transpose(
                PS[7][:, g * 128:(g + 1) * 128], AA["hacc2"][:, g * 128:(g + 1) * 128],
                AA["c_idf"]), inc=(g == 3))
        bb.wait("vector", eT)
        bb.op("vector", lambda E: E.tensor_reduce(
            ST[:, 36:40], PS[4].rearrange("p (g i) -> p g i", i=128), AX.X, OP.max), inc=True)
        bb.op("vector", lambda E: E.tensor_reduce(
            ST[:, 54:58], PS[7].rearrange("p (g i) -> p g i", i=128), AX.X, OP.min), inc=True)
        bb.wait("vector", eV)
        bb.op("vector", lambda E: E.tensor_scalar_mul(ST[:, 54:58], ST[:, 54:58], -1.0),
              inc=True)
        bb.wait("vector", eV)
        bb.op("vector", lambda E: E.tensor_tensor(ST[:, 36:40], ST[:, 36:40], ST[:, 54:58],
                                                  OP.max), inc=True)
        bb.wait("vector", eV)
        bb.op("vector", lambda E: E.tensor_scalar_max(ST[:, 36:40], ST[:, 36:40], 1e-5),
              inc=True)
        bb.wait("vector", eV)
        bb.op("vector", lambda E: E.reciprocal(ST[:, 40:44], ST[:, 36:40]))
        bb.op("vector", lambda E: E.tensor_tensor(
            ST[:, 44:48], ST[:, 36:40], AA["c_wm"][:, 1:2].broadcast_to((128, 4)), OP.mult),
            inc=True)
        bb.wait("vector", eV)
        bb.op("vector", lambda E: E.tensor_scalar_mul(ST[:, 40:44], ST[:, 40:44], 127.0))
        bb.op("vector", lambda E: E.tensor_scalar_mul(ST[:, 44:48], ST[:, 44:48], 1.0 / 127.0),
              inc=True)
        bb.wait("tensor", eV)
        for gi in range(4):
            bb.op("tensor", lambda E, gi=gi: E.transpose(
                PS[6][0:1, gi * 128:(gi + 1) * 128], ST[:, 40 + gi:41 + gi], AA["c_idf"]),
                inc=(gi == 3))
        bb.wait("vector", eT)
        bb.op("vector", lambda E: E.tensor_copy(AA["qw"][0:1, 0:512], PS[6][0:1, 0:512]),
              inc=True)
        bb.wait("tensor", eV)
        for g in range(4):
            bb.op("tensor", lambda E, g=g: E.matmul(
                PS[5][:, g * 128:(g + 1) * 128], AA["c_or"][0:1, :],
                AA["qw"][0:1, g * 128:(g + 1) * 128], start=True, stop=True), inc=(g == 3))
        bb.wait("vector", eT)
        bb.op("vector", lambda E: E.tensor_copy(AA["srow"][:, 0:512], PS[5][:, 0:512]),
              inc=True)

        bb.wait("sync", dHa[0])
        bb.wait("sync", dHa[1])
        rbase(eV, [("fbV", 0), ("fbV", 1)])
        rbase(eT, [("fbT", 0), ("fbT", 1)])
        R[("fbH", 0)] = dHb[0].n
        R[("fbH", 1)] = dHb[1].n
        for rnd in range(2):
            tss = (0, 1) if rnd == 0 else (2, 3)
            if rnd == 1:
                bb.wait("sync", dHb[0])
                bb.wait("sync", dHb[1])
                rbase(eT, [("fbT", 0), ("fbT", 1)])

            def _ldB(cc, rnd=rnd):
                par = cc % 2
                if rnd == 0:
                    rwait("sync", eV, ("fbV", par))
                    v = bb.op("sync", lambda E, cc=cc, hof=par * HW_: E.dma_start(
                        out=AA["ha"][:, hof:hof + 514], in_=sc_hact[cc][:, 0:514]),
                        sem=dHp[par], inck=16)
                else:
                    rwait("sync", eT, ("fbT", par))
                    v = bb.op("sync", lambda E, cc=cc, par=par: E.dma_start(
                        out=AA["qhb"][:, par * TILE:(par + 1) * TILE], in_=sc_qh[cc]),
                        sem=dHp[par], inck=16)
                return (v, ring_load([(sc_w2q[cc], D)])[0])

            def _w2mm(cc, qdst, w2s, par):
                s_i, s_ap, s_v, s_s = w2s
                bb.wait("tensor", s_s, s_v)
                for tsi, ts in enumerate(tss):
                    for half in range(2):
                        bank = 4 + 2 * (ts % 2) + half
                        v_ = bb.op("tensor", lambda E, w=s_ap, cc=cc, ts=ts, half=half,
                              bank=bank, q=qdst: E.matmul(
                                  PS[bank][:, 0:512], q[:, ts * 128:(ts + 1) * 128],
                                  w[:, half * 512:(half + 1) * 512],
                                  start=(cc == 0), stop=(cc == 31)),
                              inc=(tsi == 1 and half == 1))
                ring_consumed(s_i)
                R[("fbT", par)] = v_

            if rnd == 0:
                ofhF = AA["of_h"].bitcast(F32)
                oflF = AA["of_l"].bitcast(F32)
                qd0 = AA["qhb"][:, 0:TILE]
                qd1 = AA["qhb"][:, TILE:2 * TILE]
                pre = (_ldB(0), _ldB(1))
                for pc in range(16):
                    cc0, cc1 = 2 * pc, 2 * pc + 1
                    (ldv0, w2s0), (ldv1, w2s1) = pre
                    bb.wait("vector", dHp[0], ldv0)
                    rwait("scalar", eT, ("fbT", 0))
                    rwait("scalar", dHb[0], ("fbH", 0))
                    bb.wait("vector", eV)
                    R[("fbV", 0)] = bb.op("vector", lambda E: E.tensor_tensor(
                        AA["qw"][:, 0:512], AA["ha"][:, 1:513], AA["srow"][:, 0:512],
                        OP.mult), inc=True)
                    bb.wait("vector", eV)
                    wv0 = bb.op("vector", lambda E: E.tensor_scalar_add(
                        AA["qw"][:, 0:512], AA["qw"][:, 0:512], 128.5), inc=True)
                    bb.wait("vector", dHp[1], ldv1)
                    rwait("scalar", eT, ("fbT", 1))
                    rwait("scalar", dHb[1], ("fbH", 1))
                    R[("fbV", 1)] = bb.op("vector", lambda E: E.tensor_tensor(
                        AA["tmp"][:, 0:512], AA["ha"][:, HW_ + 1:HW_ + 513],
                        AA["srow"][:, 0:512], OP.mult), inc=True)
                    bb.wait("vector", eV)
                    wv1 = bb.op("vector", lambda E: E.tensor_scalar_add(
                        AA["tmp"][:, 0:512], AA["tmp"][:, 0:512], 128.5), inc=True)
                    if pc < 15:
                        pre = (_ldB(cc0 + 2), _ldB(cc1 + 2))
                    fin = round2(512, [
                        (AA["qw"], AA["qr"], AA["qg"], qd0, wv0),
                        (AA["tmp"], ofhF, oflF, qd1, wv1)])
                    for par, (cc, qdst, w2s, fv) in enumerate(
                            ((cc0, qd0, w2s0, fin[0]), (cc1, qd1, w2s1, fin[1]))):
                        bb.wait("sync", eS, fv)
                        R[("fbH", par)] = bb.op(
                            "sync", lambda E, cc=cc, q=qdst: E.dma_start(
                                out=sc_qh[cc], in_=q), sem=dHb[par], inck=16)
                        bb.wait("tensor", eS, fv)
                        _w2mm(cc, qdst, w2s, par)
            else:
                pre = _ldB(0)
                for cc in range(32):
                    par = cc % 2
                    qdst = AA["qhb"][:, par * TILE:(par + 1) * TILE]
                    ldv, w2s = pre
                    if cc < 31:
                        pre = _ldB(cc + 1)
                    bb.wait("tensor", dHp[par], ldv)
                    _w2mm(cc, qdst, w2s, par)
            bb.wait("vector", eT)
            for ts in tss:
                for half in range(2):
                    bank = 4 + 2 * (ts % 2) + half
                    bb.op("vector", lambda E, ts=ts, half=half, bank=bank:
                          E.scalar_tensor_tensor(
                              AA["x1"][:, ts * D + half * 512: ts * D + (half + 1) * 512],
                              PS[bank][:, 0:512], ST[:, 44 + ts:45 + ts],
                              AA["x1"][:, ts * D + half * 512: ts * D + (half + 1) * 512],
                              OP.mult, OP.add), inc=True)
            bb.wait("tensor", eV)

        tap("x1", AA["x1"][:, 0:4 * D], (128, 4 * D))

        if stop_after == "L0":
            bb.waitc("sync", eV, eT)
            for g in range(4):
                bb.op("sync", lambda E, g=g: E.dma_start(
                    out=y_out[seq][T0 + g * 128: T0 + (g + 1) * 128, :],
                    in_=AA["x1"][:, g * D:(g + 1) * D]), sem=dO, inck=16)
            return
        do_tile_l1(seq, t)

    def do_tile_l1(seq, t):
        T0 = t * TILE

        bb.wait("vector", eV)
        bb.wait("vector", eT)
        bb.wait("scalar", eS)
        bb.wait("scalar", eV)
        bb.wait("scalar", eT)
        if t == 0:
            bb.wait("vector", eT)
            bb.op("vector", lambda E: E.memset(AA["S_sb"][:], 0.0), inc=True)
        for g in range(4):
            bb.op("vector", lambda E, g=g: E.tensor_reduce(
                ST[:, g:g + 1], AA["x1"][:, g * D:(g + 1) * D], AX.X, OP.add), inc=True)
            bb.op("scalar", lambda E, g=g: E.activation(
                AA["sq"][:, 0:D], AA["x1"][:, g * D:(g + 1) * D], AF.Square,
                accum_out=ST[:, 6 + g:7 + g]), inc=True)
            bb.wait("scalar", eS)
        bb.wait("vector", eS)
        bb.wait("vector", eV)
        bb.op("vector", lambda E: E.tensor_scalar_mul(ST[:, 0:4], ST[:, 0:4], 1.0 / D))
        bb.op("vector", lambda E: E.tensor_scalar_mul(ST[:, 6:10], ST[:, 6:10], 1.0 / D),
              inc=True)
        bb.wait("vector", eV)
        bb.op("vector", lambda E: E.tensor_tensor(ST[:, 11:15], ST[:, 0:4], ST[:, 0:4], OP.mult),
              inc=True)
        bb.wait("vector", eV)
        bb.op("vector", lambda E: E.tensor_tensor(ST[:, 11:15], ST[:, 6:10], ST[:, 11:15],
                                                  OP.subtract), inc=True)
        bb.wait("scalar", eV)
        bb.op("scalar", lambda E: E.activation(ST[:, 16:20], ST[:, 11:15], AF.Ln, bias=CEPS5),
              inc=True)
        bb.wait("scalar", eS)
        bb.op("scalar", lambda E: E.activation(ST[:, 16:20], ST[:, 16:20], AF.Exp, scale=-0.5),
              inc=True)
        bb.wait("vector", eS)
        for g in range(4):
            bb.op("vector", lambda E, g=g: E.tensor_scalar(
                AA["xn1"][:, g * D:(g + 1) * D], AA["x1"][:, g * D:(g + 1) * D],
                ST[:, g:g + 1], ST[:, 16 + g:17 + g], OP.subtract, OP.mult),
                inc=(g == 3))

        bb.wait("scalar", eV)
        for g in range(4):
            bb.op("scalar", lambda E, g=g: E.activation(
                AA["sq"][:, 0:D], AA["xn1"][:, g * D:(g + 1) * D], AF.Square,
                accum_out=ST[:, 21 + g:22 + g]), inc=True)
            bb.wait("scalar", eS)
        bb.op("scalar", lambda E: E.activation(ST[:, 26:30], ST[:, 21:25], AF.Ln, bias=CEPS5,
                                               scale=1.0 / D), inc=True)
        bb.wait("scalar", eS)
        bb.op("scalar", lambda E: E.activation(ST[:, 26:30], ST[:, 26:30], AF.Exp, scale=-0.5),
              inc=True)
        for g in range(4):
            bb.wait("vector", eS)
            bb.wait("vector", eT)
            bb.op("vector", lambda E, g=g: E.tensor_scalar(
                AA["tmp"][:], AA["xn1"][:, g * D:(g + 1) * D], ST[:, 26 + g:27 + g], None,
                OP.mult), inc=True)
            bb.wait("scalar", eV)
            bb.op("scalar", lambda E: E.activation(AA["tms_h"][:], AA["tmp"][:], AF.Copy),
                  inc=True)
            bb.wait("vector", eS)
            bb.op("vector", lambda E: E.tensor_tensor(
                AA["tms_l"][:], AA["tmp"][:], AA["tms_h"][:], OP.subtract), inc=True)
            bb.waitc("tensor", eV, eS)
            for dc in range(8):
                pv = PSB[dc % 2]
                for which, src_b in (("h", AA["tms_h"]), ("l", AA["tms_l"])):
                    dst = AA["h1f_h"] if which == "h" else AA["h1f_l"]
                    bb.op("tensor", lambda E, sb_=src_b, dc=dc, pv=pv: E.transpose(
                        pv[:, 0:128], sb_[:, dc * 128:(dc + 1) * 128], AA["c_idb"]), inc=True)
                    bb.wait("scalar", eT)
                    bb.op("scalar", lambda E, d=dst, dc=dc, g=g, pv=pv: E.copy(
                        d[:, dc * TILE + g * 128: dc * TILE + (g + 1) * 128], pv[:, 0:128]),
                        inc=True)
                    bb.wait("tensor", eS)

        for wwhich, scw, dsth, dstl in (("q", sc_wq, AA["q_fm_h"], AA["q_fm_l"]),
                                        ("k", sc_wk, AA["k_fm_h"], AA["k_fm_l"])):
            for fog in range(2):
                bb.waitc("tensor", eS, eV)
                for kc in range(8):
                    sh_i, sh_ap, sh_v, sh_s = ring_load(
                        [(scw[0][kc][:, fog * 512:(fog + 1) * 512], 512)])[0]
                    sl_i, sl_ap, sl_v, sl_s = ring_load(
                        [(scw[1][kc][:, fog * 512:(fog + 1) * 512], 512)])[0]
                    bb.wait("tensor", sh_s, sh_v)
                    bb.wait("tensor", sl_s, sl_v)
                    for fi in range(4):
                        for pi, (wap, xap) in enumerate((
                                (sh_ap, AA["h1f_h"]), (sh_ap, AA["h1f_l"]),
                                (sl_ap, AA["h1f_h"]))):
                            bb.op("tensor", lambda E, w=wap, x=xap, fi=fi, kc=kc, pi=pi:
                                  E.matmul(PS[fi][:, 0:512],
                                           w[:, fi * 128:(fi + 1) * 128],
                                           x[:, kc * TILE: (kc + 1) * TILE],
                                           start=(kc == 0 and pi == 0),
                                           stop=(kc == 7 and pi == 2)),
                                  inc=(fi == 3 and pi == 2))
                    ring_consumed(sh_i)
                    ring_consumed(sl_i)
                bb.wait("scalar", eT)
                for fi in range(4):
                    fo = fog * 4 + fi
                    bb.op("scalar", lambda E, d=dsth, fo=fo, fi=fi: E.copy(
                        d.bitcast(F32)[:, 0:1] if False else d[:, fo * TILE:(fo + 1) * TILE],
                        PS[fi][:, 0:512]), inc=True)
                bb.wait("vector", eS)
                for fi in range(4):
                    fo = fog * 4 + fi
                    bb.op("vector", lambda E, dh=dsth, dl=dstl, fo=fo, fi=fi: E.tensor_tensor(
                        dl[:, fo * TILE:(fo + 1) * TILE], PS[fi][:, 0:512],
                        dh[:, fo * TILE:(fo + 1) * TILE], OP.subtract), inc=(fi == 3))

        for wwhich, scw, dsth, dstl in (("kt", sc_wk, AA["k1t_h"], AA["k1t_l"]),
                                        ("vt", sc_wv, AA["v1t_h"], AA["v1t_l"])):
            bb.waitc("tensor", eS, eV)
            for kc in range(8):
                sh_i, sh_ap, sh_v, sh_s = ring_load([(scw[0][kc], D)])[0]
                sl_i, sl_ap, sl_v, sl_s = ring_load([(scw[1][kc], D)])[0]
                bb.wait("tensor", sh_s, sh_v)
                bb.wait("tensor", sl_s, sl_v)
                for ts in range(4):
                    for half in range(2):
                        bank = 2 * ts + half
                        for pi, (wap, xap) in enumerate((
                                (sh_ap, AA["h1f_h"]), (sh_ap, AA["h1f_l"]),
                                (sl_ap, AA["h1f_h"]))):
                            bb.op("tensor", lambda E, w=wap, x=xap, kc=kc, ts=ts, half=half,
                                  bank=bank, pi=pi:
                                  E.matmul(PS[bank][:, 0:512],
                                           x[:, kc * TILE + ts * 128: kc * TILE + (ts + 1) * 128],
                                           w[:, half * 512:(half + 1) * 512],
                                           start=(kc == 0 and pi == 0),
                                           stop=(kc == 7 and pi == 2)),
                                  inc=(ts == 3 and half == 1 and pi == 2))
                ring_consumed(sh_i)
                ring_consumed(sl_i)
            bb.wait("scalar", eT)
            for ts in range(4):
                for half in range(2):
                    bank = 2 * ts + half
                    bb.op("scalar", lambda E, d=dsth, ts=ts, half=half, bank=bank: E.copy(
                        d[:, ts * D + half * 512: ts * D + (half + 1) * 512],
                        PS[bank][:, 0:512]), inc=(ts == 3 and half == 1))
            bb.wait("vector", eS)
            for ts in range(4):
                for half in range(2):
                    bank = 2 * ts + half
                    bb.op("vector", lambda E, dh=dsth, dl=dstl, ts=ts, half=half, bank=bank:
                          E.tensor_tensor(
                              dl[:, ts * D + half * 512: ts * D + (half + 1) * 512],
                              PS[bank][:, 0:512],
                              dh[:, ts * D + half * 512: ts * D + (half + 1) * 512],
                              OP.subtract), inc=(ts == 3 and half == 1))

        rbase(eV, [("cmV", 0), ("cmV", 1), ("conV", 0), ("conV", 1), ("cx2V",),
                   ("amV", 0), ("amV", 1)]
              + [("csV", h) for h in range(NH1)] + [("cSl", h) for h in range(NH1)])
        rbase(eS, [("cnS", 0), ("cnS", 1)])
        rbase(eT, [("coT", 0), ("coT", 1), ("ctT", 0), ("ctT", 1)])

        def _cp(h):
            pu = h % 2
            return (pu, 6 + pu, 4 + pu, 31 if pu == 0 else 33,
                    AA["ssb"][:, pu * 140:pu * 140 + 128],
                    AA["pt_h"][:, pu * 128:(pu + 1) * 128],
                    AA["pt_l"][:, pu * 128:(pu + 1) * 128],
                    AA["et_h"][:, pu * 140:pu * 140 + 128],
                    AA["et_l"][:, pu * 140:pu * 140 + 128],
                    AA["sq"][:, pu * 128:(pu + 1) * 128])

        for c in range(4):
            cu = {}
            for i in range(10):
                if i < 8:
                    h = i
                    pu, ob_, tbk, stb2, AMF, amh_, aml_, onh_, onl_, sq_ = _cp(h)
                    rwait("tensor", eV, ("cmV", pu))
                    for pi, (ka, qa) in enumerate((
                            (AA["k_fm_h"], AA["q_fm_h"]), (AA["k_fm_h"], AA["q_fm_l"]),
                            (AA["k_fm_l"], AA["q_fm_h"]))):
                        v_ = bb.op("tensor", lambda E, ka=ka, qa=qa, h=h, c=c, pi=pi, pu=pu:
                              E.matmul(PS[pu][:, 0:128],
                                       ka[:, h * TILE + c * 128: h * TILE + (c + 1) * 128],
                                       qa[:, h * TILE + c * 128: h * TILE + (c + 1) * 128],
                                       start=(pi == 0), stop=(pi == 2)), inc=(pi == 2))
                    cu[h] = {}
                    bb.wait("vector", eT, v_)
                    R[("cmV", pu)] = bb.op(
                        "vector", lambda E, d=AMF, pu=pu: E.tensor_tensor(
                            d[:], PS[pu][:, 0:128], AA["c_tril"], OP.mult), inc=True)
                    bb.wait("vector", eV)
                    rwait("vector", eT, ("coT", pu))
                    bb.op("vector", lambda E, s=AMF, d=amh_: E.tensor_copy(d[:], s[:]),
                          inc=True)
                    bb.wait("vector", eV)
                    R[("amV", pu)] = cu[h]["am"] = bb.op(
                        "vector", lambda E, s=AMF, dh=amh_, dl=aml_: E.tensor_tensor(
                            dl[:], s[:], dh[:], OP.subtract), inc=True)
                if 1 <= i <= 8:
                    h = i - 1
                    pu, ob_, tbk, stb2, AMF, amh_, aml_, onh_, onl_, sq_ = _cp(h)
                    bb.wait("tensor", eV, cu[h]["am"])
                    rwait("tensor", eV, ("conV", pu))
                    rwait("tensor", eV, ("cSl", h))
                    for pi, (aa, va) in enumerate((
                            (amh_, AA["v1t_h"]), (amh_, AA["v1t_l"]),
                            (aml_, AA["v1t_h"]))):
                        bb.op("tensor", lambda E, aa=aa, va=va, h=h, c=c, pi=pi, ob=ob_:
                              E.matmul(PS[ob][:, 0:128], aa[:],
                                       va[:, c * D + h * 128: c * D + (h + 1) * 128],
                                       start=(pi == 0), stop=False))
                    if not (t == 0 and c == 0):
                        for pi, (qa, Sa) in enumerate((
                                (AA["q_fm_h"], AA["S_h"]), (AA["q_fm_h"], AA["S_l"]),
                                (AA["q_fm_l"], AA["S_h"]))):
                            v_ = bb.op("tensor", lambda E, qa=qa, Sa=Sa, h=h, c=c, pi=pi,
                                  ob=ob_:
                                  E.matmul(PS[ob][:, 0:128],
                                           qa[:, h * TILE + c * 128: h * TILE + (c + 1) * 128],
                                           Sa[:, h * 128:(h + 1) * 128],
                                           start=False, stop=(pi == 2)), inc=(pi == 2))
                    else:
                        v_ = bb.op("tensor", lambda E, h=h, c=c, ob=ob_, aa=aml_:
                              E.matmul(PS[ob][:, 0:128], aa[:],
                                       AA["v1t_l"][:, c * D + h * 128: c * D + (h + 1) * 128],
                                       start=False, stop=True), inc=True)
                    R[("coT", pu)] = cu[h]["o"] = v_
                    bb.wait("scalar", eT, cu[h]["o"])
                    bb.op("scalar", lambda E, d=sq_, ob=ob_, stb2=stb2: E.activation(
                        d[:], PS[ob][:, 0:128], AF.Square, accum_out=ST[:, stb2:stb2 + 1]),
                        inc=True)
                    bb.wait("scalar", eS)
                    rwait("scalar", eV, ("conV", pu))
                    bb.op("scalar", lambda E, stb2=stb2: E.activation(
                        ST[:, stb2 + 1:stb2 + 2], ST[:, stb2:stb2 + 1], AF.Ln,
                        bias=CEPS5, scale=1.0 / 128.0), inc=True)
                    bb.wait("scalar", eS)
                    cu[h]["exp"] = bb.op("scalar", lambda E, stb2=stb2: E.activation(
                        ST[:, stb2 + 1:stb2 + 2], ST[:, stb2 + 1:stb2 + 2], AF.Exp,
                        scale=-0.5), inc=True)
                    bb.wait("vector", eS, cu[h]["exp"])
                    rwait("vector", eT, ("ctT", pu))
                    bb.op("vector", lambda E, d=onh_, ob=ob_, stb2=stb2: E.tensor_scalar(
                        d[:], PS[ob][:, 0:128], ST[:, stb2 + 1:stb2 + 2], None, OP.mult),
                        inc=True)
                    bb.wait("vector", eV)
                    R[("conV", pu)] = cu[h]["onl"] = bb.op(
                        "vector", lambda E, dh=onh_, dl=onl_, ob=ob_, stb2=stb2:
                        E.scalar_tensor_tensor(
                            dl[:], PS[ob][:, 0:128], ST[:, stb2 + 1:stb2 + 2], dh[:],
                            OP.mult, OP.subtract), inc=True)
                if 2 <= i <= 9:
                    h = i - 2
                    pu, ob_, tbk, stb2, AMF, amh_, aml_, onh_, onl_, sq_ = _cp(h)
                    bb.wait("tensor", eV, cu[h]["onl"])
                    rwait("tensor", eS, ("cnS", pu))
                    bb.op("tensor", lambda E, s=onh_, tbk=tbk: E.transpose(
                        PSB[tbk][:, 0:128], s[:], AA["c_idb"]))
                    R[("ctT", pu)] = bb.op("tensor", lambda E, s=onl_, tbk=tbk: E.transpose(
                        PSB[tbk][:, 128:256], s[:], AA["c_idb"]), inc=True)
                    bb.wait("scalar", eT, R[("ctT", pu)])
                    bb.op("scalar", lambda E, h=h, tbk=tbk: E.copy(
                        AA["onT_h"][:, h * 128:(h + 1) * 128], PSB[tbk][:, 0:128]))
                    R[("cnS", pu)] = bb.op("scalar", lambda E, h=h, tbk=tbk: E.copy(
                        AA["onT_l"][:, h * 128:(h + 1) * 128], PSB[tbk][:, 128:256]),
                        inc=True)
                    sb_ = 2 + h // 4
                    so = (h % 4) * 128
                    rwait("tensor", eV, ("csV", h))
                    for pi, (ka, va) in enumerate((
                            (AA["k1t_h"], AA["v1t_h"]), (AA["k1t_h"], AA["v1t_l"]),
                            (AA["k1t_l"], AA["v1t_h"]))):
                        v_ = bb.op("tensor", lambda E, ka=ka, va=va, h=h, c=c, pi=pi,
                              sb_=sb_, so=so:
                              E.matmul(PS[sb_][:, so:so + 128],
                                       ka[:, c * D + h * 128: c * D + (h + 1) * 128],
                                       va[:, c * D + h * 128: c * D + (h + 1) * 128],
                                       start=(pi == 0), stop=(pi == 2)),
                              inc=(pi == 2))
                    bb.wait("vector", eT, v_)
                    R[("csV", h)] = bb.op("vector", lambda E, sb_=sb_, so=so, h=h:
                        E.tensor_tensor(
                            AA["S_sb"][:, h * 128:(h + 1) * 128],
                            AA["S_sb"][:, h * 128:(h + 1) * 128],
                            PS[sb_][:, so:so + 128], OP.add), inc=True)
                    bb.wait("vector", eV)
                    bb.op("vector", lambda E, h=h: E.tensor_copy(
                        AA["S_h"][:, h * 128:(h + 1) * 128],
                        AA["S_sb"][:, h * 128:(h + 1) * 128]), inc=True)
                    bb.wait("vector", eV)
                    R[("cSl", h)] = bb.op("vector", lambda E, h=h: E.tensor_tensor(
                        AA["S_l"][:, h * 128:(h + 1) * 128],
                        AA["S_sb"][:, h * 128:(h + 1) * 128],
                        AA["S_h"][:, h * 128:(h + 1) * 128], OP.subtract), inc=True)

            rwait("tensor", eS, ("cnS", 0), ("cnS", 1))
            rwait("tensor", eV, ("cx2V",))
            for kc in range(8):
                wh_i, wh_ap, wh_v, wh_s = ring_load([(sc_wo[0][kc], D)])[0]
                wl_i, wl_ap, wl_v, wl_s = ring_load([(sc_wo[1][kc], D)])[0]
                bb.wait("tensor", wh_s, wh_v)
                bb.wait("tensor", wl_s, wl_v)
                for half in range(2):
                    for pi, (oa, wa) in enumerate((
                            (AA["onT_h"], wh_ap), (AA["onT_h"], wl_ap), (AA["onT_l"], wh_ap))):
                        bb.op("tensor", lambda E, oa=oa, wa=wa, kc=kc, half=half, pi=pi:
                              E.matmul(PS[4 + half][:, 0:512],
                                       oa[:, kc * 128:(kc + 1) * 128],
                                       wa[:, half * 512:(half + 1) * 512],
                                       start=(kc == 0 and pi == 0),
                                       stop=(kc == 7 and pi == 2)),
                              inc=(half == 1 and pi == 2))
                ring_consumed(wh_i)
                ring_consumed(wl_i)
            bb.wait("vector", eT)
            for half in range(2):
                v_ = bb.op("vector", lambda E, c=c, half=half: E.tensor_tensor(
                    AA["x2"][:, c * D + half * 512: c * D + (half + 1) * 512],
                    PS[4 + half][:, 0:512],
                    AA["xn1"][:, c * D + half * 512: c * D + (half + 1) * 512], OP.add),
                    inc=(half == 1))
            R[("cx2V",)] = v_

        bb.wait("scalar", eV)
        for g in range(4):
            bb.op("scalar", lambda E, g=g: E.activation(
                AA["sq"][:, 0:D], AA["x2"][:, g * D:(g + 1) * D], AF.Square,
                accum_out=ST[:, 21 + g:22 + g]), inc=True)
            bb.wait("scalar", eS)
        bb.op("scalar", lambda E: E.activation(ST[:, 26:30], ST[:, 21:25], AF.Ln, bias=CEPS5,
                                               scale=1.0 / D), inc=True)
        bb.wait("scalar", eS)
        bb.op("scalar", lambda E: E.activation(ST[:, 26:30], ST[:, 26:30], AF.Exp, scale=-0.5),
              inc=True)
        for g in range(4):
            bb.wait("vector", eS)
            bb.wait("vector", eT)
            bb.op("vector", lambda E, g=g: E.tensor_scalar(
                AA["tmp"][:], AA["x2"][:, g * D:(g + 1) * D], ST[:, 26 + g:27 + g], None,
                OP.mult), inc=True)
            bb.wait("scalar", eV)
            bb.op("scalar", lambda E: E.activation(AA["tms_h"][:], AA["tmp"][:], AF.Copy),
                  inc=True)
            bb.wait("vector", eS)
            bb.op("vector", lambda E: E.tensor_tensor(
                AA["tms_l"][:], AA["tmp"][:], AA["tms_h"][:], OP.subtract), inc=True)
            bb.waitc("tensor", eV, eS)
            for dc in range(8):
                pv = PSB[dc % 2]
                for which, src_b in (("h", AA["tms_h"]), ("l", AA["tms_l"])):
                    dst = AA["h2f_h"] if which == "h" else AA["h2f_l"]
                    bb.op("tensor", lambda E, sb_=src_b, dc=dc, pv=pv: E.transpose(
                        pv[:, 0:128], sb_[:, dc * 128:(dc + 1) * 128], AA["c_idb"]), inc=True)
                    bb.wait("scalar", eT)
                    bb.op("scalar", lambda E, d=dst, dc=dc, g=g, pv=pv: E.copy(
                        d[:, dc * TILE + g * 128: dc * TILE + (g + 1) * 128], pv[:, 0:128]),
                        inc=True)
                    bb.wait("tensor", eS)

        for gp in range(2):
            bb.waitc("vector", eS, eT)
            wvs = []
            for gi in range(2):
                g = 2 * gp + gi
                c0 = 36 if gi == 0 else 44
                Wb = AA["qw"] if gi == 0 else AA["tmp"]
                bb.op("vector", lambda E, g=g, c0=c0: E.tensor_reduce(
                    ST[:, c0:c0 + 1], AA["xn1"][:, g * D:(g + 1) * D], AX.X, OP.max,
                    apply_absolute_value=True), inc=True)
                bb.wait("vector", eV)
                bb.op("vector", lambda E, c0=c0: E.tensor_scalar_add(
                    ST[:, c0:c0 + 1], ST[:, c0:c0 + 1], 1e-8), inc=True)
                bb.wait("vector", eV)
                bb.op("vector", lambda E, c0=c0: E.reciprocal(
                    ST[:, c0 + 1:c0 + 2], ST[:, c0:c0 + 1]))
                bb.op("vector", lambda E, g=g, c0=c0: E.tensor_tensor(
                    ST[:, 40 + g:41 + g], ST[:, c0:c0 + 1], AA["c_wm"][:, 2:3], OP.mult),
                    inc=True)
                bb.wait("vector", eV)
                bb.op("vector", lambda E, c0=c0: E.tensor_scalar_mul(
                    ST[:, c0 + 2:c0 + 3], ST[:, c0 + 1:c0 + 2], 127.0))
                bb.op("vector", lambda E, g=g: E.tensor_scalar_mul(
                    ST[:, 40 + g:41 + g], ST[:, 40 + g:41 + g], 1.0 / 127.0), inc=True)
                bb.wait("vector", eV)
                wvs.append(bb.op("vector", lambda E, g=g, c0=c0, W=Wb: E.tensor_scalar(
                    W[:], AA["xn1"][:, g * D:(g + 1) * D], ST[:, c0 + 2:c0 + 3], 128.5,
                    OP.mult, OP.add), inc=True))
            round2(D, [
                (AA["qw"], AA["qr"], AA["qg"],
                 AA["xn1q"][:, (2 * gp) * D:(2 * gp + 1) * D], wvs[0]),
                (AA["tmp"], AA["Bt"][:, 0:D], AA["Bt"][:, D:2 * D],
                 AA["xn1q"][:, (2 * gp + 1) * D:(2 * gp + 2) * D], wvs[1])])
            bb.waitc("tensor", eV, eS)
            for gi in range(2):
                g = 2 * gp + gi
                for dc in range(8):
                    pv = PSB[dc % 2]
                    bb.op("tensor", lambda E, dc=dc, g=g, pv=pv: E.transpose(
                        pv[:, 0:128], AA["xn1q"][:, g * D + dc * 128: g * D + (dc + 1) * 128],
                        AA["c_idb"]), inc=True)
                    bb.wait("scalar", eT)
                    bb.op("scalar", lambda E, dc=dc, g=g, pv=pv: E.copy(
                        AA["x1qf"][:, dc * TILE + g * 128: dc * TILE + (g + 1) * 128],
                        pv[:, 0:128]), inc=True)
                    bb.wait("tensor", eS)

        GUF = AA["ssb"]
        R[("obD", 0)] = dOp[0].n
        R[("obD", 1)] = dOp[1].n
        nout = 0
        for half in range(2):
            rbase(eV, [("muV", 0), ("muV", 1)])
            rbase(eS, [("muS", 0), ("muS", 1)])
            bb.waitc("tensor", eV, eS)
            for ic in range(32):
                gpar = ic % 2
                gb0 = 2 * gpar
                gb1 = 2 * gpar + 1
                gh_i, gh_ap, gh_v, gh_s = ring_load([(sc_wg[0][ic], D)])[0]
                gl_i, gl_ap, gl_v, gl_s = ring_load([(sc_wg[1][ic], D)])[0]
                uh_i, uh_ap, uh_v, uh_s = ring_load([(sc_wu[0][ic], D)])[0]
                ul_i, ul_ap, ul_v, ul_s = ring_load([(sc_wu[1][ic], D)])[0]
                rwait("tensor", eV, ("muV", gpar))
                rwait("tensor", eS, ("muS", gpar))
                bb.wait("tensor", gh_s, gh_v)
                bb.wait("tensor", gl_s, gl_v)
                bb.wait("tensor", uh_s, uh_v)
                bb.wait("tensor", ul_s, ul_v)
                for bank, wh, wl in ((gb0, gh_ap, gl_ap), (gb1, uh_ap, ul_ap)):
                    for pi, (wap, xap) in enumerate((
                            (wh, AA["h2f_h"]), (wh, AA["h2f_l"]), (wl, AA["h2f_h"]))):
                        for kc in range(8):
                            bb.op("tensor", lambda E, w=wap, x=xap, kc=kc, bank=bank, pi=pi,
                                  half=half:
                                  E.matmul(PS[bank][:, 0:256],
                                           w[:, kc * 128:(kc + 1) * 128],
                                           x[:, kc * TILE + half * 256: kc * TILE + half * 256 + 256],
                                           start=(kc == 0 and pi == 0),
                                           stop=(kc == 7 and pi == 2)),
                                  inc=(bank == gb1 and kc == 7 and pi == 2))
                for si in (gh_i, gl_i, uh_i, ul_i):
                    ring_consumed(si)
                bb.wait("scalar", eT)
                R[("muS", gpar)] = bb.op(
                    "scalar", lambda E, gb0=gb0: E.activation(
                        AA["sg"][:, 0:256], PS[gb0][:, 0:256], AF.Silu), inc=True)
                bb.wait("vector", eS)
                R[("muV", gpar)] = bb.op("vector", lambda E, gb1=gb1: E.tensor_tensor(
                    GUF[:, 0:256], AA["sg"][:, 0:256], PS[gb1][:, 0:256], OP.mult), inc=True)
                bb.wait("scalar", eV)
                bb.op("scalar", lambda E: E.activation(AA["gu_h"][:, 0:256], GUF[:, 0:256],
                                                       AF.Copy), inc=True)
                bb.wait("vector", eS)
                bb.op("vector", lambda E: E.tensor_tensor(
                    AA["gu_l"][:, 0:256], GUF[:, 0:256], AA["gu_h"][:, 0:256], OP.subtract),
                    inc=True)
                dh_i, dh_ap, dh_v, dh_s = ring_load([(sc_wd[0][ic], D)])[0]
                dl_i, dl_ap, dl_v, dl_s = ring_load([(sc_wd[1][ic], D)])[0]
                bb.wait("tensor", dh_s, dh_v)
                bb.wait("tensor", dl_s, dl_v)
                bb.wait("tensor", eV)
                for ts2 in range(2):
                    for dhf in range(2):
                        bank = 4 + 2 * ts2 + dhf
                        for pi, (ga, wa) in enumerate((
                                (AA["gu_h"], dh_ap), (AA["gu_h"], dl_ap), (AA["gu_l"], dh_ap))):
                            bb.op("tensor", lambda E, ga=ga, wa=wa, ic=ic, ts2=ts2, dhf=dhf,
                                  bank=bank, pi=pi:
                                  E.matmul(PS[bank][:, 0:512],
                                           ga[:, ts2 * 128:(ts2 + 1) * 128],
                                           wa[:, dhf * 512:(dhf + 1) * 512],
                                           start=(ic == 0 and pi == 0),
                                           stop=(ic == 31 and pi == 2)),
                                  inc=(ts2 == 1 and dhf == 1 and pi == 2))
                ring_consumed(dh_i)
                ring_consumed(dl_i)
            for ts2 in range(2):
                ts = half * 2 + ts2
                bb.waitc("tensor", eV)
                for kc in range(8):
                    fh_i, fh_ap, fh_v, fh_s = ring_load([(sc_l1f[kc], D)])[0]
                    bb.wait("tensor", fh_s, fh_v)
                    for dhf in range(2):
                        bb.op("tensor", lambda E, w=fh_ap, kc=kc, ts=ts, dhf=dhf:
                              E.matmul(PS[2 + dhf][:, 0:512],
                                       AA["x1qf"][:, kc * TILE + ts * 128: kc * TILE + (ts + 1) * 128],
                                       w[:, dhf * 512:(dhf + 1) * 512],
                                       start=(kc == 0), stop=(kc == 7)),
                              inc=(dhf == 1))
                    ring_consumed(fh_i)
                bb.wait("vector", eT)
                for dhf in range(2):
                    bank_d = 4 + 2 * ts2 + dhf
                    opar = nout % 2
                    nout += 1
                    ob2_ = AA["tmp"][:, opar * 512:(opar + 1) * 512]
                    rwait("vector", dOp[opar], ("obD", opar))
                    bb.op("vector", lambda E, ts=ts, dhf=dhf, d=ob2_: E.scalar_tensor_tensor(
                        d[:], PS[2 + dhf][:, 0:512], ST[:, 40 + ts:41 + ts],
                        AA["x1"][:, ts * D + dhf * 512: ts * D + (dhf + 1) * 512],
                        OP.mult, OP.add), inc=True)
                    bb.wait("vector", eV)
                    bb.op("vector", lambda E, ts=ts, dhf=dhf, d=ob2_: E.tensor_tensor(
                        d[:], d[:],
                        AA["x2"][:, ts * D + dhf * 512: ts * D + (dhf + 1) * 512], OP.add),
                        inc=True)
                    bb.wait("vector", eV)
                    bb.op("vector", lambda E, bd=bank_d, d=ob2_: E.tensor_tensor(
                        d[:], d[:], PS[bd][:, 0:512], OP.add), inc=True)
                    bb.wait("sync", eV)
                    R[("obD", opar)] = bb.op(
                        "sync", lambda E, ts=ts, dhf=dhf, T0=T0, d=ob2_: E.dma_start(
                            out=y_out[seq][T0 + ts * 128: T0 + (ts + 1) * 128,
                                           dhf * 512:(dhf + 1) * 512],
                            in_=d[:]), sem=dOp[opar], inck=16)

    if stop_after == "prep":
        for seq in range(n_seq):
            for t in range(NT):
                bb.waitc("sync", eV, eS, eG, eT)
                for g in range(4):
                    bb.op("sync", lambda E, g=g, seq=seq, t=t: E.dma_start(
                        out=y_out[seq][t * TILE + g * 128: t * TILE + (g + 1) * 128, :],
                        in_=x_in[seq][t * TILE + g * 128: t * TILE + (g + 1) * 128, :]),
                        sem=dO, inck=16)
    else:
        for seq in range(n_seq):
            for t in range(NT):
                do_tile(seq, t)

    bb.wait("sync", dO)
    bb.wait("sync", dOp[0])
    bb.wait("sync", dOp[1])
    bb.op("sync", lambda E: E.nop(nofuse=True), inc=True)

    with nc.Block() as block:
        bb.finish(block)
    return nc, dbg



def host_inputs(inp):
    f32 = np.float32
    out = {}
    out["w_qkvT"] = np.ascontiguousarray(inp["L0_attn_qkv_w"].T)
    out["w_projT"] = np.ascontiguousarray(inp["L0_attn_proj_w"].T)
    out["w_w1T"] = np.ascontiguousarray(inp["L0_ffn_w1"].T)
    out["w_w2T"] = np.ascontiguousarray(inp["L0_ffn_w2"].T)
    out["w_l1fT"] = np.ascontiguousarray(inp["L1_ffn_weight"].T)
    out["w_q1"] = np.ascontiguousarray(inp["L1_attn_Wq"])
    out["w_k1"] = np.ascontiguousarray(inp["L1_attn_Wk"])
    out["w_v1"] = np.ascontiguousarray(inp["L1_attn_Wv"])
    out["w_o1"] = np.ascontiguousarray(inp["L1_attn_Wo"])
    out["w_g1"] = np.ascontiguousarray(inp["L1_attn_Wgate"])
    out["w_u1"] = np.ascontiguousarray(inp["L1_attn_Wup"])
    out["w_d1"] = np.ascontiguousarray(inp["L1_attn_Wdown"])
    out["qkv_b_qk"] = np.ascontiguousarray(inp["L0_attn_qkv_b"][:2 * D].reshape(16, 128).T)
    out["qkv_b_vcol"] = np.ascontiguousarray(inp["L0_attn_qkv_b"][2 * D:].reshape(8, 128).T)
    out["proj_b_row"] = np.ascontiguousarray(inp["L0_attn_proj_b"].reshape(1, D))
    out["conv_w3"] = np.ascontiguousarray(
        inp["L0_ffn_conv_w"].reshape(32, 128, 3).transpose(2, 1, 0))
    out["conv_b_c"] = np.ascontiguousarray(inp["L0_ffn_conv_b"].reshape(32, 128).T)
    out["anorm_c"] = np.ascontiguousarray(inp["L1_attn_attn_norm_w"].reshape(8, 128).T)
    out["mnorm_c"] = np.ascontiguousarray(inp["L1_attn_mlp_norm_w"].reshape(8, 128).T)
    out["onorm_c"] = np.ascontiguousarray(inp["L1_attn_attn_out_norm_w"].reshape(128, 1))

    rpb = inp["L0_attn_rpb"]
    ii = np.arange(128)[:, None]
    jl = np.arange(JW)[None, :]
    out["brpb"] = np.ascontiguousarray(rpb[:, np.clip(6 + jl - ii, 0, 24)].astype(f32))
    r = jl - 6 - ii
    m_int = np.where((r >= -6) & (r <= 6), 0.0, -1e30).astype(f32)
    kloc = jl - 6
    ni_f = np.maximum(ii - 6, 0)
    m_first = np.where((kloc >= ni_f) & (kloc <= ni_f + 12), 0.0, -1e30).astype(f32)
    ni_l = np.minimum(ii - 6, 128 - 13)
    m_last = np.where((kloc >= ni_l) & (kloc <= ni_l + 12), 0.0, -1e30).astype(f32)
    out["bmask3"] = np.ascontiguousarray(np.stack([m_int, m_first, m_last]))
    jj = np.arange(128)
    out["tril_T"] = np.ascontiguousarray((jj[:, None] <= jj[None, :]).astype(f32))
    out["ident_f"] = np.eye(128, dtype=f32)
    import ml_dtypes
    out["ident_b"] = np.eye(128, dtype=ml_dtypes.bfloat16)
    out["ones_row"] = np.ones((1, 128), f32)
    out["ones_col"] = np.ones((128, 1), f32)
    cst = np.zeros((128, 8), f32)
    cst[:, 0] = -128.0
    cst[:, 1] = -0.5
    cst[:, 2] = 0.5
    cst[:, 3] = 1e-5
    cst[:, 4] = 1e-8
    cst[:, 6] = 1.0
    cst[:, 7] = 0.5
    out["consts_in"] = cst
    out["spc_in"] = np.zeros((128, 64), f32)
    return out


def kernel(**inputs):
    n_cores = 8
    Bf, Tf = inputs["L0_attn_x"].shape[0], inputs["L0_attn_x"].shape[1]
    per = Bf // n_cores
    nc, _ = build(n_seq=per, seq_len=Tf)
    common = host_inputs(inputs)
    x = np.ascontiguousarray(inputs["L0_attn_x"])
    in_maps = []
    for c in range(n_cores):
        m = dict(common)
        m["x_in"] = np.ascontiguousarray(x[c * per:(c + 1) * per])
        in_maps.append(m)
    res = run_bass_kernel_spmd(nc, in_maps, core_ids=list(range(n_cores)))
    return np.concatenate([r["y_out"] for r in res.results], axis=0)
